# version 14
# baseline (speedup 1.0000x reference)
import numpy as np
import concourse.bass as bass
import concourse.mybir as mybir
from concourse.bass_utils import run_bass_kernel_spmd

F32 = mybir.dt.float32
BF16 = mybir.dt.bfloat16
AF = mybir.ActivationFunctionType
ALU = mybir.AluOpType
AX = mybir.AxisListType

D = 1024
NH = 16
HD = 64
NE = 8
H0 = 2048
H1 = 4096
NS = 16
LN_EPS = 1e-5
RMS_EPS = 1.1920929e-07


ENGINES = ("pe", "act", "dve", "pool", "sp")


class _Op:
    __slots__ = ("eng", "fn", "idx", "slot", "need_inc", "cnt", "waits", "tag")

    def __init__(self, eng, fn, idx, slot, tag):
        self.eng = eng
        self.fn = fn
        self.idx = idx
        self.slot = slot
        self.need_inc = False
        self.cnt = None
        self.waits = {}
        self.tag = tag


class KB:
    def __init__(self, nc):
        self.nc = nc
        self.streams = {e: [] for e in ENGINES}
        self.slots = {}
        self.slot_eng = {}
        self.last_w = {}
        self.readers = {}
        self.n_ops = 0

    def _stream_of(self, op):
        return op.slot if op.slot is not None else op.eng

    def _edge(self, consumer, producer):
        if producer is consumer:
            return
        if (producer.eng == "pe" and consumer.eng == "pe"
                and producer.slot is None and consumer.slot is None):
            return
        sk = self._stream_of(producer)
        if consumer.waits.get(sk, 0) < producer.idx:
            consumer.waits[sk] = producer.idx
        producer.need_inc = True

    def _record(self, op, r, w):
        for k in r:
            p = self.last_w.get(k)
            if p is not None:
                self._edge(op, p)
            self.readers.setdefault(k, {})[self._stream_of(op)] = op
        for k in w:
            p = self.last_w.get(k)
            if p is not None:
                self._edge(op, p)
            for rd in self.readers.get(k, {}).values():
                self._edge(op, rd)
            self.readers[k] = {}
            self.last_w[k] = op

    def op(self, eng, fn, r=(), w=(), tag=""):
        st = self.streams[eng]
        op = _Op(eng, fn, len(st) + 1, None, tag)
        self._record(op, r, w)
        st.append(op)
        self.n_ops += 1
        return op

    def dma(self, eng, slot, fn, r=(), w=(), tag=""):
        assert eng in ("sp", "act", "pool")
        sl = self.slots.setdefault(slot, [])
        if slot in self.slot_eng:
            assert self.slot_eng[slot] == eng, (slot, eng)
        else:
            self.slot_eng[slot] = eng
        op = _Op(eng, fn, len(sl) + 1, slot, tag)
        if sl:
            self._edge(op, sl[-1])
        self._record(op, r, w)
        sl.append(op)
        self.streams[eng].append(op)
        self.n_ops += 1
        return op

    def barrier(self):
        tails = []
        for st in self.streams.values():
            for o in reversed(st):
                if o.slot is None:
                    tails.append(o)
                    break
        for sl in self.slots.values():
            if sl:
                tails.append(sl[-1])
        for i, t in enumerate(tails):
            k = ("__bar__", i)
            self.last_w[k] = t
            self.readers[k] = {}
        self._bar_keys = [("__bar__", i) for i in range(len(tails))]
        self._bar_pending = dict.fromkeys(ENGINES, True)
        self._bar_tails = tails
        return self._bar_keys

    def _maybe_bar(self, op):
        if getattr(self, "_bar_pending", None) and self._bar_pending.get(op.eng):
            for t in self._bar_tails:
                self._edge(op, t)
            self._bar_pending[op.eng] = False

    def finalize_and_emit(self):
        nc = self.nc
        tails = []
        for st in self.streams.values():
            for o in reversed(st):
                if o.slot is None:
                    tails.append(o)
                    break
        for sl in self.slots.values():
            if sl:
                tails.append(sl[-1])
        fin = _Op("sp", lambda e: e.nop(), len(self.streams["sp"]) + 1, None,
                  "final")
        for t in tails:
            self._edge(fin, t)
        self.streams["sp"].append(fin)

        for e, st in self.streams.items():
            c = 0
            for op in st:
                if op.slot is not None:
                    continue
                if op.need_inc:
                    c += 1
                    op.cnt = c
        for slot, sl in self.slots.items():
            c = 0
            for op in sl:
                op.need_inc = True
                c += 16
                op.cnt = c

        nextcnt = {}
        for e, st in self.streams.items():
            arr = {}
            pend = []
            for op in st:
                if op.slot is not None:
                    continue
                pend.append(op.idx)
                if op.need_inc:
                    for i in pend:
                        arr[i] = op.cnt
                    pend = []
            for i in pend:
                arr[i] = None
            nextcnt[e] = arr
        for slot, sl in self.slots.items():
            nextcnt[slot] = {op.idx: op.cnt for op in sl}

        used = set()
        for st in self.streams.values():
            for op in st:
                for k in op.waits:
                    used.add(k)
                if op.need_inc:
                    used.add(self._stream_of(op))
        from contextlib import ExitStack
        self._ctx = ExitStack()
        sems = {}
        for i, k in enumerate(sorted(used, key=str)):
            sems[k] = self._ctx.enter_context(nc.semaphore(f"s{i}"))

        eng_attr = {"pe": "tensor", "act": "scalar", "dve": "vector",
                    "pool": "gpsimd", "sp": "sync"}
        with nc.Block() as blk:
            for e in ENGINES:
                st = self.streams[e]
                if not st:
                    continue

                def body(engine, _st=st):
                    hw = {}
                    for op in _st:
                        for skey, pidx in sorted(op.waits.items(),
                                                 key=lambda kv: str(kv[0])):
                            val = nextcnt[skey][pidx]
                            assert val is not None, (op.tag, skey, pidx)
                            if hw.get(skey, 0) < val:
                                engine.wait_ge(sems[skey], val)
                                hw[skey] = val
                        inst = op.fn(engine)
                        if op.need_inc:
                            sk = self._stream_of(op)
                            inst.then_inc(sems[sk], 16 if op.slot else 1)

                getattr(blk, eng_attr[e])(body)
        return self._ctx


_orig_op, _orig_dma = KB.op, KB.dma


def _op(self, eng, fn, r=(), w=(), tag=""):
    o = _orig_op(self, eng, fn, r, w, tag)
    self._maybe_bar(o)
    return o


def _dma(self, eng, slot, fn, r=(), w=(), tag=""):
    o = _orig_dma(self, eng, slot, fn, r, w, tag)
    self._maybe_bar(o)
    return o


KB.op, KB.dma = _op, _dma



def build_nc(T=4096, dev_scratch=False):
    assert T % 128 == 0 and T >= 512
    NT = T // 128
    TQ = min(1024, T)
    NQ = T // TQ
    QG = min(1024, T)
    NQG = T // QG
    NJT = T // 128
    NTC = T // 512

    nc = bass.Bass("TRN2", target_bir_lowering=False)

    ins = {}
    def din(name, shape):
        t = nc.dram_tensor(name, shape, F32, kind="ExternalInput")
        ins[name] = t
        return t

    X = din("x", [T, D])
    Wqkv = din("w_qkv", [3 * D, D])
    Gw = din("gate_w", [NE, D])
    Gb = din("gate_b", [NE])
    Rw = din("rms_w", [NE])
    W1_0 = din("l0_w1", [NE, D, H0])
    B1_0 = din("l0_b1", [NE, H0])
    W2_0 = din("l0_w2", [NE, H0, D])
    B2_0 = din("l0_b2", [NE, D])
    Win = din("w_in", [D, D])
    Wout = din("w_out", [D, D])
    Alog = din("a_log", [D, NS])
    Bssm = din("b_ssm", [D, NS])
    Cssm = din("c_ssm", [D, NS])
    DTlog = din("dt_log", [D])
    Dssm = din("d_ssm", [D])
    Cent = din("centroids", [NE, D])
    W1_1 = din("l1_w1", [NE, D, H1])
    B1_1 = din("l1_b1", [NE, H1])
    W2_1 = din("l1_w2", [NE, H1, D])
    B2_1 = din("l1_b2", [NE, D])
    OUT = nc.dram_tensor("out", [T, D], F32, kind="ExternalOutput")

    sck = "ExternalOutput" if dev_scratch else "Internal"
    sc_xn0t = nc.dram_tensor("sc_xn0t", [128, 8 * T], BF16, kind=sck)
    sc_w10 = nc.dram_tensor("sc_w10", [NE, D, H0], BF16, kind="Internal")
    sc_w20 = nc.dram_tensor("sc_w20", [NE, H0, D], BF16, kind="Internal")
    sc_w11 = nc.dram_tensor("sc_w11", [NE, D, H1], BF16, kind="Internal")
    sc_w21 = nc.dram_tensor("sc_w21", [NE, H1, D], BF16, kind="Internal")
    sc_qt = nc.dram_tensor("sc_qt", [D, T], BF16, kind=sck)
    sc_kt = nc.dram_tensor("sc_kt", [D, T], BF16, kind=sck)
    sc_vt = nc.dram_tensor("sc_vt", [D, T], BF16, kind=sck)
    sc_s0 = nc.dram_tensor("sc_s0", [T, D], F32, kind=sck)
    sc_x1 = nc.dram_tensor("sc_x1", [T, D], F32, kind=sck)
    sc_xn1 = nc.dram_tensor("sc_xn1", [T, D], BF16, kind=sck)
    sc_s1 = nc.dram_tensor("sc_s1", [T, D], F32, kind=sck)

    from contextlib import ExitStack
    ctx = ExitStack()
    AR1 = ctx.enter_context(nc.sbuf_tensor("AR1", [128, max(8 * T, 32768)], BF16))
    AR2 = ctx.enter_context(nc.sbuf_tensor("AR2", [128, 33280], BF16))
    AR3 = ctx.enter_context(nc.sbuf_tensor("AR3", [128, 8192], F32))
    AR4 = ctx.enter_context(nc.sbuf_tensor("AR4", [128, 8192], BF16))
    SM = ctx.enter_context(nc.sbuf_tensor("SM", [128, 5632], F32))
    SMB = ctx.enter_context(nc.sbuf_tensor("SMB", [128, 4096], BF16))
    psA0 = ctx.enter_context(nc.psum_tensor("psA0", [128, 512], F32))
    psA1 = ctx.enter_context(nc.psum_tensor("psA1", [128, 512], F32))
    psB = ctx.enter_context(nc.psum_tensor("psB", [128, 2048], F32))
    psT = ctx.enter_context(nc.psum_tensor("psT", [128, 1024], BF16))
    psF = ctx.enter_context(nc.psum_tensor("psF", [128, 512], F32))

    SMbf = SM.ap().bitcast(BF16)
    AR3bf = AR3.ap().bitcast(BF16)

    kb = KB(nc)

    IDF = 0
    IO2 = 128
    STATS = 256
    GT = 320
    BC = 448
    NORMG = 480
    GATES = 512
    IDB = 2 * 768
    ONESB = 2 * 832
    ONESF = 896
    OC8 = 1024
    GWTB = 2 * 1026
    CENTB = 2 * 1058
    PH = 1152

    KA0, KA1, KT, KF = (("ps", "A0"), ("ps", "A1"), ("ps", "T"), ("ps", "F"))
    KBs = [("ps", "B", i) for i in range(4)]

    def psb(i):
        return psB[:, i * 512:(i + 1) * 512]

    def mm(ps_ap, lhsT_ap, rhs_ap, start, stop, r, w, tag):
        kb.op("pe", lambda e: e.matmul(ps_ap, lhsT_ap, rhs_ap,
                                       start=start, stop=stop),
              r=r, w=w, tag=tag)

    def tr(ps_ap, in_ap, id_ap, r, w, tag):
        p = in_ap.shape[0]
        ids = id_ap[0:p, 0:p]
        kb.op("pe", lambda e: e.transpose(ps_ap, in_ap, ids),
              r=r, w=w, tag=tag)

    def act(out_ap, in_ap, func, r, w, tag, bias=0.0, scale=1.0, accum=None):
        kb.op("act", lambda e: e.activation(out_ap, in_ap, func, bias=bias,
                                            scale=scale, accum_out=accum),
              r=r, w=w, tag=tag)

    def acopy(out_ap, in_ap, r, w, tag):
        kb.op("act", lambda e: e.copy(out_ap, in_ap), r=r, w=w, tag=tag)

    def tt(out_ap, a_ap, b_ap, op, r, w, tag, eng="dve"):
        kb.op(eng, lambda e: e.tensor_tensor(out_ap, a_ap, b_ap, op),
              r=r, w=w, tag=tag)

    def ts(out_ap, in_ap, s1, s2, op0, op1, r, w, tag, eng="dve"):
        kb.op(eng, lambda e: e.tensor_scalar(out_ap, in_ap, s1, s2, op0, op1),
              r=r, w=w, tag=tag)

    def stt(out_ap, in0_ap, scal, in1_ap, op0, op1, r, w, tag, eng="dve"):
        kb.op(eng, lambda e: e.scalar_tensor_tensor(out_ap, in0_ap, scal,
                                                    in1_ap, op0, op1),
              r=r, w=w, tag=tag)

    def red(out_ap, in_ap, op, r, w, tag, negate=False):
        if op == "max":
            kb.op("dve", lambda e: e.reduce_max(out_ap, in_ap, AX.X,
                                                negate=negate),
                  r=r, w=w, tag=tag)
        else:
            kb.op("dve", lambda e: e.reduce_sum(out_ap, in_ap, AX.X),
                  r=r, w=w, tag=tag)

    def dcopy(out_ap, in_ap, r, w, tag, eng="dve"):
        kb.op(eng, lambda e: e.tensor_copy(out_ap, in_ap), r=r, w=w, tag=tag)

    def recip(out_ap, in_ap, r, w, tag):
        kb.op("dve", lambda e: e.reciprocal(out_ap, in_ap), r=r, w=w, tag=tag)

    def memset(ap, v, w, tag, eng="dve"):
        kb.op(eng, lambda e: e.memset(ap, v), w=w, tag=tag)

    kb.op("pool", lambda e: e.iota(SM[:, IDF:IDF + 128], [[1, 128]],
                                   channel_multiplier=0,
                                   allow_small_or_imprecise_dtypes=True),
          w=[("sm", "idf")], tag="iota1")
    kb.op("pool", lambda e: e.iota(SM[:, IO2:IO2 + 128], [[0, 128]],
                                   channel_multiplier=1,
                                   allow_small_or_imprecise_dtypes=True),
          w=[("sm", "io2")], tag="iota2")
    tt(SM[:, IDF:IDF + 128], SM[:, IDF:IDF + 128], SM[:, IO2:IO2 + 128],
       ALU.is_equal, r=[("sm", "idf"), ("sm", "io2")], w=[("sm", "idf")],
       tag="ideq")
    dcopy(SMbf[:, IDB:IDB + 128], SM[:, IDF:IDF + 128],
          r=[("sm", "idf")], w=[("sm", "idb")], tag="idb")
    memset(SMbf[0:1, ONESB:ONESB + 128], 1.0, w=[("sm", "onesb")], tag="onesb")

    memset(SM[0:1, ONESF:ONESF + 128], 1.0, w=[("sm", "onesf")], tag="onesf")
    memset(SM[0:8, OC8:OC8 + 1], 1.0, w=[("sm", "onescol8")], tag="ones8")

    IDF_AP = SM[:, IDF:IDF + 128]
    IDB_AP = SMbf[:, IDB:IDB + 128]

    kb.op("pool", lambda e: e.iota(SM[0:1, GT:GT + 8], [[1, 8]],
                                   channel_multiplier=0,
                                   allow_small_or_imprecise_dtypes=True),
          w=[("sm", "tierow")], tag="tiota")
    ts(SM[0:1, GT:GT + 8], SM[0:1, GT:GT + 8], -(2.0 ** -13),
       7 * 2.0 ** -13, ALU.mult, ALU.add,
       r=[("sm", "tierow")], w=[("sm", "tierow")], tag="tiesc")
    kb.dma("sp", "ld_small",
           lambda e: e.dma_start(SM[0:1, GT + 8:GT + 16],
                                 Gb.ap().unsqueeze(0)),
           w=[("sm", "gbrow")], tag="gbrow")
    kb.dma("sp", "ld_small",
           lambda e: e.dma_start(SM[0:1, GT + 16:GT + 24],
                                 Rw.ap().unsqueeze(0)),
           w=[("sm", "rwrow")], tag="rwrow")

    def bcast8(dst_c, src_c, rkey, wkey, tag):
        mm(psF[:, 0:8], SM[0:1, ONESF:ONESF + 128], SM[0:1, src_c:src_c + 8],
           True, True, r=[("sm", "onesf"), rkey], w=[KF], tag=tag + "mm")
        acopy(SM[:, dst_c:dst_c + 8], psF[:, 0:8], r=[KF], w=[wkey],
              tag=tag + "ev")

    bcast8(BC, GT + 8, ("sm", "gbrow"), ("sm", "gb_bc"), "gbbc")
    bcast8(BC + 8, GT + 16, ("sm", "rwrow"), ("sm", "rw_bc"), "rwbc")
    bcast8(BC + 16, GT, ("sm", "tierow"), ("sm", "tie_bc"), "tiebc")

    GWST = 2 * (PH + 3072)
    kb.dma("pool", "gwld",
           lambda e: e.dma_start(SMbf[0:8, GWST:GWST + 1024], Gw.ap()),
           w=[("sm", "gwst")], tag="gwld")
    for k in range(8):
        tr(psT[:, k * 8:k * 8 + 8], SMbf[0:8, GWST + k * 128:GWST + (k + 1) * 128],
           IDB_AP, r=[("sm", "gwst"), ("sm", "idb")], w=[KT], tag=f"gwtr{k}")
    acopy(SMbf[:, GWTB:GWTB + 64], psT[:, 0:64], r=[KT], w=[("sm", "gwT")],
          tag="gwev")

    def gwT_block(k):
        return SMbf[:, GWTB + k * 8:GWTB + k * 8 + 8]

    def emit_ln(x_ap, kx, xn_ap, kxn, sq_scr, ksq, tag, par=0):
        SB = STATS + par * 8
        s0, s1c = SM[:, SB:SB + 1], SM[:, SB + 1:SB + 2]
        s2, s3 = SM[:, SB + 2:SB + 3], SM[:, SB + 3:SB + 4]
        s4, s5 = SM[:, SB + 4:SB + 5], SM[:, SB + 5:SB + 6]
        s6, s7 = SM[:, SB + 6:SB + 7], SM[:, SB + 7:SB + 8]
        red(s0, x_ap, "sum", r=[kx], w=[("sm", "st0", par)], tag=tag + "sum")
        tt(sq_scr, x_ap, x_ap, ALU.mult, r=[kx], w=[ksq], tag=tag + "sq")
        red(s1c, sq_scr, "sum", r=[ksq], w=[("sm", "st1", par)], tag=tag + "sqs")
        ts(s2, s0, 1.0 / D, None, ALU.mult, ALU.bypass,
           r=[("sm", "st0", par)], w=[("sm", "st2", par)], tag=tag + "mu")
        ts(s3, s2, s2, None, ALU.mult, ALU.bypass,
           r=[("sm", "st2", par)], w=[("sm", "st3", par)], tag=tag + "mu2")
        ts(s4, s1c, 1.0 / D, None, ALU.mult, ALU.bypass,
           r=[("sm", "st1", par)], w=[("sm", "st4", par)], tag=tag + "ex2")
        stt(s5, s4, LN_EPS, s3, ALU.add, ALU.subtract,
            r=[("sm", "st4", par), ("sm", "st3", par)], w=[("sm", "st5", par)],
            tag=tag + "var")
        act(s6, s5, AF.Ln, r=[("sm", "st5", par)], w=[("sm", "st6", par)],
            tag=tag + "ln")
        act(s7, s6, AF.Exp, r=[("sm", "st6", par)], w=[("sm", "st7", par)],
            tag=tag + "ex", bias=0.0, scale=-0.5)
        ts(xn_ap, x_ap, s2, s7, ALU.subtract, ALU.mult,
           r=[kx, ("sm", "st2", par), ("sm", "st7", par)], w=[kxn], tag=tag + "xn")

    def emit_t8(xn_ap, kxn, AR, artag, tt_i, tag):
        for k in range(8):
            tr(psT[:, k * 128:(k + 1) * 128], xn_ap[:, k * 128:(k + 1) * 128],
               IDB_AP, r=[kxn, ("sm", "idb")], w=[KT], tag=f"{tag}t{k}")
        dst = bass.AP(AR[:, 0:1].tensor, tt_i * 128,
                      [[AR.shape[1], 128], [T, 8], [1, 128]])
        acopy(dst, psT.ap().rearrange("p (k c) -> p k c", k=8),
              r=[KT], w=[artag + (tt_i,)], tag=f"{tag}ev")

    def wqkvT_block(k, ob):
        return AR2[:, k * 3072 + ob * 128: k * 3072 + ob * 128 + 128]

    for ob in range(24):
        st = ob % 2
        stage = SMB[:, 2048 + st * 1024: 2048 + (st + 1) * 1024]
        kst = ("smb", "wqst", st)
        kb.dma("pool", f"wq{st}",
               lambda e, ob=ob, stage=stage: e.dma_start(
                   stage, Wqkv.ap()[ob * 128:(ob + 1) * 128, :]),
               w=[kst], tag=f"wqld{ob}")
        for k in range(8):
            tr(psT[:, k * 128:(k + 1) * 128], stage[:, k * 128:(k + 1) * 128],
               IDB_AP, r=[kst, ("sm", "idb")], w=[KT], tag=f"wqtr{ob}_{k}")
        dst = bass.AP(AR2[:, 0:1].tensor, ob * 128,
                      [[33280, 128], [3072, 8], [1, 128]])
        acopy(dst, psT.ap().rearrange("p (k c) -> p k c", k=8),
              r=[KT], w=[("ar2", "wqkvT", ob)], tag=f"wqev{ob}")

    for tti in range(NT):
        st = tti % 2
        xt = SM[:, PH + st * 1024: PH + (st + 1) * 1024]
        xnt = SMB[:, st * 1024:(st + 1) * 1024]
        kx, kxn = ("sm", "xt", st), ("smb", "xn", st)
        kb.dma("sp", f"x{st}",
               lambda e, tti=tti, xt=xt: e.dma_start(
                   xt, X.ap()[tti * 128:(tti + 1) * 128, :]),
               w=[kx], tag=f"xld{tti}")
        emit_ln(xt, kx, xnt, kxn, AR3[:, st * 1024:(st + 1) * 1024],
                ("sm", "lnsq", st), f"ln0_{tti}_", par=st)
        emit_t8(xnt, kxn, AR1, ("ar1", "xnT"), tti, f"x0_{tti}")

        psg = psA0 if st == 0 else psA1
        kpg = KA0 if st == 0 else KA1
        GB = GT + st * 56
        for k in range(8):
            mm(psg[:, 0:8],
               AR1[:, k * T + tti * 128: k * T + tti * 128 + 128],
               gwT_block(k), k == 0, k == 7,
               r=[("ar1", "xnT", tti), ("sm", "gwT")], w=[kpg],
               tag=f"gl{tti}_{k}")
        g0, g1 = SM[:, GB:GB + 8], SM[:, GB + 8:GB + 16]
        l34 = SM[:, GB + 16:GB + 24]
        e8 = SM[:, GB + 24:GB + 32]
        c = lambda i: SM[:, GB + 32 + i:GB + 33 + i]
        tt(g0, psg[:, 0:8], SM[:, BC:BC + 8], ALU.add,
           r=[kpg, ("sm", "gb_bc")], w=[("sm", "glg", st)],
           tag=f"gev{tti}")
        tt(g1, g0, g0, ALU.mult, r=[("sm", "glg", st)], w=[("sm", "gsq", st)],
           tag=f"gsq{tti}")
        red(c(0), g1, "sum", r=[("sm", "gsq", st)], w=[("sm", "gms", st)],
            tag=f"gms{tti}")
        ts(c(0), c(0), 1.0 / NE, RMS_EPS, ALU.mult, ALU.add,
           r=[("sm", "gms", st)], w=[("sm", "gms", st)], tag=f"gme{tti}")
        act(c(1), c(0), AF.Ln, r=[("sm", "gms", st)], w=[("sm", "gln", st)],
            tag=f"gln{tti}")
        act(c(2), c(1), AF.Exp, r=[("sm", "gln", st)], w=[("sm", "grs", st)],
            tag=f"grs{tti}", bias=0.0, scale=-0.5)
        ts(l34, g0, c(2), None, ALU.mult, ALU.bypass,
           r=[("sm", "glg", st), ("sm", "grs", st)], w=[("sm", "gl3", st)],
           tag=f"gl3{tti}")
        tt(l34, l34, SM[:, BC + 8:BC + 16], ALU.mult,
           r=[("sm", "gl3", st), ("sm", "rw_bc")], w=[("sm", "gl4", st)],
           tag=f"gl4{tti}")
        red(c(3), l34, "max", r=[("sm", "gl4", st)],
            w=[("sm", "gm", st)], tag=f"gm{tti}")
        ts(c(4), c(3), -2.0, None, ALU.mult, ALU.bypass,
           r=[("sm", "gm", st)], w=[("sm", "gmb", st)], tag=f"gmb{tti}")
        act(e8, l34, AF.Exp,
            r=[("sm", "gl4", st), ("sm", "gmb", st)], w=[("sm", "ge8", st)],
            tag=f"gex{tti}", bias=c(4), scale=2.0)
        red(c(5), e8, "sum", r=[("sm", "ge8", st)],
            w=[("sm", "gsm", st)], tag=f"gs{tti}")
        ts(c(6), c(5), 1e-6, 1.0, ALU.mult, ALU.add,
           r=[("sm", "gsm", st)], w=[("sm", "gt2", st)], tag=f"gt{tti}")
        recip(SM[:, NORMG + tti:NORMG + tti + 1], c(6),
              r=[("sm", "gt2", st)], w=[("sm", "normg", tti)],
              tag=f"gng{tti}")

    for ob in range(24):
        stg = ob % 2
        stage = AR4[:, stg * 4096: stg * 4096 + T]
        kstage = ("ar4", "qkvst", stg)
        for tc in range(NTC):
            ps, kps = psb(tc % 4), KBs[tc % 4]
            for k in range(8):
                mm(ps[:, 0:512], wqkvT_block(k, ob),
                   AR1[:, k * T + tc * 512: k * T + tc * 512 + 512],
                   k == 0, k == 7,
                   r=[("ar2", "wqkvT", ob)] + [("ar1", "xnT", t4)
                      for t4 in range(tc * 4, tc * 4 + 4)],
                   w=[kps], tag=f"qkv{ob}_{tc}_{k}")
            scale = 0.125 if ob < 8 else 1.0
            act(stage[:, tc * 512:(tc + 1) * 512], ps[:, 0:512], AF.Copy,
                r=[kps], w=[kstage], tag=f"qe{ob}_{tc}", scale=scale)
        dstt = sc_qt if ob < 8 else (sc_kt if ob < 16 else sc_vt)
        row = (ob % 8) * 128
        kb.dma("sp", f"qkvo{stg}",
               lambda e, dstt=dstt, row=row, stage=stage: e.dma_start(
                   dstt.ap()[row:row + 128, :], stage[:, 0:T]),
               r=[kstage], tag=f"qst{ob}")

    for (Wsrc, Wdst, nm) in ((W1_0, sc_w10, "w10"), (W2_0, sc_w20, "w20"),
                             (W1_1, sc_w11, "w11"), (W2_1, sc_w21, "w21")):
        sz = Wsrc.ap().shape[1] * Wsrc.ap().shape[2]
        for e_i in range(NE):
            kb.dma("pool", "wcast",
                   lambda e, Wsrc=Wsrc, Wdst=Wdst, e_i=e_i, sz=sz:
                   e.dma_start(
                       bass.AP(Wdst, e_i * sz, [[sz, 1], [1, sz]]),
                       bass.AP(Wsrc, e_i * sz, [[sz, 1], [1, sz]])),
                   r=[("ar1", "xnT", NT - 1)],
                   w=[("hbm", nm, e_i)], tag=f"wc{nm}_{e_i}")

    kb.dma("sp", "xn0sp",
           lambda e: e.dma_start(sc_xn0t.ap(), AR1[:, 0:8 * T]),
           r=[("ar1", "xnT", t) for t in range(NT)], tag="xn0out")

    kb.barrier()

    def vaug_block(h, jt):
        c = (jt * NH + h) * 65
        return AR2[:, c:c + 65]

    memset(bass.AP(AR2[:, 0:1].tensor, 64,
                   [[33280, 128], [65, NH * NJT], [1, 1]]), 1.0,
           w=[("ar2", "vones")], tag="vones")

    for h in range(NH):
        stg = h % 2
        vstage = AR4[:, stg * 4096: stg * 4096 + T]
        kvst = ("ar4", "vst", stg)
        kb.dma("sp", f"vld{stg}",
               lambda e, h=h, vstage=vstage: e.dma_start(
                   vstage[0:64, :], sc_vt.ap()[h * 64:(h + 1) * 64, :]),
               w=[kvst], tag=f"vld{h}")
        for j0 in range(0, NJT, 8):
            nspan = min(8, NJT - j0)
            for jj in range(nspan):
                jt = j0 + jj
                tr(psT[:, jj * 64:jj * 64 + 64],
                   vstage[0:64, jt * 128:(jt + 1) * 128], IDB_AP,
                   r=[kvst, ("sm", "idb")], w=[KT], tag=f"vt{h}_{jt}")
            dst = bass.AP(AR2[:, 0:1].tensor, (j0 * NH + h) * 65,
                          [[33280, 128], [65 * NH, nspan], [1, 64]])
            src = bass.AP(psT[:, 0:1].tensor, 0,
                          [[1024, 128], [64, nspan], [1, 64]])
            acopy(dst, src, r=[KT], w=[("ar2", "va", h, j0 // 8)],
                  tag=f"vev{h}_{j0}")

    kb.barrier()
    memset(SMB[64:65, 0:T], 1.0, w=[("smb", "kones")], tag="kones0")
    memset(AR4[64:65, 0:T], 1.0, w=[("ar4", "kones")], tag="kones1")
    NQT = QG // 128

    def kaug_buf(par):
        return SMB if par == 0 else AR4

    def kkey(par):
        return ("smb", "kaug") if par == 0 else ("ar4", "kaug")

    QA = [8192 - 2 * QG, 8192 - QG]
    pend_fin = [None]

    for g in range(NQG):
        for h in range(NH):
            par = h % 2
            KB_ = kaug_buf(par)
            QA0 = QA[par]
            kq = ("ar4", "qaug", par)
            kqm = ("ar4", "qaugm", par)
            kb.dma("sp", "kaug" + str(par),
                   lambda e, h=h, KB_=KB_: e.dma_start(
                       KB_[0:64, 0:T], sc_kt.ap()[h * 64:(h + 1) * 64, :]),
                   w=[kkey(par)], tag=f"kld{g}_{h}")
            kb.dma("sp", "qaug" + str(par),
                   lambda e, g=g, h=h, QA0=QA0: e.dma_start(
                       AR4[0:64, QA0:QA0 + QG],
                       sc_qt.ap()[h * 64:(h + 1) * 64, g * QG:(g + 1) * QG]),
                   w=[kq], tag=f"qld{g}_{h}")
            for q0 in range(0, NQT, 4):
                nb = min(4, NQT - q0)
                for i in range(nb):
                    qt = q0 + i
                    rhs = bass.AP(KB_[:, 0:1].tensor, 0,
                                  [[KB_.shape[1], 64], [NJT, 128]])
                    mm(psF[:, i * 128:(i + 1) * 128],
                       AR4[0:64, QA0 + qt * 128: QA0 + qt * 128 + 128], rhs,
                       True, True, r=[kq, kkey(par)],
                       w=[KF], tag=f"pa{g}_{h}_{qt}")
                for i in range(nb):
                    qt = q0 + i
                    red(SMbf[:, 2 * (PH + 4) + qt:2 * (PH + 4) + qt + 1],
                        psF[:, i * 128:(i + 1) * 128], "max",
                        r=[KF], w=[("sm", "negm", qt)],
                        tag=f"pam{g}_{h}_{qt}", negate=True)
            for qt in range(NQT):
                tr(psT[0:1, qt * 128:(qt + 1) * 128],
                   SMbf[:, 2 * (PH + 4) + qt:2 * (PH + 4) + qt + 1], IDB_AP,
                   r=[("sm", "negm", qt), ("sm", "idb")], w=[KT],
                   tag=f"pat{g}_{h}_{qt}")
            acopy(AR4[64:65, QA0:QA0 + QG], psT[0:1, 0:QG],
                  r=[KT], w=[kqm], tag=f"pam2{g}_{h}")
            if pend_fin[0] is not None:
                pend_fin[0]()
                pend_fin[0] = None
            NQC = QG // 512

            def bmm(jt):
                half = (jt % 2) * QG
                for qc in range(NQC):
                    ii = (half + qc * 512) // 512
                    mm(psB[:, half + qc * 512: half + (qc + 1) * 512],
                       KB_[0:65, jt * 128:(jt + 1) * 128],
                       AR4[0:65, QA0 + qc * 512: QA0 + qc * 512 + 512],
                       True, True,
                       r=[kkey(par), ("smb" if par == 0 else "ar4", "kones"),
                          kq, kqm],
                       w=[KBs[ii]], tag=f"pb{g}_{h}_{jt}_{qc}")

            def bexp(u):
                width = min(2 * QG, 2048)
                act(AR1[:, 2 * u * QG: 2 * u * QG + width],
                    psB[:, 0:width], AF.Exp,
                    r=KBs[0:width // 512],
                    w=[("ar1", "sT", 2 * u), ("ar1", "sT", 2 * u + 1)],
                    tag=f"pbe{g}_{h}_{u}")

            def avmm(jt):
                for qc in range(NQC):
                    ps, kps = (psA0, KA0) if qc == 0 else (psA1, KA1)
                    mm(ps[0:65, 0:512], vaug_block(h, jt),
                       AR1[:, jt * QG + qc * 512: jt * QG + qc * 512 + 512],
                       jt == 0, jt == NJT - 1,
                       r=[("ar2", "va", h, jt // 8), ("ar2", "vones"),
                          ("ar1", "sT", jt)],
                       w=[kps], tag=f"av{g}_{h}_{qc}_{jt}")

            bmm(0)
            bmm(1)
            bexp(0)
            for u in range(1, NJT // 2):
                bmm(2 * u)
                bmm(2 * u + 1)
                bexp(u)
                avmm(2 * (u - 1))
                avmm(2 * (u - 1) + 1)
            avmm(NJT - 2)
            avmm(NJT - 1)
            def fin_head(g=g, h=h):
                for qc in range(NQC):
                    ps, kps = (psA0, KA0) if qc == 0 else (psA1, KA1)
                    acopy(SM[0:65, PH + 16 + qc * 512: PH + 16 + qc * 512 + 512],
                          ps[0:65, 0:512], r=[kps], w=[("sm", "oh", qc)],
                          tag=f"ave{g}_{h}_{qc}")
                for qt in range(NQT):
                    psf, kpsf = psb(qt % 4), KBs[qt % 4]
                    tr(psf[:, 0:65],
                       SM[0:65, PH + 16 + qt * 128: PH + 16 + qt * 128 + 128],
                       IDF_AP, r=[("sm", "oh", qt * 128 // 512), ("sm", "idf")],
                       w=[kpsf], tag=f"ot{g}_{h}_{qt}")
                    recip(SM[:, PH + 2 + qt % 2:PH + 3 + qt % 2], psf[:, 64:65],
                          r=[kpsf], w=[("sm", "linv", qt % 2)],
                          tag=f"ol{g}_{h}_{qt}")
                    ts(AR3[:, qt * 1024 + h * 64: qt * 1024 + h * 64 + 64],
                       psf[:, 0:64], SM[:, PH + 2 + qt % 2:PH + 3 + qt % 2],
                       None, ALU.mult,
                       ALU.bypass, r=[kpsf, ("sm", "linv", qt % 2)],
                       w=[("ar3", "sg", qt, h)], tag=f"os{g}_{h}_{qt}")
            pend_fin[0] = fin_head
        if pend_fin[0] is not None:
            pend_fin[0]()
            pend_fin[0] = None
        s0dst = bass.AP(sc_s0, g * QG * 1024,
                        [[1024, 128], [128 * 1024, NQT], [1, 1024]])
        kb.dma("sp", "s0o",
               lambda e, s0dst=s0dst: e.dma_start(s0dst, AR3[:, 0:NQT * 1024]),
               r=[("ar3", "sg", qt, h) for qt in range(NQT)
                  for h in range(NH)],
               tag=f"s0st{g}")

    kb.barrier()

    def emit_moe(layer, W1d, B1d, W2d, B2d, H, finalize_tile,
                 gr_a, gr_tag, hb, xnT_blk,
                 quarter_pre=None, granule_post=None, quarter_post=None):
        NHQ = H // 1024
        B1ST = PH + 3072
        B1C = PH + 3200
        for e_i in range(NE):
            nrow = H // 128
            kb.dma("sp", "b1ld",
                   lambda e, e_i=e_i, nrow=nrow: e.dma_start(
                       SM[0:nrow, B1ST:B1ST + 128],
                       B1d.ap()[e_i:e_i + 1, :].rearrange(
                           "one (a b) -> (one a) b", b=128)),
                   w=[("sm", "stage1")], tag=f"b1l{layer}_{e_i}")
            tr(psF[:, 0:nrow], SM[0:nrow, B1ST:B1ST + 128], IDF_AP,
               r=[("sm", "stage1"), ("sm", "idf")], w=[KF],
               tag=f"b1t{layer}_{e_i}")
            acopy(SM[:, B1C + e_i * nrow: B1C + (e_i + 1) * nrow],
                  psF[:, 0:nrow], r=[KF], w=[("sm", "b1c", e_i)],
                  tag=f"b1e{layer}_{e_i}")
        B2B = PH + 3456
        if layer == 0:
            kb.dma("sp", "b2ld",
                   lambda e: e.dma_start(SM[0:8, B2B:B2B + 1024], B2d.ap()),
                   w=[("sm", "b2f")], tag="b2ld0")
            for dc in range(2):
                mm(psF[0:1, 0:512], SM[0:8, OC8:OC8 + 1],
                   SM[0:8, B2B + dc * 512:B2B + dc * 512 + 512], True, True,
                   r=[("sm", "onescol8"), ("sm", "b2f")], w=[KF],
                   tag=f"b2s{dc}")
                acopy(SM[0:1, PH + dc * 512:PH + dc * 512 + 512],
                      psF[0:1, 0:512], r=[KF], w=[("sm", "p2g")],
                      tag=f"b2se{dc}")
            for dc in range(2):
                mm(psF[:, 0:512], SM[0:1, ONESF:ONESF + 128],
                   SM[0:1, PH + dc * 512:PH + dc * 512 + 512],
                   True, True,
                   r=[("sm", "onesf"), ("sm", "p2g")], w=[KF],
                   tag=f"b2bc{dc}")
                acopy(SM[:, B2B + dc * 512:B2B + dc * 512 + 512],
                      psF[:, 0:512], r=[KF],
                      w=[("sm", "b2bc"), ("sm", "b2f")],
                      tag=f"b2bce{dc}")
        else:
            kb.dma("pool", "b2ld1",
                   lambda e: e.dma_start(SMB[0:8, 2176:2176 + 1024], B2d.ap()),
                   w=[("smb", "b2sb")], tag="b2ld1")

        for q in range(NQ):
            if quarter_pre is not None:
                quarter_pre(q)
            if layer == 1:
                for mt in range(TQ // 128):
                    tti2 = q * (TQ // 128) + mt
                    tr(psF[0:8, 0:128],
                       SM[:, GATES + tti2 * 8:GATES + tti2 * 8 + 8],
                       IDF_AP, r=[("sm", "gates", tti2), ("sm", "idf")],
                       w=[KF], tag=f"gTt{tti2}")
                    acopy(SMB[0:8, 2048:2048 + 128], psF[0:8, 0:128],
                          r=[KF], w=[("smb", "gTm")], tag=f"gTe{tti2}")
                    for dc in range(2):
                        mm(psF[:, 0:512],
                           SMB[0:8, 2048:2048 + 128],
                           SMB[0:8, 2176 + dc * 512:2176 + dc * 512 + 512],
                           True, True,
                           r=[("smb", "gTm"), ("smb", "b2sb")],
                           w=[KF], tag=f"b2m{q}_{mt}_{dc}")
                        acopy(AR3[:, mt * 1024 + dc * 512:
                                  mt * 1024 + dc * 512 + 512],
                              psF[:, 0:512], r=[KF],
                              w=[("ar3", "acc", mt, dc)],
                              tag=f"b2me{q}_{mt}_{dc}")
            gi = 0
            for e_i in range(NE):
                for hq in range(NHQ):
                    sl = (q * NE * NHQ + e_i * NHQ + hq) % 2
                    W1g = gr_a[:, sl * 16384: sl * 16384 + 8192]
                    W2g = gr_a[:, sl * 16384 + 8192: sl * 16384 + 16384]
                    k1, k2 = (gr_tag, "w1g", sl), (gr_tag, "w2g", sl)
                    W1b, W2b = (sc_w10, sc_w20) if layer == 0 else \
                        (sc_w11, sc_w21)
                    nm1, nm2 = ("w10", "w20") if layer == 0 else \
                        ("w11", "w21")
                    w1src = bass.AP(W1b, e_i * D * H + hq * 1024,
                                    [[H, 128], [128 * H, 8], [1, 1024]])
                    w2src = bass.AP(W2b, e_i * H * D + hq * 1024 * D,
                                    [[D, 128], [128 * D, 8], [1, 1024]])
                    kb.dma("sp", f"w1g{sl}",
                           lambda e, w1src=w1src, W1g=W1g: e.dma_start(
                               W1g, w1src),
                           r=[("hbm", nm1, e_i)],
                           w=[k1], tag=f"w1l{layer}_{q}_{e_i}_{hq}")
                    kb.dma("sp", f"w2g{sl}",
                           lambda e, w2src=w2src, W2g=W2g: e.dma_start(
                               W2g, w2src),
                           r=[("hbm", nm2, e_i)],
                           w=[k2], tag=f"w2l{layer}_{q}_{e_i}_{hq}")
                    nrow = H // 128
                    for hm in range(8):
                        for tc in range(TQ // 512):
                            ii = (hm * (TQ // 512) + tc) % 4
                            ps, kps = psb(ii), KBs[ii]
                            for k in range(8):
                                xap, xkeys = xnT_blk(k, q, tc)
                                mm(ps[:, 0:512],
                                   W1g[:, k * 1024 + hm * 128:
                                       k * 1024 + hm * 128 + 128],
                                   xap, k == 0, k == 7,
                                   r=[k1] + xkeys,
                                   w=[kps],
                                   tag=f"m1_{layer}_{q}_{e_i}_{hq}_{hm}_{tc}_{k}")
                            bcol = SM[:, B1C + e_i * nrow + hq * 8 + hm:
                                      B1C + e_i * nrow + hq * 8 + hm + 1]
                            hap, hkey = hb(hm, tc)
                            act(hap, ps[:, 0:512], AF.Gelu,
                                r=[kps, ("sm", "b1c", e_i)],
                                w=[hkey],
                                tag=f"ge{layer}_{q}_{e_i}_{hq}_{hm}_{tc}",
                                bias=bcol)
                    for mt in range(TQ // 128):
                        for dc in range(2):
                            ps, kps = ((psA0, KA0) if (mt * 2 + dc) % 2 == 0
                                       else (psA1, KA1))
                            for hk in range(8):
                                hap, hkey = hb(hk, (mt * 128) // 512)
                                mm(ps[:, 0:512],
                                   hap[:, mt * 128 - ((mt * 128) // 512) * 512:
                                       mt * 128 - ((mt * 128) // 512) * 512 + 128],
                                   W2g[:, hk * 1024 + dc * 512:
                                       hk * 1024 + dc * 512 + 512],
                                   hk == 0, hk == 7,
                                   r=[k2, hkey],
                                   w=[kps],
                                   tag=f"m2_{layer}_{q}_{e_i}_{hq}_{mt}_{dc}_{hk}")
                            accp = AR3[:, mt * 1024 + dc * 512:
                                       mt * 1024 + dc * 512 + 512]
                            kacc = ("ar3", "acc", mt, dc)
                            if layer == 0:
                                if e_i == 0 and hq == 0:
                                    dcopy(accp, ps[:, 0:512], r=[kps],
                                          w=[kacc],
                                          tag=f"ac0_{layer}_{q}_{e_i}_{hq}_{mt}_{dc}")
                                else:
                                    tt(accp, ps[:, 0:512], accp, ALU.add,
                                       r=[kps, kacc], w=[kacc],
                                       tag=f"ac_{layer}_{q}_{e_i}_{hq}_{mt}_{dc}")
                            else:
                                gcol = SM[:, GATES + (q * (TQ // 128) + mt) * 8
                                          + e_i: GATES +
                                          (q * (TQ // 128) + mt) * 8 + e_i + 1]
                                stt(accp, ps[:, 0:512], gcol, accp,
                                    ALU.mult, ALU.add,
                                    r=[kps, kacc, ("sm", "gates",
                                                   q * (TQ // 128) + mt)],
                                    w=[kacc],
                                    tag=f"ac_{layer}_{q}_{e_i}_{hq}_{mt}_{dc}")
                    if granule_post is not None:
                        granule_post(q, gi)
                    gi += 1
            if quarter_post is not None:
                quarter_post(q)
            for mt in range(TQ // 128):
                if finalize_tile is not None:
                    finalize_tile(q, mt)

    kb.dma("sp", "xn0ld",
           lambda e: e.dma_start(AR1[:, 0:8 * T], sc_xn0t.ap()),
           w=[("ar1", "xnT", t) for t in range(NT)], tag="xn0in")

    def fin_l0(q, mt):
        tti = q * (TQ // 128) + mt
        gt = SM[:, PH:PH + 1024]
        xt = SM[:, PH + 1024:PH + 2048]
        stl = SM[:, PH + 2048:PH + 3072]
        kg, kx, ks = ("sm", "p2g"), ("sm", "p2x"), ("sm", "p2s")
        ngcol = SM[:, NORMG + tti:NORMG + tti + 1]
        racc = [("ar3", "acc", mt, dc) for dc in range(2)]
        tt(stl, AR3[:, mt * 1024:(mt + 1) * 1024],
           SM[:, PH + 3456:PH + 3456 + 1024], ALU.add,
           r=racc + [("sm", "b2bc")], w=[ks], tag=f"ab2{q}_{mt}")
        act(gt, stl, AF.Silu,
            r=[ks, ("sm", "normg", tti)], w=[kg], tag=f"sil{q}_{mt}",
            scale=ngcol)
        kb.dma("sp", "xld2",
               lambda e, tti=tti, xt=xt: e.dma_start(
                   xt, X.ap()[tti * 128:(tti + 1) * 128, :]),
               w=[kx], tag=f"x2l{tti}")
        kb.dma("sp", "sld",
               lambda e, tti=tti, stl=stl: e.dma_start(
                   stl, sc_s0.ap()[tti * 128:(tti + 1) * 128, :]),
               r=[kg], w=[ks], tag=f"s0l{tti}")
        tt(gt, stl, gt, ALU.mult, r=[ks, kg], w=[kg], tag=f"sg{tti}")
        tt(xt, gt, xt, ALU.add, r=[kg, kx], w=[kx], tag=f"x1_{tti}")
        kb.dma("sp", "x1o",
               lambda e, tti=tti, xt=xt: e.dma_start(
                   sc_x1.ap()[tti * 128:(tti + 1) * 128, :], xt),
               r=[kx], tag=f"x1s{tti}")
        xnt = SMB[:, (tti % 2) * 1024:(tti % 2) * 1024 + 1024]
        kxn = ("smb", "xn1", tti % 2)
        emit_ln(xt, kx, xnt, kxn, gt, kg, f"ln1_{tti}_", par=tti % 2)
        kb.dma("sp", "xn1o",
               lambda e, tti=tti, xnt=xnt: e.dma_start(
                   sc_xn1.ap()[tti * 128:(tti + 1) * 128, :], xnt),
               r=[kxn], tag=f"xn1s{tti}")

    def xnT_l0(k, q, tc):
        tok = q * TQ + tc * 512
        return (AR1[:, k * T + tok: k * T + tok + 512],
                [("ar1", "xnT", t4) for t4 in
                 range(tok // 128, tok // 128 + 4)])

    emit_moe(0, W1_0, B1_0, W2_0, B2_0, H0, fin_l0,
             AR2, "ar2", lambda hm, tc: (
                 AR4[:, hm * TQ + tc * 512: hm * TQ + (tc + 1) * 512],
                 ("ar4", "hbuf", hm, tc)),
             xnT_l0)
    kb.barrier()

    AR4f = AR4.ap().bitcast(F32)
    SMBf = SMB.ap().bitcast(F32)
    XIN0 = 8192
    HB0 = 16384
    WSL = 24576
    sc_winT = nc.dram_tensor("sc_winT", [128, 8192], BF16, kind="Internal")
    sc_woT = nc.dram_tensor("sc_woT", [128, 8192], BF16, kind="Internal")

    SSP = PH + 3456
    DTC = SSP + 384
    DSC = SSP + 392
    EAC = SSP + 400
    ST0 = SSP + 448
    kb.dma("sp", "dtld",
           lambda e: e.dma_start(SM[0:8, PH + 3072:PH + 3072 + 128],
                                 DTlog.ap().rearrange("(a b) -> a b", b=128)),
           w=[("sm", "stage1")], tag="dtld")
    tr(psF[:, 0:8], SM[0:8, PH + 3072:PH + 3072 + 128], IDF_AP,
       r=[("sm", "stage1"), ("sm", "idf")], w=[KF], tag="dttr")
    act(SM[:, DTC:DTC + 8], psF[:, 0:8], AF.Exp,
        r=[KF], w=[("sm", "dtc")], tag="dtex")
    kb.dma("sp", "dtld",
           lambda e: e.dma_start(SM[0:8, PH + 3072:PH + 3072 + 128],
                                 Dssm.ap().rearrange("(a b) -> a b", b=128)),
           r=[KF], w=[("sm", "stage1")], tag="dsld")
    tr(psF[:, 8:16], SM[0:8, PH + 3072:PH + 3072 + 128], IDF_AP,
       r=[("sm", "stage1"), ("sm", "idf")], w=[KF], tag="dstr")
    acopy(SM[:, DSC:DSC + 8], psF[:, 8:16],
          r=[KF], w=[("sm", "dsc")], tag="dsev")

    for ct in range(8):
        base = SSP + ct * 48
        kb.dma("sp", "assld",
               lambda e, ct=ct: e.dma_start(
                   SM[:, EAC + 32:EAC + 48],
                   Alog.ap()[ct * 128:(ct + 1) * 128, :]),
               w=[("sm", "alst")], tag=f"ald{ct}")
        kb.dma("sp", "assld",
               lambda e, ct=ct, base=base: e.dma_start(
                   SM[:, base + 16:base + 32],
                   Bssm.ap()[ct * 128:(ct + 1) * 128, :]),
               w=[("sm", "bssm", ct)], tag=f"bld{ct}")
        kb.dma("sp", "assld",
               lambda e, ct=ct, base=base: e.dma_start(
                   SM[:, base + 32:base + 48],
                   Cssm.ap()[ct * 128:(ct + 1) * 128, :]),
               w=[("sm", "cssm", ct)], tag=f"cld{ct}")
        eA = SM[:, EAC:EAC + 16]
        t1 = SM[:, EAC + 16:EAC + 32]
        act(eA, SM[:, EAC + 32:EAC + 48], AF.Exp,
            r=[("sm", "alst")], w=[("sm", "eA")], tag=f"eA{ct}")
        ts(t1, eA, SM[:, DTC + ct:DTC + ct + 1], -1.0, ALU.mult,
           ALU.mult, r=[("sm", "eA"), ("sm", "dtc")], w=[("sm", "t1")],
           tag=f"dtA{ct}")
        act(SM[:, base:base + 16], t1, AF.Exp,
            r=[("sm", "t1")], w=[("sm", "abar", ct)], tag=f"ab{ct}")
        ts(t1, SM[:, base:base + 16], 1.0, -1.0, ALU.subtract, ALU.mult,
           r=[("sm", "abar", ct)], w=[("sm", "t1")], tag=f"bb1{ct}")
        recip(SM[:, EAC + 32:EAC + 48], eA, r=[("sm", "eA"), ("sm", "alst")],
              w=[("sm", "alst")], tag=f"bb2{ct}")
        tt(t1, t1, SM[:, EAC + 32:EAC + 48], ALU.mult,
           r=[("sm", "t1"), ("sm", "alst")], w=[("sm", "t1")], tag=f"bb3{ct}")
        tt(SM[:, base + 16:base + 32], t1, SM[:, base + 16:base + 32],
           ALU.mult, r=[("sm", "t1"), ("sm", "bssm", ct)],
           w=[("sm", "bbar", ct)], tag=f"bb4{ct}")
    memset(SM[:, ST0:ST0 + 128], 0.0,
           w=[("sm", "sst", ct, n) for ct in range(8) for n in range(NS)],
           tag="st0")

    kb.dma("pool", "centld",
           lambda e: e.dma_start(AR2[0:8, WSL:WSL + 1024], Cent.ap()),
           w=[("ar2", "wslot")], tag="centld")
    for k in range(8):
        tr(psT[:, k * 8:k * 8 + 8],
           AR2[0:8, WSL + k * 128:WSL + (k + 1) * 128],
           IDB_AP, r=[("ar2", "wslot"), ("sm", "idb")], w=[KT],
           tag=f"ctr{k}")
    acopy(SMbf[:, CENTB:CENTB + 64], psT[:, 0:64], r=[KT],
          w=[("sm", "centT")], tag="cev")

    def build_wT(Wd, sc_dst, tag):
        for rb in range(8):
            kb.dma("pool", "wiold",
                   lambda e, rb=rb: e.dma_start(
                       AR2[:, HB0:HB0 + 1024],
                       Wd.ap()[rb * 128:(rb + 1) * 128, :]),
                   w=[("ar2", "hbuf", 0, 0), ("ar2", "hbuf", 0, 1)],
                   tag=f"{tag}l{rb}")
            for k in range(8):
                tr(psT[:, k * 128:(k + 1) * 128],
                   AR2[:, HB0 + k * 128:HB0 + (k + 1) * 128], IDB_AP,
                   r=[("ar2", "hbuf", 0, 0), ("ar2", "hbuf", 0, 1),
                      ("sm", "idb")], w=[KT],
                   tag=f"{tag}t{rb}_{k}")
            dst = bass.AP(AR2[:, 0:1].tensor, WSL + rb * 128,
                          [[33280, 128], [1024, 8], [1, 128]])
            acopy(dst, psT.ap().rearrange("p (k c) -> p k c", k=8),
                  r=[KT], w=[("ar2", "wslot")], tag=f"{tag}e{rb}")
        kb.dma("sp", "wTout",
               lambda e: e.dma_start(sc_dst.ap(), AR2[:, WSL:WSL + 8192]),
               r=[("ar2", "wslot")], w=[("hbm", tag)], tag=f"{tag}sp")

    build_wT(Wout, sc_woT, "wo")
    build_wT(Win, sc_winT, "wi")

    def xnT_l1(k, q, tc):
        return (AR2[:, k * TQ + tc * 512: k * TQ + tc * 512 + 512],
                [("ar2", "xn1Tq")])

    def p34_quarter_pre(q):
        if q > 0:
            kb.dma("sp", "wTin",
                   lambda e: e.dma_start(AR2[:, WSL:WSL + 8192],
                                         sc_winT.ap()),
                   r=[("hbm", "wi")], w=[("ar2", "wslot")], tag=f"wTin{q}")
        for mt in range(TQ // 128):
            tti = q * (TQ // 128) + mt
            par = mt % 2
            stg = AR4[:, par * 2048:par * 2048 + 1024]
            kb.dma("sp", f"xn1l{par}",
                   lambda e, tti=tti, stg=stg: e.dma_start(
                       stg, sc_xn1.ap()[tti * 128:(tti + 1) * 128, :]),
                   w=[("sm", "bx", par)], tag=f"x1ld{tti}")
            for k in range(8):
                tr(psT[:, k * 128:(k + 1) * 128],
                   stg[:, k * 128:(k + 1) * 128], IDB_AP,
                   r=[("sm", "bx", par), ("sm", "idb")], w=[KT],
                   tag=f"x1t{tti}_{k}")
            dst = bass.AP(AR2[:, 0:1].tensor, mt * 128,
                          [[33280, 128], [TQ, 8], [1, 128]])
            acopy(dst, psT.ap().rearrange("p (k c) -> p k c", k=8),
                  r=[KT], w=[("ar2", "xn1Tq")], tag=f"x1e{tti}")
        for mt in range(TQ // 128):
            tti = q * (TQ // 128) + mt
            for k in range(8):
                mm(psF[:, 8:16],
                   AR2[:, k * TQ + mt * 128: k * TQ + mt * 128 + 128],
                   SMbf[:, CENTB + k * 8:CENTB + k * 8 + 8], k == 0, k == 7,
                   r=[("ar2", "xn1Tq"), ("sm", "centT")], w=[KF],
                   tag=f"cl{tti}_{k}")
            gg = lambda i: SM[:, GT + i * 8:GT + (i + 1) * 8]
            cc = lambda i: SM[:, GT + 96 + i:GT + 97 + i]
            act(gg(0), psF[:, 8:16], AF.Sigmoid, r=[KF],
                w=[("sm", "cg")], tag=f"csg{tti}")
            ts(gg(1), psF[:, 8:16], 16.635532, None, ALU.min, ALU.bypass,
               r=[KF], w=[("sm", "crc")], tag=f"crc{tti}")
            tt(gg(1), gg(1), SM[:, BC + 16:BC + 24], ALU.add,
               r=[("sm", "crc"), ("sm", "tie_bc")], w=[("sm", "cr")],
               tag=f"cr{tti}")
            red(cc(0), gg(1), "max", r=[("sm", "cr")], w=[("sm", "cm1")],
                tag=f"cm1{tti}")
            ts(gg(2), gg(1), cc(0), None, ALU.is_equal, ALU.bypass,
               r=[("sm", "cr"), ("sm", "cm1")], w=[("sm", "ceq")],
               tag=f"ceq{tti}")
            stt(gg(3), gg(2), -1e30, gg(1), ALU.mult, ALU.add,
                r=[("sm", "ceq"), ("sm", "cr")], w=[("sm", "cr2")],
                tag=f"cr2{tti}")
            red(cc(1), gg(3), "max", r=[("sm", "cr2")], w=[("sm", "cm2")],
                tag=f"cm2{tti}")
            ts(gg(4), gg(1), cc(1), None, ALU.is_ge, ALU.bypass,
               r=[("sm", "cr"), ("sm", "cm2")], w=[("sm", "csel")],
               tag=f"csel{tti}")
            tt(gg(5), gg(0), gg(4), ALU.mult,
               r=[("sm", "cg"), ("sm", "csel")], w=[("sm", "cmsk")],
               tag=f"cmsk{tti}")
            red(cc(2), gg(5), "sum", r=[("sm", "cmsk")], w=[("sm", "csum")],
                tag=f"csum{tti}")
            recip(cc(3), cc(2), r=[("sm", "csum")], w=[("sm", "crs")],
                  tag=f"crs{tti}")
            ts(SM[:, GATES + tti * 8:GATES + tti * 8 + 8], gg(5), cc(3),
               None, ALU.mult, ALU.bypass,
               r=[("sm", "cmsk"), ("sm", "crs")], w=[("sm", "gates", tti)],
               tag=f"cgf{tti}")
        for cm in range(8):
            for tc in range(TQ // 512):
                ii = (cm * (TQ // 512) + tc) % 4
                ps, kps = psb(ii), KBs[ii]
                for k in range(8):
                    mm(ps[:, 0:512],
                       AR2[:, WSL + k * 1024 + cm * 128:
                           WSL + k * 1024 + cm * 128 + 128],
                       AR2[:, k * TQ + tc * 512: k * TQ + tc * 512 + 512],
                       k == 0, k == 7,
                       r=[("ar2", "wslot"), ("ar2", "xn1Tq")],
                       w=[kps], tag=f"wi{q}_{cm}_{tc}_{k}")
                acopy(AR2[:, XIN0 + cm * TQ + tc * 512:
                          XIN0 + cm * TQ + tc * 512 + 512],
                      ps[:, 0:512], r=[kps],
                      w=[("ar2", "xin", cm)], tag=f"wie{q}_{cm}_{tc}")
        kb.dma("sp", "wTin",
               lambda e: e.dma_start(AR2[:, WSL:WSL + 8192], sc_woT.ap()),
               r=[("hbm", "wo")], w=[("ar2", "wslot")], tag=f"woTin{q}")

    def p34_scan_piece(q, ct, nseg):
        base = SSP + ct * 48
        kxin = ("ar2", "xin", ct)
        xin_ap = AR2[:, XIN0 + ct * TQ: XIN0 + ct * TQ + TQ]
        YY = AR4f[:, 3072:3072 + TQ]
        HH = AR4f[:, 2048:2048 + TQ]
        if nseg == 0:
            ts(YY, xin_ap, SM[:, DSC + ct:DSC + ct + 1], None,
               ALU.mult, ALU.bypass, r=[kxin, ("sm", "dsc")],
               w=[("sm", "yy")], tag=f"yD{q}_{ct}")
        for n in range(nseg * 4, nseg * 4 + 4):
            par = n % 2
            bxp = AR4f[:, par * 1024:par * 1024 + TQ]
            kbx = ("sm", "bx", par)
            act(bxp, xin_ap, AF.Copy,
                r=[kxin, ("sm", "bbar", ct)], w=[kbx],
                tag=f"bx{q}_{ct}_{n}",
                scale=SM[:, base + 16 + n:base + 17 + n])
            kb.op("dve", lambda e, base=base, n=n, ct=ct, bxp=bxp:
                  e.tensor_tensor_scan(
                      HH, SM[:, base + n:base + n + 1].broadcast_to(
                          [128, TQ]),
                      bxp, SM[:, ST0 + ct * 16 + n:ST0 + ct * 16 + n + 1],
                      ALU.mult, ALU.add),
                  r=[kbx, ("sm", "abar", ct), ("sm", "sst", ct, n)],
                  w=[("sm", "hh")], tag=f"sc{q}_{ct}_{n}")
            dcopy(SM[:, ST0 + ct * 16 + n:ST0 + ct * 16 + n + 1],
                  HH[:, TQ - 1:TQ], r=[("sm", "hh")],
                  w=[("sm", "sst", ct, n)], tag=f"sv{q}_{ct}_{n}")
            stt(YY, HH, SM[:, base + 32 + n:base + 33 + n], YY,
                ALU.mult, ALU.add,
                r=[("sm", "hh"), ("sm", "cssm", ct), ("sm", "yy")],
                w=[("sm", "yy")], tag=f"ya{q}_{ct}_{n}")
        if nseg == 3:
            dcopy(bass.AP(AR2[:, 0:1].tensor, XIN0 + ct * TQ,
                          [[33280, 128], [1, TQ]]), YY,
                  r=[("sm", "yy")], w=[kxin, ("ar2", "yb", ct)],
                  tag=f"yb{q}_{ct}", eng="pool")

    def p34_quarter_post(q):
        rot = [(psA0, KA0), (psA1, KA1),
               (psb(0), KBs[0]), (psb(1), KBs[1])]
        for mt in range(TQ // 128):
            for dc in range(2):
                ps, kps = rot[(mt * 2 + dc) % 4]
                for ct in range(8):
                    mm(ps[:, 0:512],
                       bass.AP(AR2[:, 0:1].tensor,
                               XIN0 + ct * TQ + mt * 128,
                               [[33280, 128], [1, 128]]),
                       AR2[:, WSL + ct * 1024 + dc * 512:
                           WSL + ct * 1024 + dc * 512 + 512],
                       ct == 0, ct == 7,
                       r=[("ar2", "yb", ct), ("ar2", "wslot")],
                       w=[kps], tag=f"wo{q}_{mt}_{dc}_{ct}")
                acopy(SMBf[:, dc * 512:dc * 512 + 512],
                      ps[:, 0:512], r=[kps], w=[("smb", "s1t", dc)],
                      tag=f"woe{q}_{mt}_{dc}")
            fin_l1(q, mt)

    NGRAN = NE * (H1 // 1024)
    def p34_granule_post(q, gi):
        for pidx in (2 * gi, 2 * gi + 1):
            if pidx < 32:
                p34_scan_piece(q, pidx // 4, pidx % 4)

    def fin_l1(q, mt):
        tti = q * (TQ // 128) + mt
        par = mt % 2
        gt = SM[:, PH + par * 1024:PH + par * 1024 + 1024]
        xt = SM[:, PH + 2048:PH + 3072]
        kg = ("sm", "p4g", par)
        kx = ("sm", "p4x")
        s1ap = SMBf[:, 0:1024]
        ks1 = [("smb", "s1t", dc) for dc in range(2)]
        racc = [("ar3", "acc", mt, dc) for dc in range(2)]
        act(gt, AR3[:, mt * 1024:(mt + 1) * 1024], AF.Silu,
            r=racc, w=[kg], tag=f"sl1{q}_{mt}")
        kb.dma("sp", "x1l",
               lambda e, tti=tti, xt=xt: e.dma_start(
                   xt, sc_x1.ap()[tti * 128:(tti + 1) * 128, :]),
               w=[kx], tag=f"x1l{tti}")
        tt(gt, s1ap, gt, ALU.mult, r=ks1 + [kg], w=[kg], tag=f"sg1{tti}")
        tt(xt, gt, xt, ALU.add, r=[kg, kx], w=[kx], tag=f"x2_{tti}")
        kb.dma("sp", "outo",
               lambda e, tti=tti, xt=xt: e.dma_start(
                   OUT.ap()[tti * 128:(tti + 1) * 128, :], xt),
               r=[kx], tag=f"outs{tti}")

    emit_moe(1, W1_1, B1_1, W2_1, B2_1, H1, None,
             AR1, "ar1", lambda hm, tc: (
                 AR2[:, HB0 + hm * TQ + tc * 512:
                     HB0 + hm * TQ + (tc + 1) * 512],
                 ("ar2", "hbuf", hm, tc)),
             xnT_l1, quarter_pre=p34_quarter_pre,
             granule_post=p34_granule_post, quarter_post=p34_quarter_post)

    guard = kb.finalize_and_emit()
    ctx2 = ctx
    return nc, (ctx2, guard)



_CACHE = {}


def _get_nc(T):
    if T not in _CACHE:
        _CACHE[T] = build_nc(T)
    return _CACHE[T]


def kernel(L0_attn_x, L0_attn_W_qkv, L0_ffn_gate_w, L0_ffn_gate_b,
           L0_ffn_rms_w, L0_ffn_w1, L0_ffn_b1, L0_ffn_w2, L0_ffn_b2,
           L1_attn_W_in, L1_attn_W_out, L1_attn_A_log, L1_attn_B_ssm,
           L1_attn_C_ssm, L1_attn_dt_log, L1_attn_D_ssm, L1_ffn_centroids,
           L1_ffn_W1, L1_ffn_b1, L1_ffn_W2, L1_ffn_b2):
    x = np.ascontiguousarray(np.asarray(L0_attn_x, dtype=np.float32))
    B, T, d = x.shape
    assert B == 8 and d == D
    nc, _guard = _get_nc(T)
    f32 = lambda a: np.ascontiguousarray(np.asarray(a, dtype=np.float32))
    shared = {
        "w_qkv": f32(L0_attn_W_qkv), "gate_w": f32(L0_ffn_gate_w),
        "gate_b": f32(L0_ffn_gate_b), "rms_w": f32(L0_ffn_rms_w),
        "l0_w1": f32(L0_ffn_w1), "l0_b1": f32(L0_ffn_b1),
        "l0_w2": f32(L0_ffn_w2), "l0_b2": f32(L0_ffn_b2),
        "w_in": f32(L1_attn_W_in), "w_out": f32(L1_attn_W_out),
        "a_log": f32(L1_attn_A_log), "b_ssm": f32(L1_attn_B_ssm),
        "c_ssm": f32(L1_attn_C_ssm), "dt_log": f32(L1_attn_dt_log),
        "d_ssm": f32(L1_attn_D_ssm), "centroids": f32(L1_ffn_centroids),
        "l1_w1": f32(L1_ffn_W1), "l1_b1": f32(L1_ffn_b1),
        "l1_w2": f32(L1_ffn_W2), "l1_b2": f32(L1_ffn_b2),
    }
    in_maps = [dict(shared, x=x[i]) for i in range(8)]
    res = run_bass_kernel_spmd(nc, in_maps, core_ids=list(range(8)))
    return np.stack([res.results[i]["out"] for i in range(8)], axis=0)


# version 15
# speedup vs baseline: 1.1061x; 1.1061x over previous
import numpy as np
import concourse.bass as bass
import concourse.mybir as mybir
from concourse.bass_utils import run_bass_kernel_spmd

F32 = mybir.dt.float32
BF16 = mybir.dt.bfloat16
AF = mybir.ActivationFunctionType
ALU = mybir.AluOpType
AX = mybir.AxisListType

D = 1024
NH = 16
HD = 64
NE = 8
H0 = 2048
H1 = 4096
NS = 16
LN_EPS = 1e-5
RMS_EPS = 1.1920929e-07


ENGINES = ("pe", "act", "dve", "pool", "sp")


class _Op:
    __slots__ = ("eng", "fn", "idx", "slot", "need_inc", "cnt", "waits", "tag")

    def __init__(self, eng, fn, idx, slot, tag):
        self.eng = eng
        self.fn = fn
        self.idx = idx
        self.slot = slot
        self.need_inc = False
        self.cnt = None
        self.waits = {}
        self.tag = tag


class KB:
    def __init__(self, nc):
        self.nc = nc
        self.streams = {e: [] for e in ENGINES}
        self.slots = {}
        self.slot_eng = {}
        self.last_w = {}
        self.readers = {}
        self.n_ops = 0

    def _stream_of(self, op):
        return op.slot if op.slot is not None else op.eng

    def _edge(self, consumer, producer):
        if producer is consumer:
            return
        if (producer.eng == "pe" and consumer.eng == "pe"
                and producer.slot is None and consumer.slot is None):
            return
        sk = self._stream_of(producer)
        if consumer.waits.get(sk, 0) < producer.idx:
            consumer.waits[sk] = producer.idx
        producer.need_inc = True

    def _record(self, op, r, w):
        for k in r:
            p = self.last_w.get(k)
            if p is not None:
                self._edge(op, p)
            self.readers.setdefault(k, {})[self._stream_of(op)] = op
        for k in w:
            p = self.last_w.get(k)
            if p is not None:
                self._edge(op, p)
            for rd in self.readers.get(k, {}).values():
                self._edge(op, rd)
            self.readers[k] = {}
            self.last_w[k] = op

    def op(self, eng, fn, r=(), w=(), tag=""):
        st = self.streams[eng]
        op = _Op(eng, fn, len(st) + 1, None, tag)
        self._record(op, r, w)
        st.append(op)
        self.n_ops += 1
        return op

    def dma(self, eng, slot, fn, r=(), w=(), tag=""):
        assert eng in ("sp", "act", "pool")
        sl = self.slots.setdefault(slot, [])
        if slot in self.slot_eng:
            assert self.slot_eng[slot] == eng, (slot, eng)
        else:
            self.slot_eng[slot] = eng
        op = _Op(eng, fn, len(sl) + 1, slot, tag)
        if sl:
            self._edge(op, sl[-1])
        self._record(op, r, w)
        sl.append(op)
        self.streams[eng].append(op)
        self.n_ops += 1
        return op

    def barrier(self):
        tails = []
        for st in self.streams.values():
            for o in reversed(st):
                if o.slot is None:
                    tails.append(o)
                    break
        for sl in self.slots.values():
            if sl:
                tails.append(sl[-1])
        for i, t in enumerate(tails):
            k = ("__bar__", i)
            self.last_w[k] = t
            self.readers[k] = {}
        self._bar_keys = [("__bar__", i) for i in range(len(tails))]
        self._bar_pending = dict.fromkeys(ENGINES, True)
        self._bar_tails = tails
        return self._bar_keys

    def _maybe_bar(self, op):
        if getattr(self, "_bar_pending", None) and self._bar_pending.get(op.eng):
            for t in self._bar_tails:
                self._edge(op, t)
            self._bar_pending[op.eng] = False

    def finalize_and_emit(self):
        nc = self.nc
        tails = []
        for st in self.streams.values():
            for o in reversed(st):
                if o.slot is None:
                    tails.append(o)
                    break
        for sl in self.slots.values():
            if sl:
                tails.append(sl[-1])
        fin = _Op("sp", lambda e: e.nop(), len(self.streams["sp"]) + 1, None,
                  "final")
        for t in tails:
            self._edge(fin, t)
        self.streams["sp"].append(fin)

        for e, st in self.streams.items():
            c = 0
            for op in st:
                if op.slot is not None:
                    continue
                if op.need_inc:
                    c += 1
                    op.cnt = c
        for slot, sl in self.slots.items():
            c = 0
            for op in sl:
                op.need_inc = True
                c += 16
                op.cnt = c

        nextcnt = {}
        for e, st in self.streams.items():
            arr = {}
            pend = []
            for op in st:
                if op.slot is not None:
                    continue
                pend.append(op.idx)
                if op.need_inc:
                    for i in pend:
                        arr[i] = op.cnt
                    pend = []
            for i in pend:
                arr[i] = None
            nextcnt[e] = arr
        for slot, sl in self.slots.items():
            nextcnt[slot] = {op.idx: op.cnt for op in sl}

        used = set()
        for st in self.streams.values():
            for op in st:
                for k in op.waits:
                    used.add(k)
                if op.need_inc:
                    used.add(self._stream_of(op))
        from contextlib import ExitStack
        self._ctx = ExitStack()
        sems = {}
        for i, k in enumerate(sorted(used, key=str)):
            sems[k] = self._ctx.enter_context(nc.semaphore(f"s{i}"))

        eng_attr = {"pe": "tensor", "act": "scalar", "dve": "vector",
                    "pool": "gpsimd", "sp": "sync"}
        with nc.Block() as blk:
            for e in ENGINES:
                st = self.streams[e]
                if not st:
                    continue

                def body(engine, _st=st):
                    hw = {}
                    for op in _st:
                        for skey, pidx in sorted(op.waits.items(),
                                                 key=lambda kv: str(kv[0])):
                            val = nextcnt[skey][pidx]
                            assert val is not None, (op.tag, skey, pidx)
                            if hw.get(skey, 0) < val:
                                engine.wait_ge(sems[skey], val)
                                hw[skey] = val
                        inst = op.fn(engine)
                        if op.need_inc:
                            sk = self._stream_of(op)
                            inst.then_inc(sems[sk], 16 if op.slot else 1)

                getattr(blk, eng_attr[e])(body)
        return self._ctx


_orig_op, _orig_dma = KB.op, KB.dma


def _op(self, eng, fn, r=(), w=(), tag=""):
    o = _orig_op(self, eng, fn, r, w, tag)
    self._maybe_bar(o)
    return o


def _dma(self, eng, slot, fn, r=(), w=(), tag=""):
    o = _orig_dma(self, eng, slot, fn, r, w, tag)
    self._maybe_bar(o)
    return o


KB.op, KB.dma = _op, _dma



def build_nc(T=4096, dev_scratch=False):
    assert T % 128 == 0 and T >= 512
    NT = T // 128
    TQ = min(1024, T)
    NQ = T // TQ
    QG = min(1024, T)
    NQG = T // QG
    NJT = T // 128
    NTC = T // 512

    nc = bass.Bass("TRN2", target_bir_lowering=False)

    ins = {}
    def din(name, shape):
        t = nc.dram_tensor(name, shape, F32, kind="ExternalInput")
        ins[name] = t
        return t

    X = din("x", [T, D])
    Wqkv = din("w_qkv", [3 * D, D])
    Gw = din("gate_w", [NE, D])
    Gb = din("gate_b", [NE])
    Rw = din("rms_w", [NE])
    W1_0 = din("l0_w1", [NE, D, H0])
    B1_0 = din("l0_b1", [NE, H0])
    W2_0 = din("l0_w2", [NE, H0, D])
    B2_0 = din("l0_b2", [NE, D])
    Win = din("w_in", [D, D])
    Wout = din("w_out", [D, D])
    Alog = din("a_log", [D, NS])
    Bssm = din("b_ssm", [D, NS])
    Cssm = din("c_ssm", [D, NS])
    DTlog = din("dt_log", [D])
    Dssm = din("d_ssm", [D])
    Cent = din("centroids", [NE, D])
    W1_1 = din("l1_w1", [NE, D, H1])
    B1_1 = din("l1_b1", [NE, H1])
    W2_1 = din("l1_w2", [NE, H1, D])
    B2_1 = din("l1_b2", [NE, D])
    OUT = nc.dram_tensor("out", [T, D], F32, kind="ExternalOutput")

    sck = "ExternalOutput" if dev_scratch else "Internal"
    sc_xn0t = nc.dram_tensor("sc_xn0t", [128, 8 * T], BF16, kind=sck)
    sc_w10 = nc.dram_tensor("sc_w10", [NE, D, H0], BF16, kind="Internal")
    sc_w20 = nc.dram_tensor("sc_w20", [NE, H0, D], BF16, kind="Internal")
    sc_w11 = nc.dram_tensor("sc_w11", [NE, D, H1], BF16, kind="Internal")
    sc_w21 = nc.dram_tensor("sc_w21", [NE, H1, D], BF16, kind="Internal")
    sc_qt = nc.dram_tensor("sc_qt", [D, T], BF16, kind=sck)
    sc_kt = nc.dram_tensor("sc_kt", [D, T], BF16, kind=sck)
    sc_vt = nc.dram_tensor("sc_vt", [D, T], BF16, kind=sck)
    sc_s0 = nc.dram_tensor("sc_s0", [T, D], F32, kind=sck)
    sc_x1 = nc.dram_tensor("sc_x1", [T, D], F32, kind=sck)
    sc_xn1 = nc.dram_tensor("sc_xn1", [T, D], BF16, kind=sck)
    sc_s1 = nc.dram_tensor("sc_s1", [T, D], F32, kind=sck)

    from contextlib import ExitStack
    ctx = ExitStack()
    AR1 = ctx.enter_context(nc.sbuf_tensor("AR1", [128, max(8 * T, 32768)], BF16))
    AR2 = ctx.enter_context(nc.sbuf_tensor("AR2", [128, 33280], BF16))
    AR3 = ctx.enter_context(nc.sbuf_tensor("AR3", [128, 8192], F32))
    AR4 = ctx.enter_context(nc.sbuf_tensor("AR4", [128, 8192], BF16))
    SM = ctx.enter_context(nc.sbuf_tensor("SM", [128, 5632], F32))
    SMB = ctx.enter_context(nc.sbuf_tensor("SMB", [128, 4096], BF16))
    psA0 = ctx.enter_context(nc.psum_tensor("psA0", [128, 512], F32))
    psA1 = ctx.enter_context(nc.psum_tensor("psA1", [128, 512], F32))
    psB = ctx.enter_context(nc.psum_tensor("psB", [128, 2048], F32))
    psT = ctx.enter_context(nc.psum_tensor("psT", [128, 1024], BF16))
    psF = ctx.enter_context(nc.psum_tensor("psF", [128, 512], F32))

    SMbf = SM.ap().bitcast(BF16)
    AR3bf = AR3.ap().bitcast(BF16)

    kb = KB(nc)

    IDF = 0
    IO2 = 128
    STATS = 256
    GT = 320
    BC = 448
    NORMG = 480
    GATES = 512
    IDB = 2 * 768
    ONESB = 2 * 832
    ONESF = 896
    OC8 = 1024
    GWTB = 2 * 1026
    CENTB = 2 * 1058
    PH = 1152

    KA0, KA1, KT, KF = (("ps", "A0"), ("ps", "A1"), ("ps", "T"), ("ps", "F"))
    KBs = [("ps", "B", i) for i in range(4)]

    def psb(i):
        return psB[:, i * 512:(i + 1) * 512]

    def mm(ps_ap, lhsT_ap, rhs_ap, start, stop, r, w, tag):
        kb.op("pe", lambda e: e.matmul(ps_ap, lhsT_ap, rhs_ap,
                                       start=start, stop=stop),
              r=r, w=w, tag=tag)

    def tr(ps_ap, in_ap, id_ap, r, w, tag):
        p = in_ap.shape[0]
        ids = id_ap[0:p, 0:p]
        kb.op("pe", lambda e: e.transpose(ps_ap, in_ap, ids),
              r=r, w=w, tag=tag)

    def act(out_ap, in_ap, func, r, w, tag, bias=0.0, scale=1.0, accum=None):
        kb.op("act", lambda e: e.activation(out_ap, in_ap, func, bias=bias,
                                            scale=scale, accum_out=accum),
              r=r, w=w, tag=tag)

    def acopy(out_ap, in_ap, r, w, tag):
        kb.op("act", lambda e: e.copy(out_ap, in_ap), r=r, w=w, tag=tag)

    def tt(out_ap, a_ap, b_ap, op, r, w, tag, eng="dve"):
        kb.op(eng, lambda e: e.tensor_tensor(out_ap, a_ap, b_ap, op),
              r=r, w=w, tag=tag)

    def ts(out_ap, in_ap, s1, s2, op0, op1, r, w, tag, eng="dve"):
        kb.op(eng, lambda e: e.tensor_scalar(out_ap, in_ap, s1, s2, op0, op1),
              r=r, w=w, tag=tag)

    def stt(out_ap, in0_ap, scal, in1_ap, op0, op1, r, w, tag, eng="dve"):
        kb.op(eng, lambda e: e.scalar_tensor_tensor(out_ap, in0_ap, scal,
                                                    in1_ap, op0, op1),
              r=r, w=w, tag=tag)

    def red(out_ap, in_ap, op, r, w, tag, negate=False):
        if op == "max":
            kb.op("dve", lambda e: e.reduce_max(out_ap, in_ap, AX.X,
                                                negate=negate),
                  r=r, w=w, tag=tag)
        else:
            kb.op("dve", lambda e: e.reduce_sum(out_ap, in_ap, AX.X),
                  r=r, w=w, tag=tag)

    def dcopy(out_ap, in_ap, r, w, tag, eng="dve"):
        kb.op(eng, lambda e: e.tensor_copy(out_ap, in_ap), r=r, w=w, tag=tag)

    def recip(out_ap, in_ap, r, w, tag):
        kb.op("dve", lambda e: e.reciprocal(out_ap, in_ap), r=r, w=w, tag=tag)

    def memset(ap, v, w, tag, eng="dve"):
        kb.op(eng, lambda e: e.memset(ap, v), w=w, tag=tag)

    kb.op("pool", lambda e: e.iota(SM[:, IDF:IDF + 128], [[1, 128]],
                                   channel_multiplier=0,
                                   allow_small_or_imprecise_dtypes=True),
          w=[("sm", "idf")], tag="iota1")
    kb.op("pool", lambda e: e.iota(SM[:, IO2:IO2 + 128], [[0, 128]],
                                   channel_multiplier=1,
                                   allow_small_or_imprecise_dtypes=True),
          w=[("sm", "io2")], tag="iota2")
    tt(SM[:, IDF:IDF + 128], SM[:, IDF:IDF + 128], SM[:, IO2:IO2 + 128],
       ALU.is_equal, r=[("sm", "idf"), ("sm", "io2")], w=[("sm", "idf")],
       tag="ideq")
    dcopy(SMbf[:, IDB:IDB + 128], SM[:, IDF:IDF + 128],
          r=[("sm", "idf")], w=[("sm", "idb")], tag="idb")
    memset(SMbf[0:1, ONESB:ONESB + 128], 1.0, w=[("sm", "onesb")], tag="onesb")

    memset(SM[0:1, ONESF:ONESF + 128], 1.0, w=[("sm", "onesf")], tag="onesf")
    memset(SM[0:8, OC8:OC8 + 1], 1.0, w=[("sm", "onescol8")], tag="ones8")

    IDF_AP = SM[:, IDF:IDF + 128]
    IDB_AP = SMbf[:, IDB:IDB + 128]

    kb.op("pool", lambda e: e.iota(SM[0:1, GT:GT + 8], [[1, 8]],
                                   channel_multiplier=0,
                                   allow_small_or_imprecise_dtypes=True),
          w=[("sm", "tierow")], tag="tiota")
    ts(SM[0:1, GT:GT + 8], SM[0:1, GT:GT + 8], -(2.0 ** -13),
       7 * 2.0 ** -13, ALU.mult, ALU.add,
       r=[("sm", "tierow")], w=[("sm", "tierow")], tag="tiesc")
    kb.dma("sp", "ld_small",
           lambda e: e.dma_start(SM[0:1, GT + 8:GT + 16],
                                 Gb.ap().unsqueeze(0)),
           w=[("sm", "gbrow")], tag="gbrow")
    kb.dma("sp", "ld_small",
           lambda e: e.dma_start(SM[0:1, GT + 16:GT + 24],
                                 Rw.ap().unsqueeze(0)),
           w=[("sm", "rwrow")], tag="rwrow")

    def bcast8(dst_c, src_c, rkey, wkey, tag):
        mm(psF[:, 0:8], SM[0:1, ONESF:ONESF + 128], SM[0:1, src_c:src_c + 8],
           True, True, r=[("sm", "onesf"), rkey], w=[KF], tag=tag + "mm")
        acopy(SM[:, dst_c:dst_c + 8], psF[:, 0:8], r=[KF], w=[wkey],
              tag=tag + "ev")

    bcast8(BC, GT + 8, ("sm", "gbrow"), ("sm", "gb_bc"), "gbbc")
    bcast8(BC + 8, GT + 16, ("sm", "rwrow"), ("sm", "rw_bc"), "rwbc")
    bcast8(BC + 16, GT, ("sm", "tierow"), ("sm", "tie_bc"), "tiebc")

    GWST = 2 * (PH + 3072)
    kb.dma("pool", "gwld",
           lambda e: e.dma_start(SMbf[0:8, GWST:GWST + 1024], Gw.ap()),
           w=[("sm", "gwst")], tag="gwld")
    for k in range(8):
        tr(psT[:, k * 8:k * 8 + 8], SMbf[0:8, GWST + k * 128:GWST + (k + 1) * 128],
           IDB_AP, r=[("sm", "gwst"), ("sm", "idb")], w=[KT], tag=f"gwtr{k}")
    acopy(SMbf[:, GWTB:GWTB + 64], psT[:, 0:64], r=[KT], w=[("sm", "gwT")],
          tag="gwev")

    def gwT_block(k):
        return SMbf[:, GWTB + k * 8:GWTB + k * 8 + 8]

    def emit_ln(x_ap, kx, xn_ap, kxn, sq_scr, ksq, tag, par=0):
        SB = STATS + par * 8
        s0, s1c = SM[:, SB:SB + 1], SM[:, SB + 1:SB + 2]
        s2, s3 = SM[:, SB + 2:SB + 3], SM[:, SB + 3:SB + 4]
        s4, s5 = SM[:, SB + 4:SB + 5], SM[:, SB + 5:SB + 6]
        s6, s7 = SM[:, SB + 6:SB + 7], SM[:, SB + 7:SB + 8]
        red(s0, x_ap, "sum", r=[kx], w=[("sm", "st0", par)], tag=tag + "sum")
        tt(sq_scr, x_ap, x_ap, ALU.mult, r=[kx], w=[ksq], tag=tag + "sq")
        red(s1c, sq_scr, "sum", r=[ksq], w=[("sm", "st1", par)], tag=tag + "sqs")
        ts(s2, s0, 1.0 / D, None, ALU.mult, ALU.bypass,
           r=[("sm", "st0", par)], w=[("sm", "st2", par)], tag=tag + "mu")
        ts(s3, s2, s2, None, ALU.mult, ALU.bypass,
           r=[("sm", "st2", par)], w=[("sm", "st3", par)], tag=tag + "mu2")
        ts(s4, s1c, 1.0 / D, None, ALU.mult, ALU.bypass,
           r=[("sm", "st1", par)], w=[("sm", "st4", par)], tag=tag + "ex2")
        stt(s5, s4, LN_EPS, s3, ALU.add, ALU.subtract,
            r=[("sm", "st4", par), ("sm", "st3", par)], w=[("sm", "st5", par)],
            tag=tag + "var")
        act(s6, s5, AF.Ln, r=[("sm", "st5", par)], w=[("sm", "st6", par)],
            tag=tag + "ln")
        act(s7, s6, AF.Exp, r=[("sm", "st6", par)], w=[("sm", "st7", par)],
            tag=tag + "ex", bias=0.0, scale=-0.5)
        ts(xn_ap, x_ap, s2, s7, ALU.subtract, ALU.mult,
           r=[kx, ("sm", "st2", par), ("sm", "st7", par)], w=[kxn], tag=tag + "xn")

    def emit_t8(xn_ap, kxn, AR, artag, tt_i, tag):
        for k in range(8):
            tr(psT[:, k * 128:(k + 1) * 128], xn_ap[:, k * 128:(k + 1) * 128],
               IDB_AP, r=[kxn, ("sm", "idb")], w=[KT], tag=f"{tag}t{k}")
        dst = bass.AP(AR[:, 0:1].tensor, tt_i * 128,
                      [[AR.shape[1], 128], [T, 8], [1, 128]])
        acopy(dst, psT.ap().rearrange("p (k c) -> p k c", k=8),
              r=[KT], w=[artag + (tt_i,)], tag=f"{tag}ev")

    def wqkvT_block(k, ob):
        return AR2[:, k * 3072 + ob * 128: k * 3072 + ob * 128 + 128]

    for ob in range(24):
        st = ob % 2
        stage = SMB[:, 2048 + st * 1024: 2048 + (st + 1) * 1024]
        kst = ("smb", "wqst", st)
        kb.dma("pool", f"wq{st}",
               lambda e, ob=ob, stage=stage: e.dma_start(
                   stage, Wqkv.ap()[ob * 128:(ob + 1) * 128, :]),
               w=[kst], tag=f"wqld{ob}")
        for k in range(8):
            tr(psT[:, k * 128:(k + 1) * 128], stage[:, k * 128:(k + 1) * 128],
               IDB_AP, r=[kst, ("sm", "idb")], w=[KT], tag=f"wqtr{ob}_{k}")
        dst = bass.AP(AR2[:, 0:1].tensor, ob * 128,
                      [[33280, 128], [3072, 8], [1, 128]])
        acopy(dst, psT.ap().rearrange("p (k c) -> p k c", k=8),
              r=[KT], w=[("ar2", "wqkvT", ob)], tag=f"wqev{ob}")

    for tti in range(NT):
        st = tti % 2
        xt = SM[:, PH + st * 1024: PH + (st + 1) * 1024]
        xnt = SMB[:, st * 1024:(st + 1) * 1024]
        kx, kxn = ("sm", "xt", st), ("smb", "xn", st)
        kb.dma("sp", f"x{st}",
               lambda e, tti=tti, xt=xt: e.dma_start(
                   xt, X.ap()[tti * 128:(tti + 1) * 128, :]),
               w=[kx], tag=f"xld{tti}")
        emit_ln(xt, kx, xnt, kxn, AR3[:, st * 1024:(st + 1) * 1024],
                ("sm", "lnsq", st), f"ln0_{tti}_", par=st)
        emit_t8(xnt, kxn, AR1, ("ar1", "xnT"), tti, f"x0_{tti}")

        psg = psA0 if st == 0 else psA1
        kpg = KA0 if st == 0 else KA1
        GB = GT + st * 56
        for k in range(8):
            mm(psg[:, 0:8],
               AR1[:, k * T + tti * 128: k * T + tti * 128 + 128],
               gwT_block(k), k == 0, k == 7,
               r=[("ar1", "xnT", tti), ("sm", "gwT")], w=[kpg],
               tag=f"gl{tti}_{k}")
        g0, g1 = SM[:, GB:GB + 8], SM[:, GB + 8:GB + 16]
        l34 = SM[:, GB + 16:GB + 24]
        e8 = SM[:, GB + 24:GB + 32]
        c = lambda i: SM[:, GB + 32 + i:GB + 33 + i]
        tt(g0, psg[:, 0:8], SM[:, BC:BC + 8], ALU.add,
           r=[kpg, ("sm", "gb_bc")], w=[("sm", "glg", st)],
           tag=f"gev{tti}")
        tt(g1, g0, g0, ALU.mult, r=[("sm", "glg", st)], w=[("sm", "gsq", st)],
           tag=f"gsq{tti}")
        red(c(0), g1, "sum", r=[("sm", "gsq", st)], w=[("sm", "gms", st)],
            tag=f"gms{tti}")
        ts(c(0), c(0), 1.0 / NE, RMS_EPS, ALU.mult, ALU.add,
           r=[("sm", "gms", st)], w=[("sm", "gms", st)], tag=f"gme{tti}")
        act(c(1), c(0), AF.Ln, r=[("sm", "gms", st)], w=[("sm", "gln", st)],
            tag=f"gln{tti}")
        act(c(2), c(1), AF.Exp, r=[("sm", "gln", st)], w=[("sm", "grs", st)],
            tag=f"grs{tti}", bias=0.0, scale=-0.5)
        ts(l34, g0, c(2), None, ALU.mult, ALU.bypass,
           r=[("sm", "glg", st), ("sm", "grs", st)], w=[("sm", "gl3", st)],
           tag=f"gl3{tti}")
        tt(l34, l34, SM[:, BC + 8:BC + 16], ALU.mult,
           r=[("sm", "gl3", st), ("sm", "rw_bc")], w=[("sm", "gl4", st)],
           tag=f"gl4{tti}")
        red(c(3), l34, "max", r=[("sm", "gl4", st)],
            w=[("sm", "gm", st)], tag=f"gm{tti}")
        ts(c(4), c(3), -2.0, None, ALU.mult, ALU.bypass,
           r=[("sm", "gm", st)], w=[("sm", "gmb", st)], tag=f"gmb{tti}")
        act(e8, l34, AF.Exp,
            r=[("sm", "gl4", st), ("sm", "gmb", st)], w=[("sm", "ge8", st)],
            tag=f"gex{tti}", bias=c(4), scale=2.0)
        red(c(5), e8, "sum", r=[("sm", "ge8", st)],
            w=[("sm", "gsm", st)], tag=f"gs{tti}")
        ts(c(6), c(5), 1e-6, 1.0, ALU.mult, ALU.add,
           r=[("sm", "gsm", st)], w=[("sm", "gt2", st)], tag=f"gt{tti}")
        recip(SM[:, NORMG + tti:NORMG + tti + 1], c(6),
              r=[("sm", "gt2", st)], w=[("sm", "normg", tti)],
              tag=f"gng{tti}")

    for ob in range(24):
        stg = ob % 2
        stage = AR4[:, stg * 4096: stg * 4096 + T]
        kstage = ("ar4", "qkvst", stg)
        for tc in range(NTC):
            ps, kps = psb(tc % 4), KBs[tc % 4]
            for k in range(8):
                mm(ps[:, 0:512], wqkvT_block(k, ob),
                   AR1[:, k * T + tc * 512: k * T + tc * 512 + 512],
                   k == 0, k == 7,
                   r=[("ar2", "wqkvT", ob)] + [("ar1", "xnT", t4)
                      for t4 in range(tc * 4, tc * 4 + 4)],
                   w=[kps], tag=f"qkv{ob}_{tc}_{k}")
            scale = 0.125 if ob < 8 else 1.0
            act(stage[:, tc * 512:(tc + 1) * 512], ps[:, 0:512], AF.Copy,
                r=[kps], w=[kstage], tag=f"qe{ob}_{tc}", scale=scale)
        dstt = sc_qt if ob < 8 else (sc_kt if ob < 16 else sc_vt)
        row = (ob % 8) * 128
        kb.dma("sp", f"qkvo{stg}",
               lambda e, dstt=dstt, row=row, stage=stage: e.dma_start(
                   dstt.ap()[row:row + 128, :], stage[:, 0:T]),
               r=[kstage], tag=f"qst{ob}")

    for (Wsrc, Wdst, nm) in ((W1_0, sc_w10, "w10"), (W2_0, sc_w20, "w20"),
                             (W1_1, sc_w11, "w11"), (W2_1, sc_w21, "w21")):
        sz = Wsrc.ap().shape[1] * Wsrc.ap().shape[2]
        for e_i in range(NE):
            kb.dma("pool", "wcast",
                   lambda e, Wsrc=Wsrc, Wdst=Wdst, e_i=e_i, sz=sz:
                   e.dma_start(
                       bass.AP(Wdst, e_i * sz, [[sz, 1], [1, sz]]),
                       bass.AP(Wsrc, e_i * sz, [[sz, 1], [1, sz]])),
                   r=[("ar1", "xnT", NT - 1)],
                   w=[("hbm", nm, e_i)], tag=f"wc{nm}_{e_i}")

    kb.dma("sp", "xn0sp",
           lambda e: e.dma_start(sc_xn0t.ap(), AR1[:, 0:8 * T]),
           r=[("ar1", "xnT", t) for t in range(NT)], tag="xn0out")

    kb.barrier()

    def vaug_block(h, jt):
        c = (jt * NH + h) * 65
        return AR2[:, c:c + 65]

    memset(bass.AP(AR2[:, 0:1].tensor, 64,
                   [[33280, 128], [65, NH * NJT], [1, 1]]), 1.0,
           w=[("ar2", "vones")], tag="vones")

    for h in range(NH):
        stg = h % 2
        vstage = AR4[:, stg * 4096: stg * 4096 + T]
        kvst = ("ar4", "vst", stg)
        kb.dma("sp", f"vld{stg}",
               lambda e, h=h, vstage=vstage: e.dma_start(
                   vstage[0:64, :], sc_vt.ap()[h * 64:(h + 1) * 64, :]),
               w=[kvst], tag=f"vld{h}")
        for j0 in range(0, NJT, 8):
            nspan = min(8, NJT - j0)
            for jj in range(nspan):
                jt = j0 + jj
                tr(psT[:, jj * 64:jj * 64 + 64],
                   vstage[0:64, jt * 128:(jt + 1) * 128], IDB_AP,
                   r=[kvst, ("sm", "idb")], w=[KT], tag=f"vt{h}_{jt}")
            dst = bass.AP(AR2[:, 0:1].tensor, (j0 * NH + h) * 65,
                          [[33280, 128], [65 * NH, nspan], [1, 64]])
            src = bass.AP(psT[:, 0:1].tensor, 0,
                          [[1024, 128], [64, nspan], [1, 64]])
            acopy(dst, src, r=[KT], w=[("ar2", "va", h, j0 // 8)],
                  tag=f"vev{h}_{j0}")

    kb.barrier()
    memset(SMB[64:65, 0:T], 1.0, w=[("smb", "kones")], tag="kones0")
    memset(AR4[64:65, 0:T], 1.0, w=[("ar4", "kones")], tag="kones1")
    NQT = QG // 128

    def kaug_buf(par):
        return SMB if par == 0 else AR4

    def kkey(par):
        return ("smb", "kaug") if par == 0 else ("ar4", "kaug")

    QA = [8192 - 2 * QG, 8192 - QG]
    pend_fin = [None]

    for g in range(NQG):
        for h in range(NH):
            par = h % 2
            KB_ = kaug_buf(par)
            QA0 = QA[par]
            kq = ("ar4", "qaug", par)
            kqm = ("ar4", "qaugm", par)
            kb.dma("sp", "kaug" + str(par),
                   lambda e, h=h, KB_=KB_: e.dma_start(
                       KB_[0:64, 0:T], sc_kt.ap()[h * 64:(h + 1) * 64, :]),
                   w=[kkey(par)], tag=f"kld{g}_{h}")
            kb.dma("sp", "qaug" + str(par),
                   lambda e, g=g, h=h, QA0=QA0: e.dma_start(
                       AR4[0:64, QA0:QA0 + QG],
                       sc_qt.ap()[h * 64:(h + 1) * 64, g * QG:(g + 1) * QG]),
                   w=[kq], tag=f"qld{g}_{h}")
            for q0 in range(0, NQT, 4):
                nb = min(4, NQT - q0)
                for i in range(nb):
                    qt = q0 + i
                    rhs = bass.AP(KB_[:, 0:1].tensor, 0,
                                  [[KB_.shape[1], 64], [NJT, 128]])
                    mm(psF[:, i * 128:(i + 1) * 128],
                       AR4[0:64, QA0 + qt * 128: QA0 + qt * 128 + 128], rhs,
                       True, True, r=[kq, kkey(par)],
                       w=[KF], tag=f"pa{g}_{h}_{qt}")
                for i in range(nb):
                    qt = q0 + i
                    red(SMbf[:, 2 * (PH + 4) + qt:2 * (PH + 4) + qt + 1],
                        psF[:, i * 128:(i + 1) * 128], "max",
                        r=[KF], w=[("sm", "negm", qt)],
                        tag=f"pam{g}_{h}_{qt}", negate=True)
            for qt in range(NQT):
                tr(psT[0:1, qt * 128:(qt + 1) * 128],
                   SMbf[:, 2 * (PH + 4) + qt:2 * (PH + 4) + qt + 1], IDB_AP,
                   r=[("sm", "negm", qt), ("sm", "idb")], w=[KT],
                   tag=f"pat{g}_{h}_{qt}")
            acopy(AR4[64:65, QA0:QA0 + QG], psT[0:1, 0:QG],
                  r=[KT], w=[kqm], tag=f"pam2{g}_{h}")
            if pend_fin[0] is not None:
                pend_fin[0]()
                pend_fin[0] = None
            NQC = QG // 512

            def bmm(jt):
                half = (jt % 2) * QG
                for qc in range(NQC):
                    ii = (half + qc * 512) // 512
                    mm(psB[:, half + qc * 512: half + (qc + 1) * 512],
                       KB_[0:65, jt * 128:(jt + 1) * 128],
                       AR4[0:65, QA0 + qc * 512: QA0 + qc * 512 + 512],
                       True, True,
                       r=[kkey(par), ("smb" if par == 0 else "ar4", "kones"),
                          kq, kqm],
                       w=[KBs[ii]], tag=f"pb{g}_{h}_{jt}_{qc}")

            def bexp(jt):
                half = (jt % 2) * QG
                act(AR1[:, jt * QG:(jt + 1) * QG],
                    psB[:, half:half + QG], AF.Exp,
                    r=[KBs[(half + qc * 512) // 512] for qc in range(NQC)],
                    w=[("ar1", "sT", jt)],
                    tag=f"pbe{g}_{h}_{jt}")

            def avmm(jt):
                for qc in range(NQC):
                    ps, kps = (psA0, KA0) if qc == 0 else (psA1, KA1)
                    mm(ps[0:65, 0:512], vaug_block(h, jt),
                       AR1[:, jt * QG + qc * 512: jt * QG + qc * 512 + 512],
                       jt == 0, jt == NJT - 1,
                       r=[("ar2", "va", h, jt // 8), ("ar2", "vones"),
                          ("ar1", "sT", jt)],
                       w=[kps], tag=f"av{g}_{h}_{qc}_{jt}")

            bmm(0)
            bexp(0)
            for jt in range(1, NJT):
                bmm(jt)
                bexp(jt)
                avmm(jt - 1)
            avmm(NJT - 1)
            def fin_head(g=g, h=h):
                for qc in range(NQC):
                    ps, kps = (psA0, KA0) if qc == 0 else (psA1, KA1)
                    acopy(SM[0:65, PH + 16 + qc * 512: PH + 16 + qc * 512 + 512],
                          ps[0:65, 0:512], r=[kps], w=[("sm", "oh", qc)],
                          tag=f"ave{g}_{h}_{qc}")
                for qt in range(NQT):
                    psf, kpsf = psb(qt % 4), KBs[qt % 4]
                    tr(psf[:, 0:65],
                       SM[0:65, PH + 16 + qt * 128: PH + 16 + qt * 128 + 128],
                       IDF_AP, r=[("sm", "oh", qt * 128 // 512), ("sm", "idf")],
                       w=[kpsf], tag=f"ot{g}_{h}_{qt}")
                    recip(SM[:, PH + 2 + qt % 2:PH + 3 + qt % 2], psf[:, 64:65],
                          r=[kpsf], w=[("sm", "linv", qt % 2)],
                          tag=f"ol{g}_{h}_{qt}")
                    ts(AR3[:, qt * 1024 + h * 64: qt * 1024 + h * 64 + 64],
                       psf[:, 0:64], SM[:, PH + 2 + qt % 2:PH + 3 + qt % 2],
                       None, ALU.mult,
                       ALU.bypass, r=[kpsf, ("sm", "linv", qt % 2)],
                       w=[("ar3", "sg", qt, h)], tag=f"os{g}_{h}_{qt}")
            pend_fin[0] = fin_head
        if pend_fin[0] is not None:
            pend_fin[0]()
            pend_fin[0] = None
        s0dst = bass.AP(sc_s0, g * QG * 1024,
                        [[1024, 128], [128 * 1024, NQT], [1, 1024]])
        kb.dma("sp", "s0o",
               lambda e, s0dst=s0dst: e.dma_start(s0dst, AR3[:, 0:NQT * 1024]),
               r=[("ar3", "sg", qt, h) for qt in range(NQT)
                  for h in range(NH)],
               tag=f"s0st{g}")

    kb.barrier()

    def emit_moe(layer, W1d, B1d, W2d, B2d, H, finalize_tile,
                 gr_a, gr_tag, hb, xnT_blk,
                 quarter_pre=None, granule_post=None, quarter_post=None):
        NHQ = H // 1024
        B1ST = PH + 3072
        B1C = PH + 3200
        for e_i in range(NE):
            nrow = H // 128
            kb.dma("sp", "b1ld",
                   lambda e, e_i=e_i, nrow=nrow: e.dma_start(
                       SM[0:nrow, B1ST:B1ST + 128],
                       B1d.ap()[e_i:e_i + 1, :].rearrange(
                           "one (a b) -> (one a) b", b=128)),
                   w=[("sm", "stage1")], tag=f"b1l{layer}_{e_i}")
            tr(psF[:, 0:nrow], SM[0:nrow, B1ST:B1ST + 128], IDF_AP,
               r=[("sm", "stage1"), ("sm", "idf")], w=[KF],
               tag=f"b1t{layer}_{e_i}")
            acopy(SM[:, B1C + e_i * nrow: B1C + (e_i + 1) * nrow],
                  psF[:, 0:nrow], r=[KF], w=[("sm", "b1c", e_i)],
                  tag=f"b1e{layer}_{e_i}")
        B2B = PH + 3456
        if layer == 0:
            kb.dma("sp", "b2ld",
                   lambda e: e.dma_start(SM[0:8, B2B:B2B + 1024], B2d.ap()),
                   w=[("sm", "b2f")], tag="b2ld0")
            for dc in range(2):
                mm(psF[0:1, 0:512], SM[0:8, OC8:OC8 + 1],
                   SM[0:8, B2B + dc * 512:B2B + dc * 512 + 512], True, True,
                   r=[("sm", "onescol8"), ("sm", "b2f")], w=[KF],
                   tag=f"b2s{dc}")
                acopy(SM[0:1, PH + dc * 512:PH + dc * 512 + 512],
                      psF[0:1, 0:512], r=[KF], w=[("sm", "p2g")],
                      tag=f"b2se{dc}")
            for dc in range(2):
                mm(psF[:, 0:512], SM[0:1, ONESF:ONESF + 128],
                   SM[0:1, PH + dc * 512:PH + dc * 512 + 512],
                   True, True,
                   r=[("sm", "onesf"), ("sm", "p2g")], w=[KF],
                   tag=f"b2bc{dc}")
                acopy(SM[:, B2B + dc * 512:B2B + dc * 512 + 512],
                      psF[:, 0:512], r=[KF],
                      w=[("sm", "b2bc"), ("sm", "b2f")],
                      tag=f"b2bce{dc}")
        else:
            kb.dma("pool", "b2ld1",
                   lambda e: e.dma_start(SMB[0:8, 2176:2176 + 1024], B2d.ap()),
                   w=[("smb", "b2sb")], tag="b2ld1")

        for q in range(NQ):
            if quarter_pre is not None:
                quarter_pre(q)
            if layer == 1:
                for mt in range(TQ // 128):
                    tti2 = q * (TQ // 128) + mt
                    tr(psF[0:8, 0:128],
                       SM[:, GATES + tti2 * 8:GATES + tti2 * 8 + 8],
                       IDF_AP, r=[("sm", "gates", tti2), ("sm", "idf")],
                       w=[KF], tag=f"gTt{tti2}")
                    acopy(SMB[0:8, 2048:2048 + 128], psF[0:8, 0:128],
                          r=[KF], w=[("smb", "gTm")], tag=f"gTe{tti2}")
                    for dc in range(2):
                        mm(psF[:, 0:512],
                           SMB[0:8, 2048:2048 + 128],
                           SMB[0:8, 2176 + dc * 512:2176 + dc * 512 + 512],
                           True, True,
                           r=[("smb", "gTm"), ("smb", "b2sb")],
                           w=[KF], tag=f"b2m{q}_{mt}_{dc}")
                        acopy(AR3[:, mt * 1024 + dc * 512:
                                  mt * 1024 + dc * 512 + 512],
                              psF[:, 0:512], r=[KF],
                              w=[("ar3", "acc", mt, dc)],
                              tag=f"b2me{q}_{mt}_{dc}")
            gi = 0
            for e_i in range(NE):
                for hq in range(NHQ):
                    sl = (q * NE * NHQ + e_i * NHQ + hq) % 2
                    W1g = gr_a[:, sl * 16384: sl * 16384 + 8192]
                    W2g = gr_a[:, sl * 16384 + 8192: sl * 16384 + 16384]
                    k1, k2 = (gr_tag, "w1g", sl), (gr_tag, "w2g", sl)
                    W1b, W2b = (sc_w10, sc_w20) if layer == 0 else \
                        (sc_w11, sc_w21)
                    nm1, nm2 = ("w10", "w20") if layer == 0 else \
                        ("w11", "w21")
                    w1src = bass.AP(W1b, e_i * D * H + hq * 1024,
                                    [[H, 128], [128 * H, 8], [1, 1024]])
                    w2src = bass.AP(W2b, e_i * H * D + hq * 1024 * D,
                                    [[D, 128], [128 * D, 8], [1, 1024]])
                    kb.dma("sp", f"w1g{sl}",
                           lambda e, w1src=w1src, W1g=W1g: e.dma_start(
                               W1g, w1src),
                           r=[("hbm", nm1, e_i)],
                           w=[k1], tag=f"w1l{layer}_{q}_{e_i}_{hq}")
                    kb.dma("sp", f"w2g{sl}",
                           lambda e, w2src=w2src, W2g=W2g: e.dma_start(
                               W2g, w2src),
                           r=[("hbm", nm2, e_i)],
                           w=[k2], tag=f"w2l{layer}_{q}_{e_i}_{hq}")
                    nrow = H // 128
                    for hm in range(8):
                        for tc in range(TQ // 512):
                            ii = (hm * (TQ // 512) + tc) % 4
                            ps, kps = psb(ii), KBs[ii]
                            for k in range(8):
                                xap, xkeys = xnT_blk(k, q, tc)
                                mm(ps[:, 0:512],
                                   W1g[:, k * 1024 + hm * 128:
                                       k * 1024 + hm * 128 + 128],
                                   xap, k == 0, k == 7,
                                   r=[k1] + xkeys,
                                   w=[kps],
                                   tag=f"m1_{layer}_{q}_{e_i}_{hq}_{hm}_{tc}_{k}")
                            bcol = SM[:, B1C + e_i * nrow + hq * 8 + hm:
                                      B1C + e_i * nrow + hq * 8 + hm + 1]
                            hap, hkey = hb(hm, tc)
                            act(hap, ps[:, 0:512], AF.Gelu,
                                r=[kps, ("sm", "b1c", e_i)],
                                w=[hkey],
                                tag=f"ge{layer}_{q}_{e_i}_{hq}_{hm}_{tc}",
                                bias=bcol)
                    for mt in range(TQ // 128):
                        for dc in range(2):
                            ps, kps = ((psA0, KA0) if (mt * 2 + dc) % 2 == 0
                                       else (psA1, KA1))
                            for hk in range(8):
                                hap, hkey = hb(hk, (mt * 128) // 512)
                                mm(ps[:, 0:512],
                                   hap[:, mt * 128 - ((mt * 128) // 512) * 512:
                                       mt * 128 - ((mt * 128) // 512) * 512 + 128],
                                   W2g[:, hk * 1024 + dc * 512:
                                       hk * 1024 + dc * 512 + 512],
                                   hk == 0, hk == 7,
                                   r=[k2, hkey],
                                   w=[kps],
                                   tag=f"m2_{layer}_{q}_{e_i}_{hq}_{mt}_{dc}_{hk}")
                            accp = AR3[:, mt * 1024 + dc * 512:
                                       mt * 1024 + dc * 512 + 512]
                            kacc = ("ar3", "acc", mt, dc)
                            if layer == 0:
                                if e_i == 0 and hq == 0:
                                    dcopy(accp, ps[:, 0:512], r=[kps],
                                          w=[kacc],
                                          tag=f"ac0_{layer}_{q}_{e_i}_{hq}_{mt}_{dc}")
                                else:
                                    tt(accp, ps[:, 0:512], accp, ALU.add,
                                       r=[kps, kacc], w=[kacc],
                                       tag=f"ac_{layer}_{q}_{e_i}_{hq}_{mt}_{dc}")
                            else:
                                gcol = SM[:, GATES + (q * (TQ // 128) + mt) * 8
                                          + e_i: GATES +
                                          (q * (TQ // 128) + mt) * 8 + e_i + 1]
                                stt(accp, ps[:, 0:512], gcol, accp,
                                    ALU.mult, ALU.add,
                                    r=[kps, kacc, ("sm", "gates",
                                                   q * (TQ // 128) + mt)],
                                    w=[kacc],
                                    tag=f"ac_{layer}_{q}_{e_i}_{hq}_{mt}_{dc}")
                    if granule_post is not None:
                        granule_post(q, gi)
                    gi += 1
            if quarter_post is not None:
                quarter_post(q)
            for mt in range(TQ // 128):
                if finalize_tile is not None:
                    finalize_tile(q, mt)

    kb.dma("sp", "xn0ld",
           lambda e: e.dma_start(AR1[:, 0:8 * T], sc_xn0t.ap()),
           w=[("ar1", "xnT", t) for t in range(NT)], tag="xn0in")

    def fin_l0(q, mt):
        tti = q * (TQ // 128) + mt
        gt = SM[:, PH:PH + 1024]
        xt = SM[:, PH + 1024:PH + 2048]
        stl = SM[:, PH + 2048:PH + 3072]
        kg, kx, ks = ("sm", "p2g"), ("sm", "p2x"), ("sm", "p2s")
        ngcol = SM[:, NORMG + tti:NORMG + tti + 1]
        racc = [("ar3", "acc", mt, dc) for dc in range(2)]
        tt(stl, AR3[:, mt * 1024:(mt + 1) * 1024],
           SM[:, PH + 3456:PH + 3456 + 1024], ALU.add,
           r=racc + [("sm", "b2bc")], w=[ks], tag=f"ab2{q}_{mt}")
        act(gt, stl, AF.Silu,
            r=[ks, ("sm", "normg", tti)], w=[kg], tag=f"sil{q}_{mt}",
            scale=ngcol)
        kb.dma("sp", "xld2",
               lambda e, tti=tti, xt=xt: e.dma_start(
                   xt, X.ap()[tti * 128:(tti + 1) * 128, :]),
               w=[kx], tag=f"x2l{tti}")
        kb.dma("sp", "sld",
               lambda e, tti=tti, stl=stl: e.dma_start(
                   stl, sc_s0.ap()[tti * 128:(tti + 1) * 128, :]),
               r=[kg], w=[ks], tag=f"s0l{tti}")
        tt(gt, stl, gt, ALU.mult, r=[ks, kg], w=[kg], tag=f"sg{tti}")
        tt(xt, gt, xt, ALU.add, r=[kg, kx], w=[kx], tag=f"x1_{tti}")
        kb.dma("sp", "x1o",
               lambda e, tti=tti, xt=xt: e.dma_start(
                   sc_x1.ap()[tti * 128:(tti + 1) * 128, :], xt),
               r=[kx], tag=f"x1s{tti}")
        xnt = SMB[:, (tti % 2) * 1024:(tti % 2) * 1024 + 1024]
        kxn = ("smb", "xn1", tti % 2)
        emit_ln(xt, kx, xnt, kxn, gt, kg, f"ln1_{tti}_", par=tti % 2)
        kb.dma("sp", "xn1o",
               lambda e, tti=tti, xnt=xnt: e.dma_start(
                   sc_xn1.ap()[tti * 128:(tti + 1) * 128, :], xnt),
               r=[kxn], tag=f"xn1s{tti}")

    def xnT_l0(k, q, tc):
        tok = q * TQ + tc * 512
        return (AR1[:, k * T + tok: k * T + tok + 512],
                [("ar1", "xnT", t4) for t4 in
                 range(tok // 128, tok // 128 + 4)])

    emit_moe(0, W1_0, B1_0, W2_0, B2_0, H0, fin_l0,
             AR2, "ar2", lambda hm, tc: (
                 AR4[:, hm * TQ + tc * 512: hm * TQ + (tc + 1) * 512],
                 ("ar4", "hbuf", hm, tc)),
             xnT_l0)
    kb.barrier()

    AR4f = AR4.ap().bitcast(F32)
    SMBf = SMB.ap().bitcast(F32)
    XIN0 = 8192
    HB0 = 16384
    WSL = 24576
    sc_winT = nc.dram_tensor("sc_winT", [128, 8192], BF16, kind="Internal")
    sc_woT = nc.dram_tensor("sc_woT", [128, 8192], BF16, kind="Internal")

    SSP = PH + 3456
    DTC = SSP + 384
    DSC = SSP + 392
    EAC = SSP + 400
    ST0 = SSP + 448
    kb.dma("sp", "dtld",
           lambda e: e.dma_start(SM[0:8, PH + 3072:PH + 3072 + 128],
                                 DTlog.ap().rearrange("(a b) -> a b", b=128)),
           w=[("sm", "stage1")], tag="dtld")
    tr(psF[:, 0:8], SM[0:8, PH + 3072:PH + 3072 + 128], IDF_AP,
       r=[("sm", "stage1"), ("sm", "idf")], w=[KF], tag="dttr")
    act(SM[:, DTC:DTC + 8], psF[:, 0:8], AF.Exp,
        r=[KF], w=[("sm", "dtc")], tag="dtex")
    kb.dma("sp", "dtld",
           lambda e: e.dma_start(SM[0:8, PH + 3072:PH + 3072 + 128],
                                 Dssm.ap().rearrange("(a b) -> a b", b=128)),
           r=[KF], w=[("sm", "stage1")], tag="dsld")
    tr(psF[:, 8:16], SM[0:8, PH + 3072:PH + 3072 + 128], IDF_AP,
       r=[("sm", "stage1"), ("sm", "idf")], w=[KF], tag="dstr")
    acopy(SM[:, DSC:DSC + 8], psF[:, 8:16],
          r=[KF], w=[("sm", "dsc")], tag="dsev")

    for ct in range(8):
        base = SSP + ct * 48
        kb.dma("sp", "assld",
               lambda e, ct=ct: e.dma_start(
                   SM[:, EAC + 32:EAC + 48],
                   Alog.ap()[ct * 128:(ct + 1) * 128, :]),
               w=[("sm", "alst")], tag=f"ald{ct}")
        kb.dma("sp", "assld",
               lambda e, ct=ct, base=base: e.dma_start(
                   SM[:, base + 16:base + 32],
                   Bssm.ap()[ct * 128:(ct + 1) * 128, :]),
               w=[("sm", "bssm", ct)], tag=f"bld{ct}")
        kb.dma("sp", "assld",
               lambda e, ct=ct, base=base: e.dma_start(
                   SM[:, base + 32:base + 48],
                   Cssm.ap()[ct * 128:(ct + 1) * 128, :]),
               w=[("sm", "cssm", ct)], tag=f"cld{ct}")
        eA = SM[:, EAC:EAC + 16]
        t1 = SM[:, EAC + 16:EAC + 32]
        act(eA, SM[:, EAC + 32:EAC + 48], AF.Exp,
            r=[("sm", "alst")], w=[("sm", "eA")], tag=f"eA{ct}")
        ts(t1, eA, SM[:, DTC + ct:DTC + ct + 1], -1.0, ALU.mult,
           ALU.mult, r=[("sm", "eA"), ("sm", "dtc")], w=[("sm", "t1")],
           tag=f"dtA{ct}")
        act(SM[:, base:base + 16], t1, AF.Exp,
            r=[("sm", "t1")], w=[("sm", "abar", ct)], tag=f"ab{ct}")
        ts(t1, SM[:, base:base + 16], 1.0, -1.0, ALU.subtract, ALU.mult,
           r=[("sm", "abar", ct)], w=[("sm", "t1")], tag=f"bb1{ct}")
        recip(SM[:, EAC + 32:EAC + 48], eA, r=[("sm", "eA"), ("sm", "alst")],
              w=[("sm", "alst")], tag=f"bb2{ct}")
        tt(t1, t1, SM[:, EAC + 32:EAC + 48], ALU.mult,
           r=[("sm", "t1"), ("sm", "alst")], w=[("sm", "t1")], tag=f"bb3{ct}")
        tt(SM[:, base + 16:base + 32], t1, SM[:, base + 16:base + 32],
           ALU.mult, r=[("sm", "t1"), ("sm", "bssm", ct)],
           w=[("sm", "bbar", ct)], tag=f"bb4{ct}")
    memset(SM[:, ST0:ST0 + 128], 0.0,
           w=[("sm", "sst", ct, n) for ct in range(8) for n in range(NS)],
           tag="st0")

    kb.dma("pool", "centld",
           lambda e: e.dma_start(AR2[0:8, WSL:WSL + 1024], Cent.ap()),
           w=[("ar2", "wslot")], tag="centld")
    for k in range(8):
        tr(psT[:, k * 8:k * 8 + 8],
           AR2[0:8, WSL + k * 128:WSL + (k + 1) * 128],
           IDB_AP, r=[("ar2", "wslot"), ("sm", "idb")], w=[KT],
           tag=f"ctr{k}")
    acopy(SMbf[:, CENTB:CENTB + 64], psT[:, 0:64], r=[KT],
          w=[("sm", "centT")], tag="cev")

    def build_wT(Wd, sc_dst, tag):
        for rb in range(8):
            kb.dma("pool", "wiold",
                   lambda e, rb=rb: e.dma_start(
                       AR2[:, HB0:HB0 + 1024],
                       Wd.ap()[rb * 128:(rb + 1) * 128, :]),
                   w=[("ar2", "hbuf", 0, 0), ("ar2", "hbuf", 0, 1)],
                   tag=f"{tag}l{rb}")
            for k in range(8):
                tr(psT[:, k * 128:(k + 1) * 128],
                   AR2[:, HB0 + k * 128:HB0 + (k + 1) * 128], IDB_AP,
                   r=[("ar2", "hbuf", 0, 0), ("ar2", "hbuf", 0, 1),
                      ("sm", "idb")], w=[KT],
                   tag=f"{tag}t{rb}_{k}")
            dst = bass.AP(AR2[:, 0:1].tensor, WSL + rb * 128,
                          [[33280, 128], [1024, 8], [1, 128]])
            acopy(dst, psT.ap().rearrange("p (k c) -> p k c", k=8),
                  r=[KT], w=[("ar2", "wslot")], tag=f"{tag}e{rb}")
        kb.dma("sp", "wTout",
               lambda e: e.dma_start(sc_dst.ap(), AR2[:, WSL:WSL + 8192]),
               r=[("ar2", "wslot")], w=[("hbm", tag)], tag=f"{tag}sp")

    build_wT(Wout, sc_woT, "wo")
    build_wT(Win, sc_winT, "wi")

    def xnT_l1(k, q, tc):
        return (AR2[:, k * TQ + tc * 512: k * TQ + tc * 512 + 512],
                [("ar2", "xn1Tq")])

    def p34_quarter_pre(q):
        if q > 0:
            kb.dma("sp", "wTin",
                   lambda e: e.dma_start(AR2[:, WSL:WSL + 8192],
                                         sc_winT.ap()),
                   r=[("hbm", "wi")], w=[("ar2", "wslot")], tag=f"wTin{q}")
        for mt in range(TQ // 128):
            tti = q * (TQ // 128) + mt
            par = mt % 2
            stg = AR4[:, par * 2048:par * 2048 + 1024]
            kb.dma("sp", f"xn1l{par}",
                   lambda e, tti=tti, stg=stg: e.dma_start(
                       stg, sc_xn1.ap()[tti * 128:(tti + 1) * 128, :]),
                   w=[("sm", "bx", par)], tag=f"x1ld{tti}")
            for k in range(8):
                tr(psT[:, k * 128:(k + 1) * 128],
                   stg[:, k * 128:(k + 1) * 128], IDB_AP,
                   r=[("sm", "bx", par), ("sm", "idb")], w=[KT],
                   tag=f"x1t{tti}_{k}")
            dst = bass.AP(AR2[:, 0:1].tensor, mt * 128,
                          [[33280, 128], [TQ, 8], [1, 128]])
            acopy(dst, psT.ap().rearrange("p (k c) -> p k c", k=8),
                  r=[KT], w=[("ar2", "xn1Tq")], tag=f"x1e{tti}")
        for mt in range(TQ // 128):
            tti = q * (TQ // 128) + mt
            for k in range(8):
                mm(psF[:, 8:16],
                   AR2[:, k * TQ + mt * 128: k * TQ + mt * 128 + 128],
                   SMbf[:, CENTB + k * 8:CENTB + k * 8 + 8], k == 0, k == 7,
                   r=[("ar2", "xn1Tq"), ("sm", "centT")], w=[KF],
                   tag=f"cl{tti}_{k}")
            gg = lambda i: SM[:, GT + i * 8:GT + (i + 1) * 8]
            cc = lambda i: SM[:, GT + 96 + i:GT + 97 + i]
            act(gg(0), psF[:, 8:16], AF.Sigmoid, r=[KF],
                w=[("sm", "cg")], tag=f"csg{tti}")
            ts(gg(1), psF[:, 8:16], 16.635532, None, ALU.min, ALU.bypass,
               r=[KF], w=[("sm", "crc")], tag=f"crc{tti}")
            tt(gg(1), gg(1), SM[:, BC + 16:BC + 24], ALU.add,
               r=[("sm", "crc"), ("sm", "tie_bc")], w=[("sm", "cr")],
               tag=f"cr{tti}")
            red(cc(0), gg(1), "max", r=[("sm", "cr")], w=[("sm", "cm1")],
                tag=f"cm1{tti}")
            ts(gg(2), gg(1), cc(0), None, ALU.is_equal, ALU.bypass,
               r=[("sm", "cr"), ("sm", "cm1")], w=[("sm", "ceq")],
               tag=f"ceq{tti}")
            stt(gg(3), gg(2), -1e30, gg(1), ALU.mult, ALU.add,
                r=[("sm", "ceq"), ("sm", "cr")], w=[("sm", "cr2")],
                tag=f"cr2{tti}")
            red(cc(1), gg(3), "max", r=[("sm", "cr2")], w=[("sm", "cm2")],
                tag=f"cm2{tti}")
            ts(gg(4), gg(1), cc(1), None, ALU.is_ge, ALU.bypass,
               r=[("sm", "cr"), ("sm", "cm2")], w=[("sm", "csel")],
               tag=f"csel{tti}")
            tt(gg(5), gg(0), gg(4), ALU.mult,
               r=[("sm", "cg"), ("sm", "csel")], w=[("sm", "cmsk")],
               tag=f"cmsk{tti}")
            red(cc(2), gg(5), "sum", r=[("sm", "cmsk")], w=[("sm", "csum")],
                tag=f"csum{tti}")
            recip(cc(3), cc(2), r=[("sm", "csum")], w=[("sm", "crs")],
                  tag=f"crs{tti}")
            ts(SM[:, GATES + tti * 8:GATES + tti * 8 + 8], gg(5), cc(3),
               None, ALU.mult, ALU.bypass,
               r=[("sm", "cmsk"), ("sm", "crs")], w=[("sm", "gates", tti)],
               tag=f"cgf{tti}")
        for cm in range(8):
            for tc in range(TQ // 512):
                ii = (cm * (TQ // 512) + tc) % 4
                ps, kps = psb(ii), KBs[ii]
                for k in range(8):
                    mm(ps[:, 0:512],
                       AR2[:, WSL + k * 1024 + cm * 128:
                           WSL + k * 1024 + cm * 128 + 128],
                       AR2[:, k * TQ + tc * 512: k * TQ + tc * 512 + 512],
                       k == 0, k == 7,
                       r=[("ar2", "wslot"), ("ar2", "xn1Tq")],
                       w=[kps], tag=f"wi{q}_{cm}_{tc}_{k}")
                acopy(AR2[:, XIN0 + cm * TQ + tc * 512:
                          XIN0 + cm * TQ + tc * 512 + 512],
                      ps[:, 0:512], r=[kps],
                      w=[("ar2", "xin", cm)], tag=f"wie{q}_{cm}_{tc}")
        kb.dma("sp", "wTin",
               lambda e: e.dma_start(AR2[:, WSL:WSL + 8192], sc_woT.ap()),
               r=[("hbm", "wo")], w=[("ar2", "wslot")], tag=f"woTin{q}")

    def p34_scan_piece(q, ct, nseg):
        base = SSP + ct * 48
        kxin = ("ar2", "xin", ct)
        xin_ap = AR2[:, XIN0 + ct * TQ: XIN0 + ct * TQ + TQ]
        YY = AR4f[:, 3072:3072 + TQ]
        HH = AR4f[:, 2048:2048 + TQ]
        if nseg == 0:
            ts(YY, xin_ap, SM[:, DSC + ct:DSC + ct + 1], None,
               ALU.mult, ALU.bypass, r=[kxin, ("sm", "dsc")],
               w=[("sm", "yy")], tag=f"yD{q}_{ct}")
        for n in range(nseg * 4, nseg * 4 + 4):
            par = n % 2
            bxp = AR4f[:, par * 1024:par * 1024 + TQ]
            kbx = ("sm", "bx", par)
            act(bxp, xin_ap, AF.Copy,
                r=[kxin, ("sm", "bbar", ct)], w=[kbx],
                tag=f"bx{q}_{ct}_{n}",
                scale=SM[:, base + 16 + n:base + 17 + n])
            kb.op("dve", lambda e, base=base, n=n, ct=ct, bxp=bxp:
                  e.tensor_tensor_scan(
                      HH, SM[:, base + n:base + n + 1].broadcast_to(
                          [128, TQ]),
                      bxp, SM[:, ST0 + ct * 16 + n:ST0 + ct * 16 + n + 1],
                      ALU.mult, ALU.add),
                  r=[kbx, ("sm", "abar", ct), ("sm", "sst", ct, n)],
                  w=[("sm", "hh")], tag=f"sc{q}_{ct}_{n}")
            dcopy(SM[:, ST0 + ct * 16 + n:ST0 + ct * 16 + n + 1],
                  HH[:, TQ - 1:TQ], r=[("sm", "hh")],
                  w=[("sm", "sst", ct, n)], tag=f"sv{q}_{ct}_{n}")
            stt(YY, HH, SM[:, base + 32 + n:base + 33 + n], YY,
                ALU.mult, ALU.add,
                r=[("sm", "hh"), ("sm", "cssm", ct), ("sm", "yy")],
                w=[("sm", "yy")], tag=f"ya{q}_{ct}_{n}")
        if nseg == 3:
            dcopy(bass.AP(AR2[:, 0:1].tensor, XIN0 + ct * TQ,
                          [[33280, 128], [1, TQ]]), YY,
                  r=[("sm", "yy")], w=[kxin, ("ar2", "yb", ct)],
                  tag=f"yb{q}_{ct}", eng="pool")

    def p34_quarter_post(q):
        rot = [(psA0, KA0), (psA1, KA1),
               (psb(0), KBs[0]), (psb(1), KBs[1])]
        for mt in range(TQ // 128):
            for dc in range(2):
                ps, kps = rot[(mt * 2 + dc) % 4]
                for ct in range(8):
                    mm(ps[:, 0:512],
                       bass.AP(AR2[:, 0:1].tensor,
                               XIN0 + ct * TQ + mt * 128,
                               [[33280, 128], [1, 128]]),
                       AR2[:, WSL + ct * 1024 + dc * 512:
                           WSL + ct * 1024 + dc * 512 + 512],
                       ct == 0, ct == 7,
                       r=[("ar2", "yb", ct), ("ar2", "wslot")],
                       w=[kps], tag=f"wo{q}_{mt}_{dc}_{ct}")
                acopy(SMBf[:, dc * 512:dc * 512 + 512],
                      ps[:, 0:512], r=[kps], w=[("smb", "s1t", dc)],
                      tag=f"woe{q}_{mt}_{dc}")
            fin_l1(q, mt)

    NGRAN = NE * (H1 // 1024)
    def p34_granule_post(q, gi):
        for pidx in (2 * gi, 2 * gi + 1):
            if pidx < 32:
                p34_scan_piece(q, pidx // 4, pidx % 4)

    def fin_l1(q, mt):
        tti = q * (TQ // 128) + mt
        par = mt % 2
        gt = SM[:, PH + par * 1024:PH + par * 1024 + 1024]
        xt = SM[:, PH + 2048:PH + 3072]
        kg = ("sm", "p4g", par)
        kx = ("sm", "p4x")
        s1ap = SMBf[:, 0:1024]
        ks1 = [("smb", "s1t", dc) for dc in range(2)]
        racc = [("ar3", "acc", mt, dc) for dc in range(2)]
        act(gt, AR3[:, mt * 1024:(mt + 1) * 1024], AF.Silu,
            r=racc, w=[kg], tag=f"sl1{q}_{mt}")
        kb.dma("sp", "x1l",
               lambda e, tti=tti, xt=xt: e.dma_start(
                   xt, sc_x1.ap()[tti * 128:(tti + 1) * 128, :]),
               w=[kx], tag=f"x1l{tti}")
        tt(gt, s1ap, gt, ALU.mult, r=ks1 + [kg], w=[kg], tag=f"sg1{tti}")
        tt(xt, gt, xt, ALU.add, r=[kg, kx], w=[kx], tag=f"x2_{tti}")
        kb.dma("sp", "outo",
               lambda e, tti=tti, xt=xt: e.dma_start(
                   OUT.ap()[tti * 128:(tti + 1) * 128, :], xt),
               r=[kx], tag=f"outs{tti}")

    emit_moe(1, W1_1, B1_1, W2_1, B2_1, H1, None,
             AR1, "ar1", lambda hm, tc: (
                 AR2[:, HB0 + hm * TQ + tc * 512:
                     HB0 + hm * TQ + (tc + 1) * 512],
                 ("ar2", "hbuf", hm, tc)),
             xnT_l1, quarter_pre=p34_quarter_pre,
             granule_post=p34_granule_post, quarter_post=p34_quarter_post)

    guard = kb.finalize_and_emit()
    ctx2 = ctx
    return nc, (ctx2, guard)



_CACHE = {}


def _get_nc(T):
    if T not in _CACHE:
        _CACHE[T] = build_nc(T)
    return _CACHE[T]


def kernel(L0_attn_x, L0_attn_W_qkv, L0_ffn_gate_w, L0_ffn_gate_b,
           L0_ffn_rms_w, L0_ffn_w1, L0_ffn_b1, L0_ffn_w2, L0_ffn_b2,
           L1_attn_W_in, L1_attn_W_out, L1_attn_A_log, L1_attn_B_ssm,
           L1_attn_C_ssm, L1_attn_dt_log, L1_attn_D_ssm, L1_ffn_centroids,
           L1_ffn_W1, L1_ffn_b1, L1_ffn_W2, L1_ffn_b2):
    x = np.ascontiguousarray(np.asarray(L0_attn_x, dtype=np.float32))
    B, T, d = x.shape
    assert B == 8 and d == D
    nc, _guard = _get_nc(T)
    f32 = lambda a: np.ascontiguousarray(np.asarray(a, dtype=np.float32))
    shared = {
        "w_qkv": f32(L0_attn_W_qkv), "gate_w": f32(L0_ffn_gate_w),
        "gate_b": f32(L0_ffn_gate_b), "rms_w": f32(L0_ffn_rms_w),
        "l0_w1": f32(L0_ffn_w1), "l0_b1": f32(L0_ffn_b1),
        "l0_w2": f32(L0_ffn_w2), "l0_b2": f32(L0_ffn_b2),
        "w_in": f32(L1_attn_W_in), "w_out": f32(L1_attn_W_out),
        "a_log": f32(L1_attn_A_log), "b_ssm": f32(L1_attn_B_ssm),
        "c_ssm": f32(L1_attn_C_ssm), "dt_log": f32(L1_attn_dt_log),
        "d_ssm": f32(L1_attn_D_ssm), "centroids": f32(L1_ffn_centroids),
        "l1_w1": f32(L1_ffn_W1), "l1_b1": f32(L1_ffn_b1),
        "l1_w2": f32(L1_ffn_W2), "l1_b2": f32(L1_ffn_b2),
    }
    in_maps = [dict(shared, x=x[i]) for i in range(8)]
    res = run_bass_kernel_spmd(nc, in_maps, core_ids=list(range(8)))
    return np.stack([res.results[i]["out"] for i in range(8)], axis=0)


# version 16
# speedup vs baseline: 1.1787x; 1.0656x over previous
import numpy as np
import concourse.bass as bass
import concourse.mybir as mybir
from concourse.bass_utils import run_bass_kernel_spmd

F32 = mybir.dt.float32
BF16 = mybir.dt.bfloat16
AF = mybir.ActivationFunctionType
ALU = mybir.AluOpType
AX = mybir.AxisListType

D = 1024
NH = 16
HD = 64
NE = 8
H0 = 2048
H1 = 4096
NS = 16
LN_EPS = 1e-5
RMS_EPS = 1.1920929e-07


ENGINES = ("pe", "act", "dve", "pool", "sp")


class _Op:
    __slots__ = ("eng", "fn", "idx", "slot", "need_inc", "cnt", "waits", "tag")

    def __init__(self, eng, fn, idx, slot, tag):
        self.eng = eng
        self.fn = fn
        self.idx = idx
        self.slot = slot
        self.need_inc = False
        self.cnt = None
        self.waits = {}
        self.tag = tag


class KB:
    def __init__(self, nc):
        self.nc = nc
        self.streams = {e: [] for e in ENGINES}
        self.slots = {}
        self.slot_eng = {}
        self.last_w = {}
        self.readers = {}
        self.n_ops = 0

    def _stream_of(self, op):
        return op.slot if op.slot is not None else op.eng

    def _edge(self, consumer, producer):
        if producer is consumer:
            return
        if (producer.eng == "pe" and consumer.eng == "pe"
                and producer.slot is None and consumer.slot is None):
            return
        sk = self._stream_of(producer)
        if consumer.waits.get(sk, 0) < producer.idx:
            consumer.waits[sk] = producer.idx
        producer.need_inc = True

    def _record(self, op, r, w):
        for k in r:
            p = self.last_w.get(k)
            if p is not None:
                self._edge(op, p)
            self.readers.setdefault(k, {})[self._stream_of(op)] = op
        for k in w:
            p = self.last_w.get(k)
            if p is not None:
                self._edge(op, p)
            for rd in self.readers.get(k, {}).values():
                self._edge(op, rd)
            self.readers[k] = {}
            self.last_w[k] = op

    def op(self, eng, fn, r=(), w=(), tag=""):
        st = self.streams[eng]
        op = _Op(eng, fn, len(st) + 1, None, tag)
        self._record(op, r, w)
        st.append(op)
        self.n_ops += 1
        return op

    def dma(self, eng, slot, fn, r=(), w=(), tag=""):
        assert eng in ("sp", "act", "pool")
        sl = self.slots.setdefault(slot, [])
        if slot in self.slot_eng:
            assert self.slot_eng[slot] == eng, (slot, eng)
        else:
            self.slot_eng[slot] = eng
        op = _Op(eng, fn, len(sl) + 1, slot, tag)
        if sl:
            self._edge(op, sl[-1])
        self._record(op, r, w)
        sl.append(op)
        self.streams[eng].append(op)
        self.n_ops += 1
        return op

    def barrier(self):
        tails = []
        for st in self.streams.values():
            for o in reversed(st):
                if o.slot is None:
                    tails.append(o)
                    break
        for sl in self.slots.values():
            if sl:
                tails.append(sl[-1])
        for i, t in enumerate(tails):
            k = ("__bar__", i)
            self.last_w[k] = t
            self.readers[k] = {}
        self._bar_keys = [("__bar__", i) for i in range(len(tails))]
        self._bar_pending = dict.fromkeys(ENGINES, True)
        self._bar_tails = tails
        return self._bar_keys

    def _maybe_bar(self, op):
        if getattr(self, "_bar_pending", None) and self._bar_pending.get(op.eng):
            for t in self._bar_tails:
                self._edge(op, t)
            self._bar_pending[op.eng] = False

    def finalize_and_emit(self):
        nc = self.nc
        tails = []
        for st in self.streams.values():
            for o in reversed(st):
                if o.slot is None:
                    tails.append(o)
                    break
        for sl in self.slots.values():
            if sl:
                tails.append(sl[-1])
        fin = _Op("sp", lambda e: e.nop(), len(self.streams["sp"]) + 1, None,
                  "final")
        for t in tails:
            self._edge(fin, t)
        self.streams["sp"].append(fin)

        for e, st in self.streams.items():
            c = 0
            for op in st:
                if op.slot is not None:
                    continue
                if op.need_inc:
                    c += 1
                    op.cnt = c
        for slot, sl in self.slots.items():
            c = 0
            for op in sl:
                op.need_inc = True
                c += 16
                op.cnt = c

        nextcnt = {}
        for e, st in self.streams.items():
            arr = {}
            pend = []
            for op in st:
                if op.slot is not None:
                    continue
                pend.append(op.idx)
                if op.need_inc:
                    for i in pend:
                        arr[i] = op.cnt
                    pend = []
            for i in pend:
                arr[i] = None
            nextcnt[e] = arr
        for slot, sl in self.slots.items():
            nextcnt[slot] = {op.idx: op.cnt for op in sl}

        used = set()
        for st in self.streams.values():
            for op in st:
                for k in op.waits:
                    used.add(k)
                if op.need_inc:
                    used.add(self._stream_of(op))
        from contextlib import ExitStack
        self._ctx = ExitStack()
        sems = {}
        for i, k in enumerate(sorted(used, key=str)):
            sems[k] = self._ctx.enter_context(nc.semaphore(f"s{i}"))

        eng_attr = {"pe": "tensor", "act": "scalar", "dve": "vector",
                    "pool": "gpsimd", "sp": "sync"}
        with nc.Block() as blk:
            for e in ENGINES:
                st = self.streams[e]
                if not st:
                    continue

                def body(engine, _st=st):
                    hw = {}
                    for op in _st:
                        for skey, pidx in sorted(op.waits.items(),
                                                 key=lambda kv: str(kv[0])):
                            val = nextcnt[skey][pidx]
                            assert val is not None, (op.tag, skey, pidx)
                            if hw.get(skey, 0) < val:
                                engine.wait_ge(sems[skey], val)
                                hw[skey] = val
                        inst = op.fn(engine)
                        if op.need_inc:
                            sk = self._stream_of(op)
                            inst.then_inc(sems[sk], 16 if op.slot else 1)

                getattr(blk, eng_attr[e])(body)
        return self._ctx


_orig_op, _orig_dma = KB.op, KB.dma


def _op(self, eng, fn, r=(), w=(), tag=""):
    o = _orig_op(self, eng, fn, r, w, tag)
    self._maybe_bar(o)
    return o


def _dma(self, eng, slot, fn, r=(), w=(), tag=""):
    o = _orig_dma(self, eng, slot, fn, r, w, tag)
    self._maybe_bar(o)
    return o


KB.op, KB.dma = _op, _dma



def build_nc(T=4096, dev_scratch=False):
    assert T % 128 == 0 and T >= 512
    NT = T // 128
    TQ = min(1024, T)
    NQ = T // TQ
    QG = min(1024, T)
    NQG = T // QG
    NJT = T // 128
    NTC = T // 512

    nc = bass.Bass("TRN2", target_bir_lowering=False)

    ins = {}
    def din(name, shape):
        t = nc.dram_tensor(name, shape, F32, kind="ExternalInput")
        ins[name] = t
        return t

    X = din("x", [T, D])
    Wqkv = din("w_qkv", [3 * D, D])
    Gw = din("gate_w", [NE, D])
    Gb = din("gate_b", [NE])
    Rw = din("rms_w", [NE])
    W1_0 = din("l0_w1", [NE, D, H0])
    B1_0 = din("l0_b1", [NE, H0])
    W2_0 = din("l0_w2", [NE, H0, D])
    B2_0 = din("l0_b2", [NE, D])
    Win = din("w_in", [D, D])
    Wout = din("w_out", [D, D])
    Alog = din("a_log", [D, NS])
    Bssm = din("b_ssm", [D, NS])
    Cssm = din("c_ssm", [D, NS])
    DTlog = din("dt_log", [D])
    Dssm = din("d_ssm", [D])
    Cent = din("centroids", [NE, D])
    W1_1 = din("l1_w1", [NE, D, H1])
    B1_1 = din("l1_b1", [NE, H1])
    W2_1 = din("l1_w2", [NE, H1, D])
    B2_1 = din("l1_b2", [NE, D])
    OUT = nc.dram_tensor("out", [T, D], F32, kind="ExternalOutput")

    sck = "ExternalOutput" if dev_scratch else "Internal"
    sc_xn0t = nc.dram_tensor("sc_xn0t", [128, 8 * T], BF16, kind=sck)
    sc_w10 = nc.dram_tensor("sc_w10", [NE, D, H0], BF16, kind="Internal")
    sc_w20 = nc.dram_tensor("sc_w20", [NE, H0, D], BF16, kind="Internal")
    sc_w11 = nc.dram_tensor("sc_w11", [NE, D, H1], BF16, kind="Internal")
    sc_w21 = nc.dram_tensor("sc_w21", [NE, H1, D], BF16, kind="Internal")
    sc_qt = nc.dram_tensor("sc_qt", [D, T], BF16, kind=sck)
    sc_kt = nc.dram_tensor("sc_kt", [D, T], BF16, kind=sck)
    sc_vt = nc.dram_tensor("sc_vt", [D, T], BF16, kind=sck)
    sc_s0 = nc.dram_tensor("sc_s0", [T, D], F32, kind=sck)
    sc_x1 = nc.dram_tensor("sc_x1", [T, D], F32, kind=sck)
    sc_xn1 = nc.dram_tensor("sc_xn1", [T, D], BF16, kind=sck)
    sc_s1 = nc.dram_tensor("sc_s1", [T, D], F32, kind=sck)

    from contextlib import ExitStack
    ctx = ExitStack()
    AR1 = ctx.enter_context(nc.sbuf_tensor("AR1", [128, max(8 * T, 32768)], BF16))
    AR2 = ctx.enter_context(nc.sbuf_tensor("AR2", [128, 33280], BF16))
    AR3 = ctx.enter_context(nc.sbuf_tensor("AR3", [128, 8192], F32))
    AR4 = ctx.enter_context(nc.sbuf_tensor("AR4", [128, 8192], BF16))
    SM = ctx.enter_context(nc.sbuf_tensor("SM", [128, 5632], F32))
    SMB = ctx.enter_context(nc.sbuf_tensor("SMB", [128, 4096], BF16))
    psA0 = ctx.enter_context(nc.psum_tensor("psA0", [128, 512], F32))
    psA1 = ctx.enter_context(nc.psum_tensor("psA1", [128, 512], F32))
    psB = ctx.enter_context(nc.psum_tensor("psB", [128, 2048], F32))
    psT = ctx.enter_context(nc.psum_tensor("psT", [128, 1024], BF16))
    psF = ctx.enter_context(nc.psum_tensor("psF", [128, 512], F32))

    SMbf = SM.ap().bitcast(BF16)
    AR3bf = AR3.ap().bitcast(BF16)

    kb = KB(nc)

    IDF = 0
    IO2 = 128
    STATS = 256
    GT = 320
    BC = 448
    NORMG = 480
    GATES = 512
    IDB = 2 * 768
    ONESB = 2 * 832
    ONESF = 896
    OC8 = 1024
    GWTB = 2 * 1026
    CENTB = 2 * 1058
    PH = 1152

    KA0, KA1, KT, KF = (("ps", "A0"), ("ps", "A1"), ("ps", "T"), ("ps", "F"))
    KBs = [("ps", "B", i) for i in range(4)]

    def psb(i):
        return psB[:, i * 512:(i + 1) * 512]

    def mm(ps_ap, lhsT_ap, rhs_ap, start, stop, r, w, tag):
        kb.op("pe", lambda e: e.matmul(ps_ap, lhsT_ap, rhs_ap,
                                       start=start, stop=stop),
              r=r, w=w, tag=tag)

    def tr(ps_ap, in_ap, id_ap, r, w, tag):
        p = in_ap.shape[0]
        ids = id_ap[0:p, 0:p]
        kb.op("pe", lambda e: e.transpose(ps_ap, in_ap, ids),
              r=r, w=w, tag=tag)

    def act(out_ap, in_ap, func, r, w, tag, bias=0.0, scale=1.0, accum=None):
        kb.op("act", lambda e: e.activation(out_ap, in_ap, func, bias=bias,
                                            scale=scale, accum_out=accum),
              r=r, w=w, tag=tag)

    def acopy(out_ap, in_ap, r, w, tag):
        kb.op("act", lambda e: e.copy(out_ap, in_ap), r=r, w=w, tag=tag)

    def tt(out_ap, a_ap, b_ap, op, r, w, tag, eng="dve"):
        kb.op(eng, lambda e: e.tensor_tensor(out_ap, a_ap, b_ap, op),
              r=r, w=w, tag=tag)

    def ts(out_ap, in_ap, s1, s2, op0, op1, r, w, tag, eng="dve"):
        kb.op(eng, lambda e: e.tensor_scalar(out_ap, in_ap, s1, s2, op0, op1),
              r=r, w=w, tag=tag)

    def stt(out_ap, in0_ap, scal, in1_ap, op0, op1, r, w, tag, eng="dve"):
        kb.op(eng, lambda e: e.scalar_tensor_tensor(out_ap, in0_ap, scal,
                                                    in1_ap, op0, op1),
              r=r, w=w, tag=tag)

    def red(out_ap, in_ap, op, r, w, tag, negate=False):
        if op == "max":
            kb.op("dve", lambda e: e.reduce_max(out_ap, in_ap, AX.X,
                                                negate=negate),
                  r=r, w=w, tag=tag)
        else:
            kb.op("dve", lambda e: e.reduce_sum(out_ap, in_ap, AX.X),
                  r=r, w=w, tag=tag)

    def dcopy(out_ap, in_ap, r, w, tag, eng="dve"):
        kb.op(eng, lambda e: e.tensor_copy(out_ap, in_ap), r=r, w=w, tag=tag)

    def recip(out_ap, in_ap, r, w, tag):
        kb.op("dve", lambda e: e.reciprocal(out_ap, in_ap), r=r, w=w, tag=tag)

    def memset(ap, v, w, tag, eng="dve"):
        kb.op(eng, lambda e: e.memset(ap, v), w=w, tag=tag)

    kb.op("pool", lambda e: e.iota(SM[:, IDF:IDF + 128], [[1, 128]],
                                   channel_multiplier=0,
                                   allow_small_or_imprecise_dtypes=True),
          w=[("sm", "idf")], tag="iota1")
    kb.op("pool", lambda e: e.iota(SM[:, IO2:IO2 + 128], [[0, 128]],
                                   channel_multiplier=1,
                                   allow_small_or_imprecise_dtypes=True),
          w=[("sm", "io2")], tag="iota2")
    tt(SM[:, IDF:IDF + 128], SM[:, IDF:IDF + 128], SM[:, IO2:IO2 + 128],
       ALU.is_equal, r=[("sm", "idf"), ("sm", "io2")], w=[("sm", "idf")],
       tag="ideq")
    dcopy(SMbf[:, IDB:IDB + 128], SM[:, IDF:IDF + 128],
          r=[("sm", "idf")], w=[("sm", "idb")], tag="idb")
    memset(SMbf[0:1, ONESB:ONESB + 128], 1.0, w=[("sm", "onesb")], tag="onesb")

    memset(SM[0:1, ONESF:ONESF + 128], 1.0, w=[("sm", "onesf")], tag="onesf")
    memset(SM[0:8, OC8:OC8 + 1], 1.0, w=[("sm", "onescol8")], tag="ones8")

    IDF_AP = SM[:, IDF:IDF + 128]
    IDB_AP = SMbf[:, IDB:IDB + 128]

    kb.op("pool", lambda e: e.iota(SM[0:1, GT:GT + 8], [[1, 8]],
                                   channel_multiplier=0,
                                   allow_small_or_imprecise_dtypes=True),
          w=[("sm", "tierow")], tag="tiota")
    ts(SM[0:1, GT:GT + 8], SM[0:1, GT:GT + 8], -(2.0 ** -13),
       7 * 2.0 ** -13, ALU.mult, ALU.add,
       r=[("sm", "tierow")], w=[("sm", "tierow")], tag="tiesc")
    kb.dma("sp", "ld_small",
           lambda e: e.dma_start(SM[0:1, GT + 8:GT + 16],
                                 Gb.ap().unsqueeze(0)),
           w=[("sm", "gbrow")], tag="gbrow")
    kb.dma("sp", "ld_small",
           lambda e: e.dma_start(SM[0:1, GT + 16:GT + 24],
                                 Rw.ap().unsqueeze(0)),
           w=[("sm", "rwrow")], tag="rwrow")

    def bcast8(dst_c, src_c, rkey, wkey, tag):
        mm(psF[:, 0:8], SM[0:1, ONESF:ONESF + 128], SM[0:1, src_c:src_c + 8],
           True, True, r=[("sm", "onesf"), rkey], w=[KF], tag=tag + "mm")
        acopy(SM[:, dst_c:dst_c + 8], psF[:, 0:8], r=[KF], w=[wkey],
              tag=tag + "ev")

    bcast8(BC, GT + 8, ("sm", "gbrow"), ("sm", "gb_bc"), "gbbc")
    bcast8(BC + 8, GT + 16, ("sm", "rwrow"), ("sm", "rw_bc"), "rwbc")
    bcast8(BC + 16, GT, ("sm", "tierow"), ("sm", "tie_bc"), "tiebc")

    GWST = 2 * (PH + 3072)
    kb.dma("pool", "gwld",
           lambda e: e.dma_start(SMbf[0:8, GWST:GWST + 1024], Gw.ap()),
           w=[("sm", "gwst")], tag="gwld")
    for k in range(8):
        tr(psT[:, k * 8:k * 8 + 8], SMbf[0:8, GWST + k * 128:GWST + (k + 1) * 128],
           IDB_AP, r=[("sm", "gwst"), ("sm", "idb")], w=[KT], tag=f"gwtr{k}")
    acopy(SMbf[:, GWTB:GWTB + 64], psT[:, 0:64], r=[KT], w=[("sm", "gwT")],
          tag="gwev")

    def gwT_block(k):
        return SMbf[:, GWTB + k * 8:GWTB + k * 8 + 8]

    def emit_ln(x_ap, kx, xn_ap, kxn, sq_scr, ksq, tag, par=0):
        SB = STATS + par * 8
        s0, s1c = SM[:, SB:SB + 1], SM[:, SB + 1:SB + 2]
        s2, s3 = SM[:, SB + 2:SB + 3], SM[:, SB + 3:SB + 4]
        s4, s5 = SM[:, SB + 4:SB + 5], SM[:, SB + 5:SB + 6]
        s6, s7 = SM[:, SB + 6:SB + 7], SM[:, SB + 7:SB + 8]
        red(s0, x_ap, "sum", r=[kx], w=[("sm", "st0", par)], tag=tag + "sum")
        tt(sq_scr, x_ap, x_ap, ALU.mult, r=[kx], w=[ksq], tag=tag + "sq")
        red(s1c, sq_scr, "sum", r=[ksq], w=[("sm", "st1", par)], tag=tag + "sqs")
        ts(s2, s0, 1.0 / D, None, ALU.mult, ALU.bypass,
           r=[("sm", "st0", par)], w=[("sm", "st2", par)], tag=tag + "mu")
        ts(s3, s2, s2, None, ALU.mult, ALU.bypass,
           r=[("sm", "st2", par)], w=[("sm", "st3", par)], tag=tag + "mu2")
        ts(s4, s1c, 1.0 / D, None, ALU.mult, ALU.bypass,
           r=[("sm", "st1", par)], w=[("sm", "st4", par)], tag=tag + "ex2")
        stt(s5, s4, LN_EPS, s3, ALU.add, ALU.subtract,
            r=[("sm", "st4", par), ("sm", "st3", par)], w=[("sm", "st5", par)],
            tag=tag + "var")
        act(s6, s5, AF.Ln, r=[("sm", "st5", par)], w=[("sm", "st6", par)],
            tag=tag + "ln")
        act(s7, s6, AF.Exp, r=[("sm", "st6", par)], w=[("sm", "st7", par)],
            tag=tag + "ex", bias=0.0, scale=-0.5)
        ts(xn_ap, x_ap, s2, s7, ALU.subtract, ALU.mult,
           r=[kx, ("sm", "st2", par), ("sm", "st7", par)], w=[kxn], tag=tag + "xn")

    def emit_t8(xn_ap, kxn, AR, artag, tt_i, tag):
        for k in range(8):
            tr(psT[:, k * 128:(k + 1) * 128], xn_ap[:, k * 128:(k + 1) * 128],
               IDB_AP, r=[kxn, ("sm", "idb")], w=[KT], tag=f"{tag}t{k}")
        dst = bass.AP(AR[:, 0:1].tensor, tt_i * 128,
                      [[AR.shape[1], 128], [T, 8], [1, 128]])
        acopy(dst, psT.ap().rearrange("p (k c) -> p k c", k=8),
              r=[KT], w=[artag + (tt_i,)], tag=f"{tag}ev")

    def wqkvT_block(k, ob):
        return AR2[:, k * 3072 + ob * 128: k * 3072 + ob * 128 + 128]

    for ob in range(24):
        st = ob % 2
        stage = SMB[:, 2048 + st * 1024: 2048 + (st + 1) * 1024]
        kst = ("smb", "wqst", st)
        kb.dma("pool", f"wq{st}",
               lambda e, ob=ob, stage=stage: e.dma_start(
                   stage, Wqkv.ap()[ob * 128:(ob + 1) * 128, :]),
               w=[kst], tag=f"wqld{ob}")
        for k in range(8):
            tr(psT[:, k * 128:(k + 1) * 128], stage[:, k * 128:(k + 1) * 128],
               IDB_AP, r=[kst, ("sm", "idb")], w=[KT], tag=f"wqtr{ob}_{k}")
        dst = bass.AP(AR2[:, 0:1].tensor, ob * 128,
                      [[33280, 128], [3072, 8], [1, 128]])
        acopy(dst, psT.ap().rearrange("p (k c) -> p k c", k=8),
              r=[KT], w=[("ar2", "wqkvT", ob)], tag=f"wqev{ob}")

    for tti in range(NT):
        st = tti % 2
        xt = SM[:, PH + st * 1024: PH + (st + 1) * 1024]
        xnt = SMB[:, st * 1024:(st + 1) * 1024]
        kx, kxn = ("sm", "xt", st), ("smb", "xn", st)
        kb.dma("sp", f"x{st}",
               lambda e, tti=tti, xt=xt: e.dma_start(
                   xt, X.ap()[tti * 128:(tti + 1) * 128, :]),
               w=[kx], tag=f"xld{tti}")
        emit_ln(xt, kx, xnt, kxn, AR3[:, st * 1024:(st + 1) * 1024],
                ("sm", "lnsq", st), f"ln0_{tti}_", par=st)
        emit_t8(xnt, kxn, AR1, ("ar1", "xnT"), tti, f"x0_{tti}")

        psg = psA0 if st == 0 else psA1
        kpg = KA0 if st == 0 else KA1
        GB = GT + st * 56
        for k in range(8):
            mm(psg[:, 0:8],
               AR1[:, k * T + tti * 128: k * T + tti * 128 + 128],
               gwT_block(k), k == 0, k == 7,
               r=[("ar1", "xnT", tti), ("sm", "gwT")], w=[kpg],
               tag=f"gl{tti}_{k}")
        g0, g1 = SM[:, GB:GB + 8], SM[:, GB + 8:GB + 16]
        l34 = SM[:, GB + 16:GB + 24]
        e8 = SM[:, GB + 24:GB + 32]
        c = lambda i: SM[:, GB + 32 + i:GB + 33 + i]
        tt(g0, psg[:, 0:8], SM[:, BC:BC + 8], ALU.add,
           r=[kpg, ("sm", "gb_bc")], w=[("sm", "glg", st)],
           tag=f"gev{tti}")
        tt(g1, g0, g0, ALU.mult, r=[("sm", "glg", st)], w=[("sm", "gsq", st)],
           tag=f"gsq{tti}")
        red(c(0), g1, "sum", r=[("sm", "gsq", st)], w=[("sm", "gms", st)],
            tag=f"gms{tti}")
        ts(c(0), c(0), 1.0 / NE, RMS_EPS, ALU.mult, ALU.add,
           r=[("sm", "gms", st)], w=[("sm", "gms", st)], tag=f"gme{tti}")
        act(c(1), c(0), AF.Ln, r=[("sm", "gms", st)], w=[("sm", "gln", st)],
            tag=f"gln{tti}")
        act(c(2), c(1), AF.Exp, r=[("sm", "gln", st)], w=[("sm", "grs", st)],
            tag=f"grs{tti}", bias=0.0, scale=-0.5)
        ts(l34, g0, c(2), None, ALU.mult, ALU.bypass,
           r=[("sm", "glg", st), ("sm", "grs", st)], w=[("sm", "gl3", st)],
           tag=f"gl3{tti}")
        tt(l34, l34, SM[:, BC + 8:BC + 16], ALU.mult,
           r=[("sm", "gl3", st), ("sm", "rw_bc")], w=[("sm", "gl4", st)],
           tag=f"gl4{tti}")
        red(c(3), l34, "max", r=[("sm", "gl4", st)],
            w=[("sm", "gm", st)], tag=f"gm{tti}")
        ts(c(4), c(3), -2.0, None, ALU.mult, ALU.bypass,
           r=[("sm", "gm", st)], w=[("sm", "gmb", st)], tag=f"gmb{tti}")
        act(e8, l34, AF.Exp,
            r=[("sm", "gl4", st), ("sm", "gmb", st)], w=[("sm", "ge8", st)],
            tag=f"gex{tti}", bias=c(4), scale=2.0)
        red(c(5), e8, "sum", r=[("sm", "ge8", st)],
            w=[("sm", "gsm", st)], tag=f"gs{tti}")
        ts(c(6), c(5), 1e-6, 1.0, ALU.mult, ALU.add,
           r=[("sm", "gsm", st)], w=[("sm", "gt2", st)], tag=f"gt{tti}")
        recip(SM[:, NORMG + tti:NORMG + tti + 1], c(6),
              r=[("sm", "gt2", st)], w=[("sm", "normg", tti)],
              tag=f"gng{tti}")

        if tti % 4 == 3:
            tc = tti // 4
            for ob in range(24):
                ps, kps = psb(ob % 4), KBs[ob % 4]
                for k in range(8):
                    mm(ps[:, 0:512], wqkvT_block(k, ob),
                       AR1[:, k * T + tc * 512: k * T + tc * 512 + 512],
                       k == 0, k == 7,
                       r=[("ar2", "wqkvT", ob)] + [("ar1", "xnT", t4)
                          for t4 in range(tc * 4, tc * 4 + 4)],
                       w=[kps], tag=f"qkv{ob}_{tc}_{k}")
                scale = 0.125 if ob < 8 else 1.0
                act(AR3bf[:, 4096 + ob * 512:4096 + (ob + 1) * 512],
                    ps[:, 0:512], AF.Copy,
                    r=[kps], w=[("ar3", "qst", ob)], tag=f"qe{ob}_{tc}",
                    scale=scale)
            for gi, dstt in ((0, sc_qt), (1, sc_kt), (2, sc_vt)):
                dst = bass.AP(dstt, tc * 512,
                              [[T, 128], [128 * T, 8], [1, 512]])
                srcap = AR3bf[:, 4096 + gi * 4096:4096 + (gi + 1) * 4096]
                kb.dma("sp", f"qkvo{gi}",
                       lambda e, dst=dst, srcap=srcap: e.dma_start(
                           dst, srcap),
                       r=[("ar3", "qst", gi * 8 + oo) for oo in range(8)],
                       tag=f"qst{gi}_{tc}")

    for (Wsrc, Wdst, nm) in ((W1_0, sc_w10, "w10"), (W2_0, sc_w20, "w20"),
                             (W1_1, sc_w11, "w11"), (W2_1, sc_w21, "w21")):
        sz = Wsrc.ap().shape[1] * Wsrc.ap().shape[2]
        for e_i in range(NE):
            kb.dma("pool", "wcast",
                   lambda e, Wsrc=Wsrc, Wdst=Wdst, e_i=e_i, sz=sz:
                   e.dma_start(
                       bass.AP(Wdst, e_i * sz, [[sz, 1], [1, sz]]),
                       bass.AP(Wsrc, e_i * sz, [[sz, 1], [1, sz]])),
                   r=[("ar1", "xnT", NT - 1)],
                   w=[("hbm", nm, e_i)], tag=f"wc{nm}_{e_i}")

    kb.dma("sp", "xn0sp",
           lambda e: e.dma_start(sc_xn0t.ap(), AR1[:, 0:8 * T]),
           r=[("ar1", "xnT", t) for t in range(NT)], tag="xn0out")

    kb.barrier()

    def vaug_block(h, jt):
        c = (jt * NH + h) * 65
        return AR2[:, c:c + 65]

    memset(bass.AP(AR2[:, 0:1].tensor, 64,
                   [[33280, 128], [65, NH * NJT], [1, 1]]), 1.0,
           w=[("ar2", "vones")], tag="vones")

    for h in range(NH):
        stg = h % 2
        vstage = AR4[:, stg * 4096: stg * 4096 + T]
        kvst = ("ar4", "vst", stg)
        kb.dma("sp", f"vld{stg}",
               lambda e, h=h, vstage=vstage: e.dma_start(
                   vstage[0:64, :], sc_vt.ap()[h * 64:(h + 1) * 64, :]),
               w=[kvst], tag=f"vld{h}")
        for j0 in range(0, NJT, 8):
            nspan = min(8, NJT - j0)
            for jj in range(nspan):
                jt = j0 + jj
                tr(psT[:, jj * 64:jj * 64 + 64],
                   vstage[0:64, jt * 128:(jt + 1) * 128], IDB_AP,
                   r=[kvst, ("sm", "idb")], w=[KT], tag=f"vt{h}_{jt}")
            dst = bass.AP(AR2[:, 0:1].tensor, (j0 * NH + h) * 65,
                          [[33280, 128], [65 * NH, nspan], [1, 64]])
            src = bass.AP(psT[:, 0:1].tensor, 0,
                          [[1024, 128], [64, nspan], [1, 64]])
            acopy(dst, src, r=[KT], w=[("ar2", "va", h, j0 // 8)],
                  tag=f"vev{h}_{j0}")

    kb.barrier()
    memset(SMB[64:65, 0:T], 1.0, w=[("smb", "kones")], tag="kones0")
    memset(AR4[64:65, 0:T], 1.0, w=[("ar4", "kones")], tag="kones1")
    NQT = QG // 128

    def kaug_buf(par):
        return SMB if par == 0 else AR4

    def kkey(par):
        return ("smb", "kaug") if par == 0 else ("ar4", "kaug")

    QA = [8192 - 2 * QG, 8192 - QG]
    pend_fin = [None]

    for g in range(NQG):
        for h in range(NH):
            par = h % 2
            KB_ = kaug_buf(par)
            QA0 = QA[par]
            kq = ("ar4", "qaug", par)
            kqm = ("ar4", "qaugm", par)
            kb.dma("sp", "kaug" + str(par),
                   lambda e, h=h, KB_=KB_: e.dma_start(
                       KB_[0:64, 0:T], sc_kt.ap()[h * 64:(h + 1) * 64, :]),
                   w=[kkey(par)], tag=f"kld{g}_{h}")
            kb.dma("sp", "qaug" + str(par),
                   lambda e, g=g, h=h, QA0=QA0: e.dma_start(
                       AR4[0:64, QA0:QA0 + QG],
                       sc_qt.ap()[h * 64:(h + 1) * 64, g * QG:(g + 1) * QG]),
                   w=[kq], tag=f"qld{g}_{h}")
            for q0 in range(0, NQT, 4):
                nb = min(4, NQT - q0)
                for i in range(nb):
                    qt = q0 + i
                    rhs = bass.AP(KB_[:, 0:1].tensor, 0,
                                  [[KB_.shape[1], 64], [NJT, 128]])
                    mm(psF[:, i * 128:(i + 1) * 128],
                       AR4[0:64, QA0 + qt * 128: QA0 + qt * 128 + 128], rhs,
                       True, True, r=[kq, kkey(par)],
                       w=[KF], tag=f"pa{g}_{h}_{qt}")
                for i in range(nb):
                    qt = q0 + i
                    red(SMbf[:, 2 * (PH + 4) + qt:2 * (PH + 4) + qt + 1],
                        psF[:, i * 128:(i + 1) * 128], "max",
                        r=[KF], w=[("sm", "negm", qt)],
                        tag=f"pam{g}_{h}_{qt}", negate=True)
            for qt in range(NQT):
                tr(psT[0:1, qt * 128:(qt + 1) * 128],
                   SMbf[:, 2 * (PH + 4) + qt:2 * (PH + 4) + qt + 1], IDB_AP,
                   r=[("sm", "negm", qt), ("sm", "idb")], w=[KT],
                   tag=f"pat{g}_{h}_{qt}")
            acopy(AR4[64:65, QA0:QA0 + QG], psT[0:1, 0:QG],
                  r=[KT], w=[kqm], tag=f"pam2{g}_{h}")
            if pend_fin[0] is not None:
                pend_fin[0]()
                pend_fin[0] = None
            NQC = QG // 512

            def bmm(jt):
                half = (jt % 2) * QG
                for qc in range(NQC):
                    ii = (half + qc * 512) // 512
                    mm(psB[:, half + qc * 512: half + (qc + 1) * 512],
                       KB_[0:65, jt * 128:(jt + 1) * 128],
                       AR4[0:65, QA0 + qc * 512: QA0 + qc * 512 + 512],
                       True, True,
                       r=[kkey(par), ("smb" if par == 0 else "ar4", "kones"),
                          kq, kqm],
                       w=[KBs[ii]], tag=f"pb{g}_{h}_{jt}_{qc}")

            def bexp(jt):
                half = (jt % 2) * QG
                act(AR1[:, jt * QG:(jt + 1) * QG],
                    psB[:, half:half + QG], AF.Exp,
                    r=[KBs[(half + qc * 512) // 512] for qc in range(NQC)],
                    w=[("ar1", "sT", jt)],
                    tag=f"pbe{g}_{h}_{jt}")

            def avmm(jt):
                for qc in range(NQC):
                    ps, kps = (psA0, KA0) if qc == 0 else (psA1, KA1)
                    mm(ps[0:65, 0:512], vaug_block(h, jt),
                       AR1[:, jt * QG + qc * 512: jt * QG + qc * 512 + 512],
                       jt == 0, jt == NJT - 1,
                       r=[("ar2", "va", h, jt // 8), ("ar2", "vones"),
                          ("ar1", "sT", jt)],
                       w=[kps], tag=f"av{g}_{h}_{qc}_{jt}")

            bmm(0)
            bexp(0)
            for jt in range(1, NJT):
                bmm(jt)
                bexp(jt)
                avmm(jt - 1)
            avmm(NJT - 1)
            def fin_head(g=g, h=h):
                for qc in range(NQC):
                    ps, kps = (psA0, KA0) if qc == 0 else (psA1, KA1)
                    acopy(SM[0:65, PH + 16 + qc * 512: PH + 16 + qc * 512 + 512],
                          ps[0:65, 0:512], r=[kps], w=[("sm", "oh", qc)],
                          tag=f"ave{g}_{h}_{qc}")
                for qt in range(NQT):
                    psf, kpsf = psb(qt % 4), KBs[qt % 4]
                    tr(psf[:, 0:65],
                       SM[0:65, PH + 16 + qt * 128: PH + 16 + qt * 128 + 128],
                       IDF_AP, r=[("sm", "oh", qt * 128 // 512), ("sm", "idf")],
                       w=[kpsf], tag=f"ot{g}_{h}_{qt}")
                    recip(SM[:, PH + 2 + qt % 2:PH + 3 + qt % 2], psf[:, 64:65],
                          r=[kpsf], w=[("sm", "linv", qt % 2)],
                          tag=f"ol{g}_{h}_{qt}")
                    ts(AR3[:, qt * 1024 + h * 64: qt * 1024 + h * 64 + 64],
                       psf[:, 0:64], SM[:, PH + 2 + qt % 2:PH + 3 + qt % 2],
                       None, ALU.mult,
                       ALU.bypass, r=[kpsf, ("sm", "linv", qt % 2)],
                       w=[("ar3", "sg", qt, h)], tag=f"os{g}_{h}_{qt}")
            pend_fin[0] = fin_head
        if pend_fin[0] is not None:
            pend_fin[0]()
            pend_fin[0] = None
        s0dst = bass.AP(sc_s0, g * QG * 1024,
                        [[1024, 128], [128 * 1024, NQT], [1, 1024]])
        kb.dma("sp", "s0o",
               lambda e, s0dst=s0dst: e.dma_start(s0dst, AR3[:, 0:NQT * 1024]),
               r=[("ar3", "sg", qt, h) for qt in range(NQT)
                  for h in range(NH)],
               tag=f"s0st{g}")

    kb.barrier()

    def emit_moe(layer, W1d, B1d, W2d, B2d, H, finalize_tile,
                 gr_a, gr_tag, hb, xnT_blk,
                 quarter_pre=None, granule_post=None, quarter_post=None):
        NHQ = H // 1024
        B1ST = PH + 3072
        B1C = PH + 3200
        for e_i in range(NE):
            nrow = H // 128
            kb.dma("sp", "b1ld",
                   lambda e, e_i=e_i, nrow=nrow: e.dma_start(
                       SM[0:nrow, B1ST:B1ST + 128],
                       B1d.ap()[e_i:e_i + 1, :].rearrange(
                           "one (a b) -> (one a) b", b=128)),
                   w=[("sm", "stage1")], tag=f"b1l{layer}_{e_i}")
            tr(psF[:, 0:nrow], SM[0:nrow, B1ST:B1ST + 128], IDF_AP,
               r=[("sm", "stage1"), ("sm", "idf")], w=[KF],
               tag=f"b1t{layer}_{e_i}")
            acopy(SM[:, B1C + e_i * nrow: B1C + (e_i + 1) * nrow],
                  psF[:, 0:nrow], r=[KF], w=[("sm", "b1c", e_i)],
                  tag=f"b1e{layer}_{e_i}")
        B2B = PH + 3456
        if layer == 0:
            kb.dma("sp", "b2ld",
                   lambda e: e.dma_start(SM[0:8, B2B:B2B + 1024], B2d.ap()),
                   w=[("sm", "b2f")], tag="b2ld0")
            for dc in range(2):
                mm(psF[0:1, 0:512], SM[0:8, OC8:OC8 + 1],
                   SM[0:8, B2B + dc * 512:B2B + dc * 512 + 512], True, True,
                   r=[("sm", "onescol8"), ("sm", "b2f")], w=[KF],
                   tag=f"b2s{dc}")
                acopy(SM[0:1, PH + dc * 512:PH + dc * 512 + 512],
                      psF[0:1, 0:512], r=[KF], w=[("sm", "p2g")],
                      tag=f"b2se{dc}")
            for dc in range(2):
                mm(psF[:, 0:512], SM[0:1, ONESF:ONESF + 128],
                   SM[0:1, PH + dc * 512:PH + dc * 512 + 512],
                   True, True,
                   r=[("sm", "onesf"), ("sm", "p2g")], w=[KF],
                   tag=f"b2bc{dc}")
                acopy(SM[:, B2B + dc * 512:B2B + dc * 512 + 512],
                      psF[:, 0:512], r=[KF],
                      w=[("sm", "b2bc"), ("sm", "b2f")],
                      tag=f"b2bce{dc}")
        else:
            kb.dma("pool", "b2ld1",
                   lambda e: e.dma_start(SMB[0:8, 2176:2176 + 1024], B2d.ap()),
                   w=[("smb", "b2sb")], tag="b2ld1")

        for q in range(NQ):
            if quarter_pre is not None:
                quarter_pre(q)
            if layer == 1:
                for mt in range(TQ // 128):
                    tti2 = q * (TQ // 128) + mt
                    tr(psF[0:8, 0:128],
                       SM[:, GATES + tti2 * 8:GATES + tti2 * 8 + 8],
                       IDF_AP, r=[("sm", "gates", tti2), ("sm", "idf")],
                       w=[KF], tag=f"gTt{tti2}")
                    acopy(SMB[0:8, 2048:2048 + 128], psF[0:8, 0:128],
                          r=[KF], w=[("smb", "gTm")], tag=f"gTe{tti2}")
                    for dc in range(2):
                        mm(psF[:, 0:512],
                           SMB[0:8, 2048:2048 + 128],
                           SMB[0:8, 2176 + dc * 512:2176 + dc * 512 + 512],
                           True, True,
                           r=[("smb", "gTm"), ("smb", "b2sb")],
                           w=[KF], tag=f"b2m{q}_{mt}_{dc}")
                        acopy(AR3[:, mt * 1024 + dc * 512:
                                  mt * 1024 + dc * 512 + 512],
                              psF[:, 0:512], r=[KF],
                              w=[("ar3", "acc", mt, dc)],
                              tag=f"b2me{q}_{mt}_{dc}")
            gi = 0
            for e_i in range(NE):
                for hq in range(NHQ):
                    sl = (q * NE * NHQ + e_i * NHQ + hq) % 2
                    W1g = gr_a[:, sl * 16384: sl * 16384 + 8192]
                    W2g = gr_a[:, sl * 16384 + 8192: sl * 16384 + 16384]
                    k1, k2 = (gr_tag, "w1g", sl), (gr_tag, "w2g", sl)
                    W1b, W2b = (sc_w10, sc_w20) if layer == 0 else \
                        (sc_w11, sc_w21)
                    nm1, nm2 = ("w10", "w20") if layer == 0 else \
                        ("w11", "w21")
                    w1src = bass.AP(W1b, e_i * D * H + hq * 1024,
                                    [[H, 128], [128 * H, 8], [1, 1024]])
                    w2src = bass.AP(W2b, e_i * H * D + hq * 1024 * D,
                                    [[D, 128], [128 * D, 8], [1, 1024]])
                    kb.dma("sp", f"w1g{sl}",
                           lambda e, w1src=w1src, W1g=W1g: e.dma_start(
                               W1g, w1src),
                           r=[("hbm", nm1, e_i)],
                           w=[k1], tag=f"w1l{layer}_{q}_{e_i}_{hq}")
                    kb.dma("sp", f"w2g{sl}",
                           lambda e, w2src=w2src, W2g=W2g: e.dma_start(
                               W2g, w2src),
                           r=[("hbm", nm2, e_i)],
                           w=[k2], tag=f"w2l{layer}_{q}_{e_i}_{hq}")
                    nrow = H // 128
                    for hm in range(8):
                        for tc in range(TQ // 512):
                            ii = (hm * (TQ // 512) + tc) % 4
                            ps, kps = psb(ii), KBs[ii]
                            for k in range(8):
                                xap, xkeys = xnT_blk(k, q, tc)
                                mm(ps[:, 0:512],
                                   W1g[:, k * 1024 + hm * 128:
                                       k * 1024 + hm * 128 + 128],
                                   xap, k == 0, k == 7,
                                   r=[k1] + xkeys,
                                   w=[kps],
                                   tag=f"m1_{layer}_{q}_{e_i}_{hq}_{hm}_{tc}_{k}")
                            bcol = SM[:, B1C + e_i * nrow + hq * 8 + hm:
                                      B1C + e_i * nrow + hq * 8 + hm + 1]
                            hap, hkey = hb(hm, tc)
                            act(hap, ps[:, 0:512], AF.Gelu,
                                r=[kps, ("sm", "b1c", e_i)],
                                w=[hkey],
                                tag=f"ge{layer}_{q}_{e_i}_{hq}_{hm}_{tc}",
                                bias=bcol)
                    for mt in range(TQ // 128):
                        for dc in range(2):
                            ps, kps = ((psA0, KA0) if (mt * 2 + dc) % 2 == 0
                                       else (psA1, KA1))
                            for hk in range(8):
                                hap, hkey = hb(hk, (mt * 128) // 512)
                                mm(ps[:, 0:512],
                                   hap[:, mt * 128 - ((mt * 128) // 512) * 512:
                                       mt * 128 - ((mt * 128) // 512) * 512 + 128],
                                   W2g[:, hk * 1024 + dc * 512:
                                       hk * 1024 + dc * 512 + 512],
                                   hk == 0, hk == 7,
                                   r=[k2, hkey],
                                   w=[kps],
                                   tag=f"m2_{layer}_{q}_{e_i}_{hq}_{mt}_{dc}_{hk}")
                            accp = AR3[:, mt * 1024 + dc * 512:
                                       mt * 1024 + dc * 512 + 512]
                            kacc = ("ar3", "acc", mt, dc)
                            if layer == 0:
                                if e_i == 0 and hq == 0:
                                    dcopy(accp, ps[:, 0:512], r=[kps],
                                          w=[kacc],
                                          tag=f"ac0_{layer}_{q}_{e_i}_{hq}_{mt}_{dc}")
                                else:
                                    tt(accp, ps[:, 0:512], accp, ALU.add,
                                       r=[kps, kacc], w=[kacc],
                                       tag=f"ac_{layer}_{q}_{e_i}_{hq}_{mt}_{dc}")
                            else:
                                gcol = SM[:, GATES + (q * (TQ // 128) + mt) * 8
                                          + e_i: GATES +
                                          (q * (TQ // 128) + mt) * 8 + e_i + 1]
                                stt(accp, ps[:, 0:512], gcol, accp,
                                    ALU.mult, ALU.add,
                                    r=[kps, kacc, ("sm", "gates",
                                                   q * (TQ // 128) + mt)],
                                    w=[kacc],
                                    tag=f"ac_{layer}_{q}_{e_i}_{hq}_{mt}_{dc}")
                    if granule_post is not None:
                        granule_post(q, gi)
                    gi += 1
            if quarter_post is not None:
                quarter_post(q)
            for mt in range(TQ // 128):
                if finalize_tile is not None:
                    finalize_tile(q, mt)

    kb.dma("sp", "xn0ld",
           lambda e: e.dma_start(AR1[:, 0:8 * T], sc_xn0t.ap()),
           w=[("ar1", "xnT", t) for t in range(NT)], tag="xn0in")

    def fin_l0(q, mt):
        tti = q * (TQ // 128) + mt
        gt = SM[:, PH:PH + 1024]
        xt = SM[:, PH + 1024:PH + 2048]
        stl = SM[:, PH + 2048:PH + 3072]
        kg, kx, ks = ("sm", "p2g"), ("sm", "p2x"), ("sm", "p2s")
        ngcol = SM[:, NORMG + tti:NORMG + tti + 1]
        racc = [("ar3", "acc", mt, dc) for dc in range(2)]
        tt(stl, AR3[:, mt * 1024:(mt + 1) * 1024],
           SM[:, PH + 3456:PH + 3456 + 1024], ALU.add,
           r=racc + [("sm", "b2bc")], w=[ks], tag=f"ab2{q}_{mt}")
        act(gt, stl, AF.Silu,
            r=[ks, ("sm", "normg", tti)], w=[kg], tag=f"sil{q}_{mt}",
            scale=ngcol)
        kb.dma("sp", "xld2",
               lambda e, tti=tti, xt=xt: e.dma_start(
                   xt, X.ap()[tti * 128:(tti + 1) * 128, :]),
               w=[kx], tag=f"x2l{tti}")
        kb.dma("sp", "sld",
               lambda e, tti=tti, stl=stl: e.dma_start(
                   stl, sc_s0.ap()[tti * 128:(tti + 1) * 128, :]),
               r=[kg], w=[ks], tag=f"s0l{tti}")
        tt(gt, stl, gt, ALU.mult, r=[ks, kg], w=[kg], tag=f"sg{tti}")
        tt(xt, gt, xt, ALU.add, r=[kg, kx], w=[kx], tag=f"x1_{tti}")
        kb.dma("sp", "x1o",
               lambda e, tti=tti, xt=xt: e.dma_start(
                   sc_x1.ap()[tti * 128:(tti + 1) * 128, :], xt),
               r=[kx], tag=f"x1s{tti}")
        xnt = SMB[:, (tti % 2) * 1024:(tti % 2) * 1024 + 1024]
        kxn = ("smb", "xn1", tti % 2)
        emit_ln(xt, kx, xnt, kxn, gt, kg, f"ln1_{tti}_", par=tti % 2)
        kb.dma("sp", "xn1o",
               lambda e, tti=tti, xnt=xnt: e.dma_start(
                   sc_xn1.ap()[tti * 128:(tti + 1) * 128, :], xnt),
               r=[kxn], tag=f"xn1s{tti}")

    def xnT_l0(k, q, tc):
        tok = q * TQ + tc * 512
        return (AR1[:, k * T + tok: k * T + tok + 512],
                [("ar1", "xnT", t4) for t4 in
                 range(tok // 128, tok // 128 + 4)])

    emit_moe(0, W1_0, B1_0, W2_0, B2_0, H0, fin_l0,
             AR2, "ar2", lambda hm, tc: (
                 AR4[:, hm * TQ + tc * 512: hm * TQ + (tc + 1) * 512],
                 ("ar4", "hbuf", hm, tc)),
             xnT_l0)
    kb.barrier()

    AR4f = AR4.ap().bitcast(F32)
    SMBf = SMB.ap().bitcast(F32)
    XIN0 = 8192
    HB0 = 16384
    WSL = 24576
    sc_winT = nc.dram_tensor("sc_winT", [128, 8192], BF16, kind="Internal")
    sc_woT = nc.dram_tensor("sc_woT", [128, 8192], BF16, kind="Internal")

    SSP = PH + 3456
    DTC = SSP + 384
    DSC = SSP + 392
    EAC = SSP + 400
    ST0 = SSP + 448
    kb.dma("sp", "dtld",
           lambda e: e.dma_start(SM[0:8, PH + 3072:PH + 3072 + 128],
                                 DTlog.ap().rearrange("(a b) -> a b", b=128)),
           w=[("sm", "stage1")], tag="dtld")
    tr(psF[:, 0:8], SM[0:8, PH + 3072:PH + 3072 + 128], IDF_AP,
       r=[("sm", "stage1"), ("sm", "idf")], w=[KF], tag="dttr")
    act(SM[:, DTC:DTC + 8], psF[:, 0:8], AF.Exp,
        r=[KF], w=[("sm", "dtc")], tag="dtex")
    kb.dma("sp", "dtld",
           lambda e: e.dma_start(SM[0:8, PH + 3072:PH + 3072 + 128],
                                 Dssm.ap().rearrange("(a b) -> a b", b=128)),
           r=[KF], w=[("sm", "stage1")], tag="dsld")
    tr(psF[:, 8:16], SM[0:8, PH + 3072:PH + 3072 + 128], IDF_AP,
       r=[("sm", "stage1"), ("sm", "idf")], w=[KF], tag="dstr")
    acopy(SM[:, DSC:DSC + 8], psF[:, 8:16],
          r=[KF], w=[("sm", "dsc")], tag="dsev")

    for ct in range(8):
        base = SSP + ct * 48
        kb.dma("sp", "assld",
               lambda e, ct=ct: e.dma_start(
                   SM[:, EAC + 32:EAC + 48],
                   Alog.ap()[ct * 128:(ct + 1) * 128, :]),
               w=[("sm", "alst")], tag=f"ald{ct}")
        kb.dma("sp", "assld",
               lambda e, ct=ct, base=base: e.dma_start(
                   SM[:, base + 16:base + 32],
                   Bssm.ap()[ct * 128:(ct + 1) * 128, :]),
               w=[("sm", "bssm", ct)], tag=f"bld{ct}")
        kb.dma("sp", "assld",
               lambda e, ct=ct, base=base: e.dma_start(
                   SM[:, base + 32:base + 48],
                   Cssm.ap()[ct * 128:(ct + 1) * 128, :]),
               w=[("sm", "cssm", ct)], tag=f"cld{ct}")
        eA = SM[:, EAC:EAC + 16]
        t1 = SM[:, EAC + 16:EAC + 32]
        act(eA, SM[:, EAC + 32:EAC + 48], AF.Exp,
            r=[("sm", "alst")], w=[("sm", "eA")], tag=f"eA{ct}")
        ts(t1, eA, SM[:, DTC + ct:DTC + ct + 1], -1.0, ALU.mult,
           ALU.mult, r=[("sm", "eA"), ("sm", "dtc")], w=[("sm", "t1")],
           tag=f"dtA{ct}")
        act(SM[:, base:base + 16], t1, AF.Exp,
            r=[("sm", "t1")], w=[("sm", "abar", ct)], tag=f"ab{ct}")
        ts(t1, SM[:, base:base + 16], 1.0, -1.0, ALU.subtract, ALU.mult,
           r=[("sm", "abar", ct)], w=[("sm", "t1")], tag=f"bb1{ct}")
        recip(SM[:, EAC + 32:EAC + 48], eA, r=[("sm", "eA"), ("sm", "alst")],
              w=[("sm", "alst")], tag=f"bb2{ct}")
        tt(t1, t1, SM[:, EAC + 32:EAC + 48], ALU.mult,
           r=[("sm", "t1"), ("sm", "alst")], w=[("sm", "t1")], tag=f"bb3{ct}")
        tt(SM[:, base + 16:base + 32], t1, SM[:, base + 16:base + 32],
           ALU.mult, r=[("sm", "t1"), ("sm", "bssm", ct)],
           w=[("sm", "bbar", ct)], tag=f"bb4{ct}")
    memset(SM[:, ST0:ST0 + 128], 0.0,
           w=[("sm", "sst", ct, n) for ct in range(8) for n in range(NS)],
           tag="st0")

    kb.dma("pool", "centld",
           lambda e: e.dma_start(AR2[0:8, WSL:WSL + 1024], Cent.ap()),
           w=[("ar2", "wslot")], tag="centld")
    for k in range(8):
        tr(psT[:, k * 8:k * 8 + 8],
           AR2[0:8, WSL + k * 128:WSL + (k + 1) * 128],
           IDB_AP, r=[("ar2", "wslot"), ("sm", "idb")], w=[KT],
           tag=f"ctr{k}")
    acopy(SMbf[:, CENTB:CENTB + 64], psT[:, 0:64], r=[KT],
          w=[("sm", "centT")], tag="cev")

    def build_wT(Wd, sc_dst, tag):
        for rb in range(8):
            kb.dma("pool", "wiold",
                   lambda e, rb=rb: e.dma_start(
                       AR2[:, HB0:HB0 + 1024],
                       Wd.ap()[rb * 128:(rb + 1) * 128, :]),
                   w=[("ar2", "hbuf", 0, 0), ("ar2", "hbuf", 0, 1)],
                   tag=f"{tag}l{rb}")
            for k in range(8):
                tr(psT[:, k * 128:(k + 1) * 128],
                   AR2[:, HB0 + k * 128:HB0 + (k + 1) * 128], IDB_AP,
                   r=[("ar2", "hbuf", 0, 0), ("ar2", "hbuf", 0, 1),
                      ("sm", "idb")], w=[KT],
                   tag=f"{tag}t{rb}_{k}")
            dst = bass.AP(AR2[:, 0:1].tensor, WSL + rb * 128,
                          [[33280, 128], [1024, 8], [1, 128]])
            acopy(dst, psT.ap().rearrange("p (k c) -> p k c", k=8),
                  r=[KT], w=[("ar2", "wslot")], tag=f"{tag}e{rb}")
        kb.dma("sp", "wTout",
               lambda e: e.dma_start(sc_dst.ap(), AR2[:, WSL:WSL + 8192]),
               r=[("ar2", "wslot")], w=[("hbm", tag)], tag=f"{tag}sp")

    build_wT(Wout, sc_woT, "wo")
    build_wT(Win, sc_winT, "wi")

    def xnT_l1(k, q, tc):
        return (AR2[:, k * TQ + tc * 512: k * TQ + tc * 512 + 512],
                [("ar2", "xn1Tq")])

    def p34_quarter_pre(q):
        if q > 0:
            kb.dma("sp", "wTin",
                   lambda e: e.dma_start(AR2[:, WSL:WSL + 8192],
                                         sc_winT.ap()),
                   r=[("hbm", "wi")], w=[("ar2", "wslot")], tag=f"wTin{q}")
        for mt in range(TQ // 128):
            tti = q * (TQ // 128) + mt
            par = mt % 2
            stg = AR4[:, par * 2048:par * 2048 + 1024]
            kb.dma("sp", f"xn1l{par}",
                   lambda e, tti=tti, stg=stg: e.dma_start(
                       stg, sc_xn1.ap()[tti * 128:(tti + 1) * 128, :]),
                   w=[("sm", "bx", par)], tag=f"x1ld{tti}")
            for k in range(8):
                tr(psT[:, k * 128:(k + 1) * 128],
                   stg[:, k * 128:(k + 1) * 128], IDB_AP,
                   r=[("sm", "bx", par), ("sm", "idb")], w=[KT],
                   tag=f"x1t{tti}_{k}")
            dst = bass.AP(AR2[:, 0:1].tensor, mt * 128,
                          [[33280, 128], [TQ, 8], [1, 128]])
            acopy(dst, psT.ap().rearrange("p (k c) -> p k c", k=8),
                  r=[KT], w=[("ar2", "xn1Tq")], tag=f"x1e{tti}")
        for mt in range(TQ // 128):
            tti = q * (TQ // 128) + mt
            for k in range(8):
                mm(psF[:, 8:16],
                   AR2[:, k * TQ + mt * 128: k * TQ + mt * 128 + 128],
                   SMbf[:, CENTB + k * 8:CENTB + k * 8 + 8], k == 0, k == 7,
                   r=[("ar2", "xn1Tq"), ("sm", "centT")], w=[KF],
                   tag=f"cl{tti}_{k}")
            gg = lambda i: SM[:, GT + i * 8:GT + (i + 1) * 8]
            cc = lambda i: SM[:, GT + 96 + i:GT + 97 + i]
            act(gg(0), psF[:, 8:16], AF.Sigmoid, r=[KF],
                w=[("sm", "cg")], tag=f"csg{tti}")
            ts(gg(1), psF[:, 8:16], 16.635532, None, ALU.min, ALU.bypass,
               r=[KF], w=[("sm", "crc")], tag=f"crc{tti}")
            tt(gg(1), gg(1), SM[:, BC + 16:BC + 24], ALU.add,
               r=[("sm", "crc"), ("sm", "tie_bc")], w=[("sm", "cr")],
               tag=f"cr{tti}")
            red(cc(0), gg(1), "max", r=[("sm", "cr")], w=[("sm", "cm1")],
                tag=f"cm1{tti}")
            ts(gg(2), gg(1), cc(0), None, ALU.is_equal, ALU.bypass,
               r=[("sm", "cr"), ("sm", "cm1")], w=[("sm", "ceq")],
               tag=f"ceq{tti}")
            stt(gg(3), gg(2), -1e30, gg(1), ALU.mult, ALU.add,
                r=[("sm", "ceq"), ("sm", "cr")], w=[("sm", "cr2")],
                tag=f"cr2{tti}")
            red(cc(1), gg(3), "max", r=[("sm", "cr2")], w=[("sm", "cm2")],
                tag=f"cm2{tti}")
            ts(gg(4), gg(1), cc(1), None, ALU.is_ge, ALU.bypass,
               r=[("sm", "cr"), ("sm", "cm2")], w=[("sm", "csel")],
               tag=f"csel{tti}")
            tt(gg(5), gg(0), gg(4), ALU.mult,
               r=[("sm", "cg"), ("sm", "csel")], w=[("sm", "cmsk")],
               tag=f"cmsk{tti}")
            red(cc(2), gg(5), "sum", r=[("sm", "cmsk")], w=[("sm", "csum")],
                tag=f"csum{tti}")
            recip(cc(3), cc(2), r=[("sm", "csum")], w=[("sm", "crs")],
                  tag=f"crs{tti}")
            ts(SM[:, GATES + tti * 8:GATES + tti * 8 + 8], gg(5), cc(3),
               None, ALU.mult, ALU.bypass,
               r=[("sm", "cmsk"), ("sm", "crs")], w=[("sm", "gates", tti)],
               tag=f"cgf{tti}")
        for cm in range(8):
            for tc in range(TQ // 512):
                ii = (cm * (TQ // 512) + tc) % 4
                ps, kps = psb(ii), KBs[ii]
                for k in range(8):
                    mm(ps[:, 0:512],
                       AR2[:, WSL + k * 1024 + cm * 128:
                           WSL + k * 1024 + cm * 128 + 128],
                       AR2[:, k * TQ + tc * 512: k * TQ + tc * 512 + 512],
                       k == 0, k == 7,
                       r=[("ar2", "wslot"), ("ar2", "xn1Tq")],
                       w=[kps], tag=f"wi{q}_{cm}_{tc}_{k}")
                acopy(AR2[:, XIN0 + cm * TQ + tc * 512:
                          XIN0 + cm * TQ + tc * 512 + 512],
                      ps[:, 0:512], r=[kps],
                      w=[("ar2", "xin", cm)], tag=f"wie{q}_{cm}_{tc}")
        kb.dma("sp", "wTin",
               lambda e: e.dma_start(AR2[:, WSL:WSL + 8192], sc_woT.ap()),
               r=[("hbm", "wo")], w=[("ar2", "wslot")], tag=f"woTin{q}")

    def p34_scan_piece(q, ct, nseg):
        base = SSP + ct * 48
        kxin = ("ar2", "xin", ct)
        xin_ap = AR2[:, XIN0 + ct * TQ: XIN0 + ct * TQ + TQ]
        YY = AR4f[:, 3072:3072 + TQ]
        HH = AR4f[:, 2048:2048 + TQ]
        if nseg == 0:
            ts(YY, xin_ap, SM[:, DSC + ct:DSC + ct + 1], None,
               ALU.mult, ALU.bypass, r=[kxin, ("sm", "dsc")],
               w=[("sm", "yy")], tag=f"yD{q}_{ct}")
        for n in range(nseg * 4, nseg * 4 + 4):
            par = n % 2
            bxp = AR4f[:, par * 1024:par * 1024 + TQ]
            kbx = ("sm", "bx", par)
            act(bxp, xin_ap, AF.Copy,
                r=[kxin, ("sm", "bbar", ct)], w=[kbx],
                tag=f"bx{q}_{ct}_{n}",
                scale=SM[:, base + 16 + n:base + 17 + n])
            kb.op("dve", lambda e, base=base, n=n, ct=ct, bxp=bxp:
                  e.tensor_tensor_scan(
                      HH, SM[:, base + n:base + n + 1].broadcast_to(
                          [128, TQ]),
                      bxp, SM[:, ST0 + ct * 16 + n:ST0 + ct * 16 + n + 1],
                      ALU.mult, ALU.add),
                  r=[kbx, ("sm", "abar", ct), ("sm", "sst", ct, n)],
                  w=[("sm", "hh")], tag=f"sc{q}_{ct}_{n}")
            dcopy(SM[:, ST0 + ct * 16 + n:ST0 + ct * 16 + n + 1],
                  HH[:, TQ - 1:TQ], r=[("sm", "hh")],
                  w=[("sm", "sst", ct, n)], tag=f"sv{q}_{ct}_{n}")
            stt(YY, HH, SM[:, base + 32 + n:base + 33 + n], YY,
                ALU.mult, ALU.add,
                r=[("sm", "hh"), ("sm", "cssm", ct), ("sm", "yy")],
                w=[("sm", "yy")], tag=f"ya{q}_{ct}_{n}")
        if nseg == 3:
            dcopy(bass.AP(AR2[:, 0:1].tensor, XIN0 + ct * TQ,
                          [[33280, 128], [1, TQ]]), YY,
                  r=[("sm", "yy")], w=[kxin, ("ar2", "yb", ct)],
                  tag=f"yb{q}_{ct}", eng="pool")

    def p34_quarter_post(q):
        rot = [(psA0, KA0), (psA1, KA1),
               (psb(0), KBs[0]), (psb(1), KBs[1])]
        for mt in range(TQ // 128):
            for dc in range(2):
                ps, kps = rot[(mt * 2 + dc) % 4]
                for ct in range(8):
                    mm(ps[:, 0:512],
                       bass.AP(AR2[:, 0:1].tensor,
                               XIN0 + ct * TQ + mt * 128,
                               [[33280, 128], [1, 128]]),
                       AR2[:, WSL + ct * 1024 + dc * 512:
                           WSL + ct * 1024 + dc * 512 + 512],
                       ct == 0, ct == 7,
                       r=[("ar2", "yb", ct), ("ar2", "wslot")],
                       w=[kps], tag=f"wo{q}_{mt}_{dc}_{ct}")
                acopy(SMBf[:, dc * 512:dc * 512 + 512],
                      ps[:, 0:512], r=[kps], w=[("smb", "s1t", dc)],
                      tag=f"woe{q}_{mt}_{dc}")
            fin_l1(q, mt)

    NGRAN = NE * (H1 // 1024)
    def p34_granule_post(q, gi):
        for pidx in (2 * gi, 2 * gi + 1):
            if pidx < 32:
                p34_scan_piece(q, pidx // 4, pidx % 4)

    def fin_l1(q, mt):
        tti = q * (TQ // 128) + mt
        par = mt % 2
        gt = SM[:, PH + par * 1024:PH + par * 1024 + 1024]
        xt = SM[:, PH + 2048:PH + 3072]
        kg = ("sm", "p4g", par)
        kx = ("sm", "p4x")
        s1ap = SMBf[:, 0:1024]
        ks1 = [("smb", "s1t", dc) for dc in range(2)]
        racc = [("ar3", "acc", mt, dc) for dc in range(2)]
        act(gt, AR3[:, mt * 1024:(mt + 1) * 1024], AF.Silu,
            r=racc, w=[kg], tag=f"sl1{q}_{mt}")
        kb.dma("sp", "x1l",
               lambda e, tti=tti, xt=xt: e.dma_start(
                   xt, sc_x1.ap()[tti * 128:(tti + 1) * 128, :]),
               w=[kx], tag=f"x1l{tti}")
        tt(gt, s1ap, gt, ALU.mult, r=ks1 + [kg], w=[kg], tag=f"sg1{tti}")
        tt(xt, gt, xt, ALU.add, r=[kg, kx], w=[kx], tag=f"x2_{tti}")
        kb.dma("sp", "outo",
               lambda e, tti=tti, xt=xt: e.dma_start(
                   OUT.ap()[tti * 128:(tti + 1) * 128, :], xt),
               r=[kx], tag=f"outs{tti}")

    emit_moe(1, W1_1, B1_1, W2_1, B2_1, H1, None,
             AR1, "ar1", lambda hm, tc: (
                 AR2[:, HB0 + hm * TQ + tc * 512:
                     HB0 + hm * TQ + (tc + 1) * 512],
                 ("ar2", "hbuf", hm, tc)),
             xnT_l1, quarter_pre=p34_quarter_pre,
             granule_post=p34_granule_post, quarter_post=p34_quarter_post)

    guard = kb.finalize_and_emit()
    ctx2 = ctx
    return nc, (ctx2, guard)



_CACHE = {}


def _get_nc(T):
    if T not in _CACHE:
        _CACHE[T] = build_nc(T)
    return _CACHE[T]


def kernel(L0_attn_x, L0_attn_W_qkv, L0_ffn_gate_w, L0_ffn_gate_b,
           L0_ffn_rms_w, L0_ffn_w1, L0_ffn_b1, L0_ffn_w2, L0_ffn_b2,
           L1_attn_W_in, L1_attn_W_out, L1_attn_A_log, L1_attn_B_ssm,
           L1_attn_C_ssm, L1_attn_dt_log, L1_attn_D_ssm, L1_ffn_centroids,
           L1_ffn_W1, L1_ffn_b1, L1_ffn_W2, L1_ffn_b2):
    x = np.ascontiguousarray(np.asarray(L0_attn_x, dtype=np.float32))
    B, T, d = x.shape
    assert B == 8 and d == D
    nc, _guard = _get_nc(T)
    f32 = lambda a: np.ascontiguousarray(np.asarray(a, dtype=np.float32))
    shared = {
        "w_qkv": f32(L0_attn_W_qkv), "gate_w": f32(L0_ffn_gate_w),
        "gate_b": f32(L0_ffn_gate_b), "rms_w": f32(L0_ffn_rms_w),
        "l0_w1": f32(L0_ffn_w1), "l0_b1": f32(L0_ffn_b1),
        "l0_w2": f32(L0_ffn_w2), "l0_b2": f32(L0_ffn_b2),
        "w_in": f32(L1_attn_W_in), "w_out": f32(L1_attn_W_out),
        "a_log": f32(L1_attn_A_log), "b_ssm": f32(L1_attn_B_ssm),
        "c_ssm": f32(L1_attn_C_ssm), "dt_log": f32(L1_attn_dt_log),
        "d_ssm": f32(L1_attn_D_ssm), "centroids": f32(L1_ffn_centroids),
        "l1_w1": f32(L1_ffn_W1), "l1_b1": f32(L1_ffn_b1),
        "l1_w2": f32(L1_ffn_W2), "l1_b2": f32(L1_ffn_b2),
    }
    in_maps = [dict(shared, x=x[i]) for i in range(8)]
    res = run_bass_kernel_spmd(nc, in_maps, core_ids=list(range(8)))
    return np.stack([res.results[i]["out"] for i in range(8)], axis=0)
